# version 03
# baseline (speedup 1.0000x reference)
import math
from contextlib import ExitStack

import numpy as np
import ml_dtypes

import concourse.bass as bass
import concourse.mybir as mybir
from concourse.bass_utils import run_bass_kernel_spmd

F32 = mybir.dt.float32
BF16 = mybir.dt.bfloat16
I32 = mybir.dt.int32
AF = mybir.ActivationFunctionType
ALU = mybir.AluOpType
AX = mybir.AxisListType
BF = ml_dtypes.bfloat16

B, NSEQ, DM, NH, DH = 2, 8192, 1024, 16, 64
TPC = 2048
NCH = DM // 128
NCORE = 8
ITW = 1024
NIT = TPC // ITW

SLOPE = [2.0 ** (-0.5 * (h + 1)) for h in range(NH)]
THRESH = 30.0
JH = [min(NSEQ, int(math.ceil(THRESH / SLOPE[h] / 128)) * 128) for h in range(NH)]
SH = [JH[h] // 128 for h in range(NH)]
JPAIR = [max(JH[2 * p], JH[2 * p + 1]) for p in range(8)]
KOFF = np.concatenate([[0], np.cumsum(JPAIR)]).astype(int)
KW_TOT = int(KOFF[-1])
VOFF = np.concatenate([[0], np.cumsum(SH)]).astype(int)
NSLOT = int(VOFF[-1])
VW_TOT = 65 * NSLOT

L0TILES = []
for _hp in range(8):
    for _h2 in range(2):
        _h = 2 * _hp + _h2
        for _it in range(NIT):
            for _s in range(SH[_h]):
                L0TILES.append((_hp, _h2, _h, _it, _s))
NL0T = len(L0TILES)


def _is_partial(it, s):
    for cq in range(4):
        i0g = 2048 * cq + ITW * it
        if i0g - 127 <= 128 * s <= i0g + ITW - 1:
            return True
    return False

L0MASK = {}
for _t, (_hp, _h2, _h, _it, _s) in enumerate(L0TILES):
    if _is_partial(_it, _s):
        L0MASK[_t] = len(L0MASK)
NL0M = len(L0MASK)



def _mk(nc, ctx):
    sb = lambda name, shape, dt=F32: ctx.enter_context(nc.sbuf_tensor(name, shape, dt))
    ps = lambda name, shape, dt=F32: ctx.enter_context(nc.psum_tensor(name, shape, dt))
    sem = lambda name: ctx.enter_context(nc.semaphore(name))
    return sb, ps, sem



def build_l0qkv():
    nc = bass.Bass("TRN2", target_bir_lowering=False, debug=False)
    x_in = nc.dram_tensor("x_fm", [DM, TPC], F32, kind="ExternalInput").ap()
    w_in = {n: nc.dram_tensor(n, [DM, DM], F32, kind="ExternalInput").ap()
            for n in ("Wq", "Wk", "Wv")}
    q_out = nc.dram_tensor("q_fm", [DM, TPC], BF16, kind="ExternalOutput").ap()
    k_out = nc.dram_tensor("k_fm", [DM, TPC], BF16, kind="ExternalOutput").ap()
    v_out = nc.dram_tensor("v_tm", [TPC, DM], BF16, kind="ExternalOutput").ap()

    ctx = ExitStack()
    sb, ps, sem = _mk(nc, ctx)
    with ctx:
        xs = sb("xs", [128, TPC], F32)
        x_bf = [sb(f"x_bf{c}", [128, TPC], BF16) for c in range(NCH)]
        wstage = sb("wstage", [128, DM], F32)
        w_bf = {n: [sb(f"{n}_bf{c}", [128, DM], BF16) for c in range(NCH)]
                for n in ("Wq", "Wk", "Wv")}
        evac = [sb(f"evac{i}", [128, TPC], BF16) for i in range(2)]
        pst = [ps(f"pst{i}", [128, 512]) for i in range(4)]

        s_ld = sem("s_ld")
        s_ldw = sem("s_ldw")
        s_cast = sem("s_cast")
        s_mm = sem("s_mm")
        s_ev = sem("s_ev")
        s_st = sem("s_st")

        with nc.Block() as block:
            @block.sync
            def _(sync):
                nld = 0
                for c in range(NCH):
                    if c > 0:
                        sync.wait_ge(s_cast, c)
                    sync.dma_start(out=xs[:],
                                   in_=x_in[128 * c:128 * (c + 1), :]).then_inc(s_ld, 16)
                    nld += 16
                i = 0
                for n in ("Wq", "Wk", "Wv"):
                    for c in range(NCH):
                        sync.wait_ge(s_cast, NCH + i)
                        sync.dma_start(
                            out=wstage[:],
                            in_=w_in[n][128 * c:128 * (c + 1), :]).then_inc(s_ldw, 16)
                        i += 1
                g = 0
                for dst in (q_out, k_out):
                    for hp in range(NCH):
                        g += 1
                        sync.wait_ge(s_ev, g)
                        sync.dma_start(out=dst[128 * hp:128 * (hp + 1), :],
                                       in_=evac[(g - 1) % 2][:]).then_inc(s_st, 16)
                        sync.wait_ge(s_st, 16 * g)
                for jt in range(TPC // 128):
                    g += 1
                    sync.wait_ge(s_ev, g)
                    sync.dma_start(out=v_out[128 * jt:128 * (jt + 1), :],
                                   in_=evac[(g - 1) % 2][:, 0:DM]).then_inc(s_st, 16)
                    sync.wait_ge(s_st, 16 * g)

            @block.vector
            def _(vector):
                for c in range(NCH):
                    vector.wait_ge(s_ld, 16 * (c + 1))
                    vector.tensor_copy(x_bf[c][:], xs[:]).then_inc(s_cast, 1)
                i = 0
                for n in ("Wq", "Wk", "Wv"):
                    for c in range(NCH):
                        vector.wait_ge(s_ldw, 16 * (i + 1))
                        vector.tensor_copy(w_bf[n][c][:],
                                           wstage[:]).then_inc(s_cast, 1)
                        i += 1
                g = 0
                for pi in range(2):
                    for hp in range(NCH):
                        g += 1
                        vector.wait_ge(s_mm, g)
                        if g > 2:
                            vector.wait_ge(s_st, 16 * (g - 2))
                        for its in range(4):
                            ins = vector.tensor_copy(
                                evac[(g - 1) % 2][:, 512 * its:512 * (its + 1)],
                                pst[its][:])
                        ins.then_inc(s_ev, 1)
                for jt in range(TPC // 128):
                    g += 1
                    vector.wait_ge(s_mm, g)
                    if g > 2:
                        vector.wait_ge(s_st, 16 * (g - 2))
                    for dc in range(2):
                        ins = vector.tensor_copy(
                            evac[(g - 1) % 2][:, 512 * dc:512 * (dc + 1)],
                            pst[dc][:])
                    ins.then_inc(s_ev, 1)

            @block.tensor
            def _(tensor):
                tensor.wait_ge(s_cast, NCH + 24)
                g = 0
                for wn in ("Wq", "Wk"):
                    for hp in range(NCH):
                        g += 1
                        if g > 1:
                            tensor.wait_ge(s_ev, g - 1)
                        for its in range(4):
                            for c in range(NCH):
                                ins = tensor.matmul(
                                    pst[its][:],
                                    w_bf[wn][c][:, 128 * hp:128 * (hp + 1)],
                                    x_bf[c][:, 512 * its:512 * (its + 1)],
                                    start=(c == 0), stop=(c == NCH - 1))
                        ins.then_inc(s_mm, 1)
                for jt in range(TPC // 128):
                    g += 1
                    tensor.wait_ge(s_ev, g - 1)
                    for dc in range(2):
                        for c in range(NCH):
                            ins = tensor.matmul(
                                pst[dc][:],
                                x_bf[c][:, 128 * jt:128 * (jt + 1)],
                                w_bf["Wv"][c][:, 512 * dc:512 * (dc + 1)],
                                start=(c == 0), stop=(c == NCH - 1))
                    ins.then_inc(s_mm, 1)
    return nc



L0GROUPS = []
_cum = 0
for _hp in range(8):
    for _h2 in range(2):
        _h = 2 * _hp + _h2
        for _it in range(NIT):
            _cum += SH[_h]
            L0GROUPS.append((_hp, _h2, _h, _it, _cum - 1))
NL0G = len(L0GROUPS)
_L0MASK_ORD = {t: i for i, t in enumerate(sorted(L0MASK))}


def build_l0attn():
    nc = bass.Bass("TRN2", target_bir_lowering=False, debug=False)
    q_in = nc.dram_tensor("q_fm", [DM, TPC], BF16, kind="ExternalInput").ap()
    kwin_in = nc.dram_tensor("kwin", [128, KW_TOT], BF16, kind="ExternalInput").ap()
    vwin_in = nc.dram_tensor("vwin", [128, VW_TOT], BF16, kind="ExternalInput").ap()
    biasv_in = nc.dram_tensor("biasv", [128, NL0T], F32, kind="ExternalInput").ap()
    maskv_in = nc.dram_tensor("maskv", [128, NL0M], F32, kind="ExternalInput").ap()
    iota_in = nc.dram_tensor("iota", [128, ITW], F32, kind="ExternalInput").ap()
    selhp_in = nc.dram_tensor("selhp", [16, 1024], BF16, kind="ExternalInput").ap()
    wo_in = nc.dram_tensor("Wo", [DM, DM], F32, kind="ExternalInput").ap()
    bo_in = nc.dram_tensor("bo_pc", [128, 8], F32, kind="ExternalInput").ap()
    x_in = nc.dram_tensor("x_fm", [DM, TPC], F32, kind="ExternalInput").ap()
    x1_out = nc.dram_tensor("x1_fm", [DM, TPC], BF16, kind="ExternalOutput").ap()
    d_l = nc.dram_tensor("d_l", [NL0G, ITW], F32, kind="Internal").ap()

    NPB = 2
    ctx = ExitStack()
    sb, ps, sem = _mk(nc, ctx)
    with ctx:
        q_sb = [sb(f"q{c}", [128, TPC], BF16) for c in range(NCH)]
        attn_sb = [sb(f"attn{c}", [128, TPC], BF16) for c in range(NCH)]
        kwin_sb = sb("kwin_s", [128, KW_TOT], BF16)
        vwin_sb = sb("vwin_s", [128, VW_TOT], BF16)
        biasv_sb = sb("biasv_s", [128, NL0T], F32)
        maskv_sb = sb("maskv_s", [128, NL0M], F32)
        iota_sb = sb("iota_s", [128, ITW], F32)
        selhp_sb = sb("selhp_s", [16, 1024], BF16)
        bo_sb = sb("bo", [128, 8], F32)
        p_sb = [sb(f"p{i}", [128, ITW], BF16) for i in range(NPB)]
        pm_sb = [sb(f"pm{i}", [128, ITW], BF16) for i in range(2)]
        msk_sb = [sb(f"msk{i}", [128, ITW], BF16) for i in range(2)]
        l_stage = [sb(f"lst{i}", [1, ITW], F32) for i in range(2)]
        l_pm = sb("l_pm", [16, TPC], F32)
        r_bf = sb("r_bf", [16, TPC], BF16)
        r_f32 = sb("r_f32", [16, TPC], F32)
        wo_stage = sb("wo_stage", [128, DM], F32)
        wo_bf = [sb(f"wo_bf{i}", [128, DM], BF16) for i in range(2)]
        x_stage = [sb("xs0", [128, TPC], F32)]
        ones_col = sb("ones_col", [128, 1], BF16)
        ones_row = sb("ones_row", [1, 128], BF16)
        ysq_sb = [sb(f"ysq{i}", [128, 512], BF16) for i in range(2)]
        mu_f = sb("mu_f", [1, 512], F32)
        m2_f = sb("m2_f", [1, 512], F32)
        t_f = sb("t_f", [1, 512], F32)
        var_f = sb("var_f", [1, 512], F32)
        lnv_f = sb("lnv_f", [1, 512], F32)
        rstd_f = sb("rstd_f", [1, 512], F32)
        mu_bf = sb("mu_bf", [1, 512], BF16)
        rstd_bf = sb("rstd_bf", [1, 512], BF16)
        tnorm = sb("tnorm", [128, 512], BF16)
        eps_sb = sb("eps_sb", [1, 1], F32)

        s_ld = sem("s_ld"); s_mm = sem("s_mm"); s_exp = sem("s_exp")
        s_msk = sem("s_msk"); s_av = sem("s_av"); s_ev = sem("s_ev")
        s_ldma = sem("s_ldma"); s_r = sem("s_r"); s_bc = sem("s_bc")
        s_n = sem("s_n"); s_wo = sem("s_wo"); s_xl = sem("s_xl")
        s_o = sem("s_o"); s_y = sem("s_y"); s_sq = sem("s_sq")
        s_s = sem("s_s"); s_stat = sem("s_stat"); s_bcln = sem("s_bcln")
        s_x1 = sem("s_x1"); s_st = sem("s_st")
        s_v = sem("s_v"); s_g = sem("s_g"); s_scs = sem("s_scs")
        s_s2 = sem("s_s2")

        NLD0 = 8 + 7

        mask_list = sorted(L0MASK)

        ctxA = ExitStack()
        ps_sc = [ctxA.enter_context(nc.psum_tensor(f"ps_sc{i}", [128, ITW], F32))
                 for i in range(2)]
        ps_av = [ctxA.enter_context(nc.psum_tensor(f"ps_av{i}", [65, ITW], F32))
                 for i in range(2)]
        with ctxA:
            with nc.Block() as block:
                @block.sync
                def _(sync):
                    for c in range(NCH):
                        sync.dma_start(out=q_sb[c][:],
                                       in_=q_in[128 * c:128 * (c + 1), :]).then_inc(s_ld, 16)
                    for dst, src in ((kwin_sb, kwin_in), (vwin_sb, vwin_in),
                                     (biasv_sb, biasv_in), (maskv_sb, maskv_in),
                                     (iota_sb, iota_in), (selhp_sb, selhp_in),
                                     (bo_sb, bo_in)):
                        sync.dma_start(out=dst[:], in_=src).then_inc(s_ld, 16)
                    for g, (hp, h2, h, it, endt) in enumerate(L0GROUPS):
                        sync.wait_ge(s_ev, g + 1)
                        sync.dma_start(out=d_l[g:g + 1, :],
                                       in_=l_stage[g % 2][:]).then_inc(s_ldma, 16)
                        sync.wait_ge(s_ldma, 16 * (g + 1))
                    sync.wait_ge(s_ldma, 16 * NL0G)
                    sync.dma_start(
                        out=l_pm[:],
                        in_=d_l.rearrange("(h it) w -> h it w", it=NIT)).then_inc(
                            s_ldma, 16)

                @block.tensor
                def _(tensor):
                    tensor.wait_ge(s_ld, 16 * NLD0)
                    gidx = -1
                    for t, (hp, h2, h, it, s) in enumerate(L0TILES):
                        first = (s == 0); last = (s == SH[h] - 1)
                        if first:
                            gidx += 1
                        scb = t % 2
                        if t >= 2:
                            tensor.wait_ge(s_exp, t - 1)
                        lk = kwin_sb[64 * h2:64 * h2 + 64,
                                     int(KOFF[hp]) + 128 * s:int(KOFF[hp]) + 128 * (s + 1)]
                        rq = q_sb[hp]
                        tensor.matmul(ps_sc[scb][:, 0:512], lk,
                                      rq[64 * h2:64 * h2 + 64, ITW * it:ITW * it + 512])
                        tensor.matmul(ps_sc[scb][:, 512:1024], lk,
                                      rq[64 * h2:64 * h2 + 64,
                                         ITW * it + 512:ITW * it + 1024]).then_inc(s_mm, 1)
                        if first and gidx >= 2:
                            tensor.wait_ge(s_ev, gidx - 1)
                        if t in L0MASK:
                            tensor.wait_ge(s_msk, _L0MASK_ORD[t] + 1)
                            rhs_p = pm_sb[_L0MASK_ORD[t] % 2]
                        else:
                            tensor.wait_ge(s_exp, t + 1)
                            rhs_p = p_sb[t % NPB]
                        slot = int(VOFF[h]) + s
                        lv = vwin_sb[:, 65 * slot:65 * slot + 65]
                        tensor.matmul(ps_av[gidx % 2][0:65, 0:512], lv,
                                      rhs_p[:, 0:512], start=first, stop=last)
                        tensor.matmul(ps_av[gidx % 2][0:65, 512:1024], lv,
                                      rhs_p[:, 512:1024], start=first,
                                      stop=last).then_inc(s_av, 1)

                @block.scalar
                def _(scalar):
                    for t, (hp, h2, h, it, s) in enumerate(L0TILES):
                        scalar.wait_ge(s_mm, t + 1)
                        if t >= NPB:
                            scalar.wait_ge(s_av, t - NPB + 1)
                        scalar.activation(p_sb[t % NPB][:], ps_sc[t % 2][:],
                                          AF.Exp, bias=biasv_sb[:, t:t + 1],
                                          scale=0.125).then_inc(s_exp, 1)

                @block.gpsimd
                def _(gpsimd):
                    for mi, t in enumerate(mask_list):
                        gpsimd.wait_ge(s_exp, t + 1)
                        if mi >= 2:
                            gpsimd.wait_ge(s_av, mask_list[mi - 2] + 1)
                        gpsimd.tensor_scalar(msk_sb[mi % 2][:], iota_sb[:],
                                             maskv_sb[:, _L0MASK_ORD[t]:
                                                      _L0MASK_ORD[t] + 1],
                                             None, ALU.is_ge).then_inc(s_g, 1)
                        gpsimd.wait_ge(s_g, mi + 1)
                        gpsimd.tensor_tensor(pm_sb[mi % 2][:], p_sb[t % NPB][:],
                                             msk_sb[mi % 2][:],
                                             ALU.mult).then_inc(s_msk, 1)

                @block.vector
                def _(vector):
                    for g, (hp, h2, h, it, endt) in enumerate(L0GROUPS):
                        vector.wait_ge(s_av, endt + 1)
                        if g >= 2:
                            vector.wait_ge(s_ldma, 16 * g)
                        vector.tensor_copy(l_stage[g % 2][:],
                                           ps_av[g % 2][64:65, :])
                        vector.tensor_copy(
                            attn_sb[hp][64 * h2:64 * h2 + 64,
                                        ITW * it:ITW * (it + 1)],
                            ps_av[g % 2][0:64, :]).then_inc(s_ev, 1)
                    vector.wait_ge(s_ldma, 16 * (NL0G + 1))
                    vector.reciprocal(r_f32[:], l_pm[:]).then_inc(s_v, 1)
                    vector.wait_ge(s_v, 1)
                    vector.tensor_copy(r_bf[:], r_f32[:]).then_inc(s_r, 1)

        ctxB = ExitStack()
        ps_br = [ctxB.enter_context(nc.psum_tensor(f"ps_br{i}", [128, 512], F32))
                 for i in range(2)]
        ps_o = [ctxB.enter_context(nc.psum_tensor(f"ps_o{i}", [128, 512], F32))
                for i in range(2)]
        ps_s1 = ctxB.enter_context(nc.psum_tensor("ps_s1", [1, 512], F32))
        ps_s2 = ctxB.enter_context(nc.psum_tensor("ps_s2", [1, 512], F32))
        ps_bm = ctxB.enter_context(nc.psum_tensor("ps_bm", [128, 512], F32))
        ps_brd = ctxB.enter_context(nc.psum_tensor("ps_brd", [128, 512], F32))
        with ctxB:
            with nc.Block() as block2:
                @block2.sync
                def _(sync):
                    for dc in range(NCH):
                        if dc >= 1:
                            sync.wait_ge(s_wo, dc)
                        sync.dma_start(
                            out=wo_stage[:].rearrange("p (c w) -> p c w", w=128),
                            in_=wo_in.rearrange("(c p) w -> p c w", p=128)
                            [:, :, 128 * dc:128 * (dc + 1)]).then_inc(s_ld, 16)
                        if dc >= 1:
                            sync.wait_ge(s_y, 4 * dc)
                        sync.dma_start(out=x_stage[0][:],
                                       in_=x_in[128 * dc:128 * (dc + 1), :]).then_inc(s_xl, 16)
                    for c in range(NCH):
                        sync.wait_ge(s_x1, 3 * NCH + c + 1)
                        sync.dma_start(out=x1_out[128 * c:128 * (c + 1), :],
                                       in_=q_sb[c][:]).then_inc(s_st, 16)
                    sync.wait_ge(s_st, 16 * NCH)

                @block2.vector
                def _(vector):
                    vector.memset(ones_col[:], 1.0)
                    vector.memset(ones_row[:], 1.0)
                    vector.memset(eps_sb[:], 1e-5)
                    k = 0
                    for hp in range(NCH):
                        for its in range(4):
                            vector.wait_ge(s_bc, k + 1)
                            vector.tensor_tensor(
                                q_sb[hp][:, 512 * its:512 * (its + 1)],
                                attn_sb[hp][:, 512 * its:512 * (its + 1)],
                                ps_br[k % 2][:], ALU.mult).then_inc(s_n, 1)
                            k += 1
                    j = 0
                    for dc in range(NCH):
                        vector.wait_ge(s_ld, 16 * (NLD0 + dc + 1))
                        vector.tensor_copy(wo_bf[dc % 2][:],
                                           wo_stage[:]).then_inc(s_wo, 1)
                        vector.wait_ge(s_xl, 16 * (dc + 1))
                        for its in range(4):
                            vector.wait_ge(s_o, j + 1)
                            vector.scalar_tensor_tensor(
                                attn_sb[dc][:, 512 * its:512 * (its + 1)],
                                ps_o[j % 2][:], bo_sb[:, dc:dc + 1],
                                x_stage[0][:, 512 * its:512 * (its + 1)],
                                ALU.add, ALU.add).then_inc(s_y, 1)
                            j += 1
                    vector.wait_ge(s_y, 4 * NCH)
                    nv = 0
                    for its in range(4):
                        vector.wait_ge(s_s, its + 1)
                        vector.wait_ge(s_s2, NCH * (its + 1))
                        vector.tensor_scalar(mu_f[:], ps_s1[:], 1.0 / DM, None,
                                             ALU.mult)
                        vector.tensor_scalar(m2_f[:], ps_s2[:], 1.0 / DM, None,
                                             ALU.mult).then_inc(s_v, 1)
                        nv += 1
                        vector.wait_ge(s_v, nv)
                        vector.tensor_tensor(t_f[:], mu_f[:], mu_f[:],
                                             ALU.mult).then_inc(s_v, 1)
                        nv += 1
                        vector.wait_ge(s_v, nv)
                        vector.tensor_tensor(var_f[:], m2_f[:], t_f[:],
                                             ALU.subtract).then_inc(s_v, 1)
                        nv += 1
                        vector.tensor_copy(mu_bf[:], mu_f[:]).then_inc(s_stat, 1)
                        vector.wait_ge(s_stat, 3 * its + 2)
                        vector.tensor_copy(rstd_bf[:], rstd_f[:]).then_inc(s_stat, 1)
                        for c in range(NCH):
                            vector.wait_ge(s_bcln, 2 * (its + 1))
                            vector.tensor_tensor(
                                tnorm[:], attn_sb[c][:, 512 * its:512 * (its + 1)],
                                ps_bm[:], ALU.subtract).then_inc(s_v, 1)
                            nv += 1
                            vector.wait_ge(s_v, nv)
                            vector.tensor_tensor(
                                q_sb[c][:, 512 * its:512 * (its + 1)],
                                tnorm[:], ps_brd[:], ALU.mult).then_inc(s_x1, 1)

                @block2.scalar
                def _(scalar):
                    for its in range(4):
                        scalar.wait_ge(s_stat, 3 * its + 1)
                        scalar.activation(lnv_f[:], var_f[:], AF.Ln,
                                          bias=eps_sb[0:1, 0:1]).then_inc(s_scs, 1)
                        scalar.wait_ge(s_scs, its + 1)
                        scalar.activation(rstd_f[:], lnv_f[:], AF.Exp,
                                          scale=-0.5).then_inc(s_stat, 1)

                @block2.gpsimd
                def _(gpsimd):
                    gq = 0
                    for its in range(4):
                        for c in range(NCH):
                            gpsimd.wait_ge(s_y, 4 * c + its + 1)
                            if gq >= 2:
                                gpsimd.wait_ge(s_s2, gq - 1)
                            gpsimd.tensor_tensor(
                                ysq_sb[gq % 2][:],
                                attn_sb[c][:, 512 * its:512 * (its + 1)],
                                attn_sb[c][:, 512 * its:512 * (its + 1)],
                                ALU.mult).then_inc(s_sq, 1)
                            gq += 1

                @block2.tensor
                def _(tensor):
                    tensor.wait_ge(s_r, 1)
                    k = 0
                    for hp in range(NCH):
                        for its in range(4):
                            if k >= 2:
                                tensor.wait_ge(s_n, k - 1)
                            tensor.matmul(ps_br[k % 2][:],
                                          selhp_sb[0:16, 128 * hp:128 * (hp + 1)],
                                          r_bf[0:16, 512 * its:512 * (its + 1)],
                                          start=True, stop=True).then_inc(s_bc, 1)
                            k += 1
                    tensor.wait_ge(s_n, 4 * NCH)
                    j = 0
                    for dc in range(NCH):
                        tensor.wait_ge(s_wo, dc + 1)
                        for its in range(4):
                            if j >= 2:
                                tensor.wait_ge(s_y, j - 1)
                            for hp in range(NCH):
                                ins = tensor.matmul(
                                    ps_o[j % 2][:],
                                    wo_bf[dc % 2][:, 128 * hp:128 * (hp + 1)],
                                    q_sb[hp][:, 512 * its:512 * (its + 1)],
                                    start=(hp == 0), stop=(hp == NCH - 1))
                            ins.then_inc(s_o, 1)
                            j += 1
                    gq = 0
                    for its in range(4):
                        if its >= 1:
                            tensor.wait_ge(s_stat, 3 * its)
                        for c in range(NCH):
                            ins = tensor.matmul(ps_s1[:], ones_col[:],
                                                attn_sb[c][:, 512 * its:512 * (its + 1)],
                                                start=(c == 0), stop=(c == NCH - 1))
                        ins.then_inc(s_s, 1)
                        for c in range(NCH):
                            tensor.wait_ge(s_sq, gq + 1)
                            ins = tensor.matmul(ps_s2[:], ones_col[:],
                                                ysq_sb[gq % 2][:],
                                                start=(c == 0), stop=(c == NCH - 1))
                            ins.then_inc(s_s2, 1)
                            gq += 1
                        tensor.wait_ge(s_stat, 3 * its + 1)
                        if its >= 1:
                            tensor.wait_ge(s_x1, 8 * its)
                        tensor.matmul(ps_bm[:], ones_row[:], mu_bf[:],
                                      start=True, stop=True)
                        tensor.wait_ge(s_stat, 3 * its + 3)
                        tensor.matmul(ps_brd[:], ones_row[:], rstd_bf[:],
                                      start=True, stop=True).then_inc(s_bcln, 2)
    return nc





class LNEmit:
    def __init__(self, nc, ctx, ctxP, name):
        sb = lambda n, shape, dt=F32: ctx.enter_context(
            nc.sbuf_tensor(f"{name}_{n}", shape, dt))
        self.ps_s1 = ctxP.enter_context(nc.psum_tensor(f"{name}_ps_s1", [1, 512], F32))
        self.ps_s2 = ctxP.enter_context(nc.psum_tensor(f"{name}_ps_s2", [1, 512], F32))
        self.ps_bm = ctxP.enter_context(nc.psum_tensor(f"{name}_ps_bm", [128, 512], F32))
        self.ps_brd = ctxP.enter_context(nc.psum_tensor(f"{name}_ps_brd", [128, 512], F32))
        self.mu_f = sb("mu_f", [1, 512]); self.m2_f = sb("m2_f", [1, 512])
        self.t_f = sb("t_f", [1, 512]); self.var_f = sb("var_f", [1, 512])
        self.lnv_f = sb("lnv_f", [1, 512]); self.rstd_f = sb("rstd_f", [1, 512])
        self.mu_bf = sb("mu_bf", [1, 512], BF16)
        self.rstd_bf = sb("rstd_bf", [1, 512], BF16)
        self.tnorm = sb("tnorm", [128, 512], BF16)
        self.ysq = [sb(f"ysq{i}", [128, 512], BF16) for i in range(2)]
        self.ones_col = sb("ones_col", [128, 1], BF16)
        self.ones_row = sb("ones_row", [1, 128], BF16)
        self.eps = sb("eps", [1, 1], F32)
        sem = lambda n: ctx.enter_context(nc.semaphore(f"{name}_{n}"))
        self.s_sq = sem("sq"); self.s_s = sem("s"); self.s_s2 = sem("s2")
        self.s_stat = sem("stat"); self.s_bcln = sem("bcln")
        self.s_x1 = sem("x1"); self.s_v = sem("v"); self.s_scs = sem("scs")
        self.s_ini = sem("ini")
        self.nv = 0

    def vector_init(self, vector):
        vector.memset(self.ones_col[:], 1.0)
        vector.memset(self.ones_row[:], 1.0)
        vector.memset(self.eps[:], 1e-5).then_inc(self.s_ini, 1)

    def pool_ops(self, gpsimd, y_ap, wait_y):
        gq = 0
        for its in range(4):
            for c in range(NCH):
                wait_y(gpsimd, its, c)
                if gq >= 2:
                    gpsimd.wait_ge(self.s_s2, gq - 1)
                gpsimd.tensor_tensor(self.ysq[gq % 2][:], y_ap(c, its),
                                     y_ap(c, its), ALU.mult).then_inc(self.s_sq, 1)
                gq += 1

    def tensor_ops(self, tensor, y_ap, wait_y):
        gq = 0
        for its in range(4):
            if its >= 1:
                tensor.wait_ge(self.s_stat, 3 * its)
            for c in range(NCH):
                wait_y(tensor, its, c)
                ins = tensor.matmul(self.ps_s1[:], self.ones_col[:],
                                    y_ap(c, its),
                                    start=(c == 0), stop=(c == NCH - 1))
            ins.then_inc(self.s_s, 1)
            for c in range(NCH):
                tensor.wait_ge(self.s_sq, gq + 1)
                tensor.matmul(self.ps_s2[:], self.ones_col[:],
                              self.ysq[gq % 2][:],
                              start=(c == 0),
                              stop=(c == NCH - 1)).then_inc(self.s_s2, 1)
                gq += 1
            tensor.wait_ge(self.s_stat, 3 * its + 1)
            if its >= 1:
                tensor.wait_ge(self.s_x1, 8 * its)
            tensor.matmul(self.ps_bm[:], self.ones_row[:], self.mu_bf[:],
                          start=True, stop=True)
            tensor.wait_ge(self.s_stat, 3 * its + 3)
            tensor.matmul(self.ps_brd[:], self.ones_row[:], self.rstd_bf[:],
                          start=True, stop=True).then_inc(self.s_bcln, 2)

    def scalar_ops(self, scalar):
        scalar.wait_ge(self.s_ini, 1)
        for its in range(4):
            scalar.wait_ge(self.s_stat, 3 * its + 1)
            scalar.activation(self.lnv_f[:], self.var_f[:], AF.Ln,
                              bias=self.eps[0:1, 0:1]).then_inc(self.s_scs, 1)
            scalar.wait_ge(self.s_scs, its + 1)
            scalar.activation(self.rstd_f[:], self.lnv_f[:], AF.Exp,
                              scale=-0.5).then_inc(self.s_stat, 1)

    def vector_ops(self, vector, y_ap, out_ap):
        for its in range(4):
            vector.wait_ge(self.s_s, its + 1)
            vector.wait_ge(self.s_s2, NCH * (its + 1))
            vector.tensor_scalar(self.mu_f[:], self.ps_s1[:], 1.0 / DM, None,
                                 ALU.mult)
            vector.tensor_scalar(self.m2_f[:], self.ps_s2[:], 1.0 / DM, None,
                                 ALU.mult).then_inc(self.s_v, 1)
            self.nv += 1
            vector.wait_ge(self.s_v, self.nv)
            vector.tensor_tensor(self.t_f[:], self.mu_f[:], self.mu_f[:],
                                 ALU.mult).then_inc(self.s_v, 1)
            self.nv += 1
            vector.wait_ge(self.s_v, self.nv)
            vector.tensor_tensor(self.var_f[:], self.m2_f[:], self.t_f[:],
                                 ALU.subtract).then_inc(self.s_v, 1)
            self.nv += 1
            vector.tensor_copy(self.mu_bf[:], self.mu_f[:]).then_inc(self.s_stat, 1)
            vector.wait_ge(self.s_stat, 3 * its + 2)
            vector.tensor_copy(self.rstd_bf[:],
                               self.rstd_f[:]).then_inc(self.s_stat, 1)
            for c in range(NCH):
                vector.wait_ge(self.s_bcln, 2 * (its + 1))
                vector.tensor_tensor(self.tnorm[:], y_ap(c, its),
                                     self.ps_bm[:], ALU.subtract).then_inc(self.s_v, 1)
                self.nv += 1
                vector.wait_ge(self.s_v, self.nv)
                vector.tensor_tensor(out_ap(c, its), self.tnorm[:],
                                     self.ps_brd[:], ALU.mult).then_inc(self.s_x1, 1)



def build_l0ffn():
    nc = bass.Bass("TRN2", target_bir_lowering=False, debug=False)
    x1_in = nc.dram_tensor("x1_fm", [DM, TPC], BF16, kind="ExternalInput").ap()
    halo_in = nc.dram_tensor("halo_pc", [128, 8], F32, kind="ExternalInput").ap()
    mur_in = nc.dram_tensor("mur_pc", [128, 8], F32, kind="ExternalInput").ap()
    muk_in = nc.dram_tensor("muk_pc", [128, 8], F32, kind="ExternalInput").ap()
    w_in = {n: nc.dram_tensor(n, [DM, DM], F32, kind="ExternalInput").ap()
            for n in ("WrT", "WkT", "WvT")}
    x2_out = nc.dram_tensor("x2_fm", [DM, TPC], BF16, kind="ExternalOutput").ap()

    ctx = ExitStack()
    sb, ps, sem = _mk(nc, ctx)
    with ctx:
        x1_sb = [sb(f"x1_{c}", [128, TPC], BF16) for c in range(NCH)]
        xr_sb = [sb(f"xr_{c}", [128, TPC], BF16) for c in range(NCH)]
        xk_sb = [sb(f"xk_{c}", [128, TPC], BF16) for c in range(NCH)]
        ka2_sb = [sb(f"ka2_{c}", [128, TPC], BF16) for c in range(NCH)]
        wA = [sb(f"wA{c}", [128, DM], BF16) for c in range(NCH)]
        wB = [sb(f"wB{c}", [128, DM], BF16) for c in range(NCH)]
        wstage = sb("wstage", [128, DM], F32)
        halo_sb = sb("halo_s", [128, 8], F32)
        mur_sb = sb("mur_s", [128, 8], F32)
        muk_sb = sb("muk_s", [128, 8], F32)
        xprev = sb("xprev", [128, TPC], BF16)
        dtmp = sb("dtmp", [128, TPC], BF16)
        ka_t = [sb(f"ka_t{i}", [128, 512], BF16) for i in range(2)]
        r_t = [sb(f"r_t{i}", [128, 512], BF16) for i in range(2)]
        t_t = [sb(f"t_t{i}", [128, 512], BF16) for i in range(2)]

        s_ld = sem("s_ld"); s_cast = sem("s_cast"); s_x = sem("s_x")
        s_ldw3 = sem("s_ldw3")
        s_mmk = sem("s_mmk"); s_rl = sem("s_rl"); s_ka2 = sem("s_ka2")
        s_mm2 = sem("s_mm2"); s_sg = sem("s_sg"); s_ff = sem("s_ff")
        s_y = sem("s_y"); s_st = sem("s_st"); s_v2 = sem("s_v2")

        ctxA = ExitStack()
        ps_k = [ctxA.enter_context(nc.psum_tensor(f"ps_k{i}", [128, 512], F32))
                for i in range(2)]
        ps_kv = [ctxA.enter_context(nc.psum_tensor(f"ps_kv{i}", [128, 512], F32))
                 for i in range(2)]
        ps_r = [ctxA.enter_context(nc.psum_tensor(f"ps_r{i}", [128, 512], F32))
                for i in range(2)]
        with ctxA:
            with nc.Block() as block:
                @block.sync
                def _(sync):
                    for c in range(NCH):
                        sync.dma_start(out=x1_sb[c][:],
                                       in_=x1_in[128 * c:128 * (c + 1), :]).then_inc(s_ld, 16)
                    for t, src_ in ((halo_sb, halo_in), (mur_sb, mur_in),
                                    (muk_sb, muk_in)):
                        sync.dma_start(out=t[:], in_=src_).then_inc(s_ld, 16)
                    i = 0
                    for wn in ("WkT", "WrT", "WvT"):
                        for c in range(NCH):
                            if i > 0:
                                sync.wait_ge(s_cast, i)
                            if wn == "WrT" and c == 0:
                                sync.wait_ge(s_mmk, 32)
                            sync.dma_start(out=wstage[:],
                                           in_=w_in[wn][128 * c:128 * (c + 1),
                                                        :]).then_inc(s_ldw3, 16)
                            i += 1

                @block.vector
                def _(vector):
                    i = 0
                    for c in range(NCH):
                        vector.wait_ge(s_ldw3, 16 * (i + 1))
                        vector.tensor_copy(wA[c][:],
                                           wstage[:]).then_inc(s_cast, 1)
                        i += 1
                    nv = 0
                    vector.wait_ge(s_ld, 16 * 11)
                    for c in range(NCH):
                        vector.tensor_copy(xprev[:, 1:TPC],
                                           x1_sb[c][:, 0:TPC - 1])
                        vector.tensor_copy(xprev[:, 0:1],
                                           halo_sb[:, c:c + 1]).then_inc(s_v2, 1)
                        nv += 1
                        vector.wait_ge(s_v2, nv)
                        vector.tensor_tensor(dtmp[:], x1_sb[c][:], xprev[:],
                                             ALU.subtract).then_inc(s_v2, 1)
                        nv += 1
                        vector.wait_ge(s_v2, nv)
                        vector.scalar_tensor_tensor(
                            xr_sb[c][:], dtmp[:], mur_sb[:, c:c + 1], xprev[:],
                            ALU.mult, ALU.add)
                        vector.scalar_tensor_tensor(
                            xk_sb[c][:], dtmp[:], muk_sb[:, c:c + 1], xprev[:],
                            ALU.mult, ALU.add).then_inc(s_x, 1)
                    for wn, dst in (("WrT", wA), ("WvT", wB)):
                        for c in range(NCH):
                            vector.wait_ge(s_ldw3, 16 * (i + 1))
                            vector.tensor_copy(dst[c][:],
                                               wstage[:]).then_inc(s_cast, 1)
                            i += 1
                    j = 0
                    for hp in range(NCH):
                        for its in range(4):
                            vector.wait_ge(s_sg, j + 1)
                            vector.scalar_tensor_tensor(
                                t_t[j % 2][:], ps_kv[j % 2][:], 1.0,
                                r_t[j % 2][:], ALU.mult, ALU.mult).then_inc(s_ff, 1)
                            vector.wait_ge(s_ff, j + 1)
                            vector.tensor_tensor(
                                x1_sb[hp][:, 512 * its:512 * (its + 1)],
                                x1_sb[hp][:, 512 * its:512 * (its + 1)],
                                t_t[j % 2][:], ALU.add).then_inc(s_y, 1)
                            j += 1

                @block.tensor
                def _(tensor):
                    j = 0
                    for hp in range(NCH):
                        for its in range(4):
                            tensor.wait_ge(s_cast, 8)
                            tensor.wait_ge(s_x, NCH)
                            if j >= 2:
                                tensor.wait_ge(s_rl, j - 1)
                            for c in range(NCH):
                                ins = tensor.matmul(
                                    ps_k[j % 2][:],
                                    wA[c][:, 128 * hp:128 * (hp + 1)],
                                    xk_sb[c][:, 512 * its:512 * (its + 1)],
                                    start=(c == 0), stop=(c == NCH - 1))
                            ins.then_inc(s_mmk, 1)
                            j += 1
                    tensor.wait_ge(s_cast, 24)
                    tensor.wait_ge(s_ka2, 32)
                    j = 0
                    for hp in range(NCH):
                        for its in range(4):
                            if j >= 2:
                                tensor.wait_ge(s_ff, j - 1)
                            for c in range(NCH):
                                ins = tensor.matmul(
                                    ps_kv[j % 2][:],
                                    wB[c][:, 128 * hp:128 * (hp + 1)],
                                    ka2_sb[c][:, 512 * its:512 * (its + 1)],
                                    start=(c == 0), stop=(c == NCH - 1))
                            ins.then_inc(s_mm2, 1)
                            if j >= 2:
                                tensor.wait_ge(s_sg, j - 1)
                            for c in range(NCH):
                                ins = tensor.matmul(
                                    ps_r[j % 2][:],
                                    wA[c][:, 128 * hp:128 * (hp + 1)],
                                    xr_sb[c][:, 512 * its:512 * (its + 1)],
                                    start=(c == 0), stop=(c == NCH - 1))
                            ins.then_inc(s_mm2, 1)
                            j += 1

                @block.scalar
                def _(scalar):
                    for j in range(32):
                        scalar.wait_ge(s_mmk, j + 1)
                        if j >= 2:
                            scalar.wait_ge(s_ka2, j - 1)
                        scalar.activation(ka_t[j % 2][:], ps_k[j % 2][:],
                                          AF.Relu).then_inc(s_rl, 1)
                    for j in range(32):
                        scalar.wait_ge(s_mm2, 2 * j + 2)
                        if j >= 2:
                            scalar.wait_ge(s_ff, j - 1)
                        scalar.activation(r_t[j % 2][:], ps_r[j % 2][:],
                                          AF.Sigmoid).then_inc(s_sg, 1)

                @block.gpsimd
                def _(gpsimd):
                    for j in range(32):
                        hp, its = divmod(j, 4)
                        gpsimd.wait_ge(s_rl, j + 1)
                        gpsimd.tensor_tensor(
                            ka2_sb[hp][:, 512 * its:512 * (its + 1)],
                            ka_t[j % 2][:], ka_t[j % 2][:],
                            ALU.mult).then_inc(s_ka2, 1)

        ctxB = ExitStack()
        ln = LNEmit(nc, ctx, ctxB, "ln2")
        with ctxB:
            y_ap = lambda c, its: x1_sb[c][:, 512 * its:512 * (its + 1)]
            out_ap = lambda c, its: xr_sb[c][:, 512 * its:512 * (its + 1)]

            def wait_y(eng, its, c):
                eng.wait_ge(s_y, 4 * NCH)

            with nc.Block() as block2:
                @block2.sync
                def _(sync):
                    for c in range(NCH):
                        sync.wait_ge(ln.s_x1, 3 * NCH + c + 1)
                        sync.dma_start(out=x2_out[128 * c:128 * (c + 1), :],
                                       in_=xr_sb[c][:]).then_inc(s_st, 16)
                    sync.wait_ge(s_st, 16 * NCH)

                @block2.vector
                def _(vector):
                    ln.vector_init(vector)
                    ln.vector_ops(vector, y_ap, out_ap)

                @block2.scalar
                def _(scalar):
                    ln.scalar_ops(scalar)

                @block2.gpsimd
                def _(gpsimd):
                    ln.pool_ops(gpsimd, y_ap, wait_y)

                @block2.tensor
                def _(tensor):
                    tensor.wait_ge(ln.s_ini, 1)
                    ln.tensor_ops(tensor, y_ap, wait_y)
    return nc



def build_l1qkv():
    nc = bass.Bass("TRN2", target_bir_lowering=False, debug=False)
    x2_in = nc.dram_tensor("x2_fm", [DM, TPC], BF16, kind="ExternalInput").ap()
    w_in = {n: nc.dram_tensor(n, [DM, DM], F32, kind="ExternalInput").ap()
            for n in ("WqT", "WkT", "WvT")}
    b_in = {n: nc.dram_tensor(n, [128, 8], F32, kind="ExternalInput").ap()
            for n in ("bq_pc", "bk_pc", "bv_pc")}
    q_out = nc.dram_tensor("q_fm", [DM, TPC], BF16, kind="ExternalOutput").ap()
    k_out = nc.dram_tensor("k_fm", [DM, TPC], BF16, kind="ExternalOutput").ap()
    v_out = nc.dram_tensor("v_fm", [DM, TPC], BF16, kind="ExternalOutput").ap()
    sv_out = nc.dram_tensor("sv_p", [128, 8], F32, kind="ExternalOutput").ap()

    ctx = ExitStack()
    sb, ps, sem = _mk(nc, ctx)
    with ctx:
        x_sb = [sb(f"x_{c}", [128, TPC], BF16) for c in range(NCH)]
        wstage = sb("wstage", [128, DM], F32)
        w_bf = [sb(f"w_bf{c}", [128, DM], BF16) for c in range(NCH)]
        bias_sb = {n: sb(n + "_s", [128, 8], F32) for n in ("bq", "bk", "bv")}
        evac = [sb(f"evac{i}", [128, TPC], BF16) for i in range(2)]
        sv_sb = sb("sv_s", [128, 8], F32)
        pst = [ps(f"pst{i}", [128, 512]) for i in range(4)]

        s_ld = sem("s_ld"); s_cast = sem("s_cast"); s_mm = sem("s_mm")
        s_ev = sem("s_ev"); s_st = sem("s_st"); s_sv = sem("s_sv")
        s_ldw4 = sem("s_ldw4")

        wnames = ("WqT", "WkT", "WvT")
        bnames = ("bq", "bk", "bv")
        outs = (q_out, k_out, v_out)

        with nc.Block() as block:
            @block.sync
            def _(sync):
                for c in range(NCH):
                    sync.dma_start(out=x_sb[c][:],
                                   in_=x2_in[128 * c:128 * (c + 1), :]).then_inc(s_ld, 16)
                for n in bnames:
                    sync.dma_start(out=bias_sb[n][:],
                                   in_=b_in[n + "_pc"]).then_inc(s_ld, 16)
                i = 0
                for wi, wn in enumerate(wnames):
                    for c in range(NCH):
                        if i > 0:
                            sync.wait_ge(s_cast, i)
                        if c == 0 and wi > 0:
                            sync.wait_ge(s_mm, 8 * wi)
                        sync.dma_start(out=wstage[:],
                                       in_=w_in[wn][128 * c:128 * (c + 1),
                                                    :]).then_inc(s_ldw4, 16)
                        i += 1

            @block.gpsimd
            def _(gpsimd):
                g = 0
                for wi in range(3):
                    for hp in range(NCH):
                        g += 1
                        gpsimd.wait_ge(s_ev, g)
                        gpsimd.dma_start(out=outs[wi][128 * hp:128 * (hp + 1), :],
                                         in_=evac[(g - 1) % 2][:]).then_inc(s_st, 16)
                        gpsimd.wait_ge(s_st, 16 * g)
                gpsimd.wait_ge(s_sv, NCH)
                gpsimd.dma_start(out=sv_out, in_=sv_sb[:]).then_inc(s_st, 16)
                gpsimd.wait_ge(s_st, 16 * (g + 1))

            @block.vector
            def _(vector):
                i = 0
                for wi in range(3):
                    for c in range(NCH):
                        vector.wait_ge(s_ldw4, 16 * (i + 1))
                        vector.tensor_copy(w_bf[c][:],
                                           wstage[:]).then_inc(s_cast, 1)
                        i += 1
                for hp in range(NCH):
                    vector.wait_ge(s_ev, 16 + hp + 1)
                    vector.reduce_sum(sv_sb[:, hp:hp + 1],
                                      evac[(16 + hp) % 2][:],
                                      axis=AX.X).then_inc(s_sv, 1)

            @block.scalar
            def _(scalar):
                scalar.wait_ge(s_ld, 16 * 11)
                g = 0
                for wi in range(3):
                    for hp in range(NCH):
                        g += 1
                        scalar.wait_ge(s_mm, g)
                        if g > 2:
                            scalar.wait_ge(s_st, 16 * (g - 2))
                        for its in range(4):
                            ins = scalar.activation(
                                evac[(g - 1) % 2][:, 512 * its:512 * (its + 1)],
                                pst[its][:], AF.Identity,
                                bias=bias_sb[bnames[wi]][:, hp:hp + 1])
                        ins.then_inc(s_ev, 1)

            @block.tensor
            def _(tensor):
                tensor.wait_ge(s_ld, 16 * 11)
                g = 0
                for wi in range(3):
                    for hp in range(NCH):
                        tensor.wait_ge(s_cast, 8 * wi + 8)
                        g += 1
                        if g > 1:
                            tensor.wait_ge(s_ev, g - 1)
                        for its in range(4):
                            for c in range(NCH):
                                ins = tensor.matmul(
                                    pst[its][:],
                                    w_bf[c][:, 128 * hp:128 * (hp + 1)],
                                    x_sb[c][:, 512 * its:512 * (its + 1)],
                                    start=(c == 0), stop=(c == NCH - 1))
                        ins.then_inc(s_mm, 1)
    return nc



DELTAS = [0, 1, 2, 4, 8, 16, 32, 64, 128, 256, 512, 1024, 2048, 4096]
BIGSLOTS = [(d, i) for d in (1024, 2048, 4096) for i in range(4)
            if 512 + 512 * i - d < 0]
BIGIDX = {s: i for i, s in enumerate(BIGSLOTS)}
NBIG = len(BIGSLOTS)


def build_l1attn():
    nc = bass.Bass("TRN2", target_bir_lowering=False, debug=False)
    x2_in = nc.dram_tensor("x2_fm", [DM, TPC], BF16, kind="ExternalInput").ap()
    q_in = nc.dram_tensor("q_fm", [DM, TPC], BF16, kind="ExternalInput").ap()
    kwin_in = nc.dram_tensor("kwin", [DM, 2560], BF16, kind="ExternalInput").ap()
    vwin_in = nc.dram_tensor("vwin", [DM, 2560], BF16, kind="ExternalInput").ap()
    kbig_in = nc.dram_tensor("kbig", [DM, 512 * NBIG], BF16, kind="ExternalInput").ap()
    vbig_in = nc.dram_tensor("vbig", [DM, 512 * NBIG], BF16, kind="ExternalInput").ap()
    sv4_in = nc.dram_tensor("sv4", [128, 32], F32, kind="ExternalInput").ap()
    selp_in = nc.dram_tensor("selp", [128, 522], BF16, kind="ExternalInput").ap()
    e2t_in = nc.dram_tensor("e2t", [2, 128], BF16, kind="ExternalInput").ap()
    wo_in = nc.dram_tensor("WoT", [DM, DM], F32, kind="ExternalInput").ap()
    bo_in = nc.dram_tensor("bo_pc", [128, 8], F32, kind="ExternalInput").ap()
    x3_out = nc.dram_tensor("x3_fm", [DM, TPC], BF16, kind="ExternalOutput").ap()

    ctx = ExitStack()
    sb, ps, sem = _mk(nc, ctx)
    with ctx:
        x2_sb = [sb(f"x2_{c}", [128, TPC], BF16) for c in range(NCH)]
        q_sb = [sb(f"q_{c}", [128, TPC], BF16) for c in range(NCH)]
        kwin_sb = [sb(f"kw_{c}", [128, 2560], BF16) for c in range(NCH)]
        vwin_sb = [sb(f"vw_{c}", [128, 2560], BF16) for c in range(NCH)]
        kbig_t = [sb(f"kb_{i}", [128, 512], BF16) for i in range(3)]
        vbig_t = [sb(f"vb_{i}", [128, 512], BF16) for i in range(3)]
        sv4_sb = sb("sv4_s", [128, 32], F32)
        sv_sb = sb("sv_s", [128, 8], F32)
        selp_sb = sb("selp_s", [128, 522], BF16)
        e2t_sb = sb("e2t_s", [2, 128], BF16)
        wstage = sb("wstage", [128, DM], F32)
        w_bf = [sb(f"w_bf{c}", [128, DM], BF16) for c in range(NCH)]
        bo_sb = sb("bo_s", [128, 8], F32)
        p_t = [sb(f"p_t{i}", [128, 512], BF16) for i in range(2)]
        pexp_sb = [sb(f"pexp{i}", [128, 512], BF16) for i in range(4)]
        u_sb = [sb(f"u{i}", [128, 512], BF16) for i in range(4)]
        tz_f = sb("tz_f", [2, 512], F32)
        rz_f = sb("rz_f", [2, 512], F32)
        rz_bf = sb("rz_bf", [2, 512], BF16)
        t_av = [sb(f"t_av{i}", [128, 512], BF16) for i in range(2)]
        acc = [sb(f"acc{i}", [128, 512], BF16) for i in range(2)]

        s_ld = sem("s_ld"); s_cast = sem("s_cast"); s_pp = sem("s_pp")
        s_scm = sem("s_scm"); s_ex = sem("s_ex"); s_u = sem("s_u")
        s_z = sem("s_z"); s_bc = sem("s_bc"); s_avv = sem("s_avv")
        s_rz = sem("s_rz"); s_bcz = sem("s_bcz"); s_ep = sem("s_ep")
        s_kb = sem("s_kb"); s_vb = sem("s_vb"); s_v5 = sem("s_v5")
        s_o = sem("s_o"); s_y = sem("s_y"); s_st = sem("s_st")

        ITERS = [(c, i) for c in range(NCH) for i in range(4)]
        big_uses = []
        for n, (c8, its) in enumerate(ITERS):
            for di, d in enumerate(DELTAS):
                if (d, its) in BIGIDX:
                    big_uses.append((n, di))
        buse_ord = {u: i for i, u in enumerate(big_uses)}
        NLD0 = 8 + 8 + 8 + 8 + 5

        e2_ap = lambda: selp_sb[:, 0:2]
        zsel_ap = lambda b: selp_sb[:, 2 + 2 * b:2 + 2 * b + 2]
        selu_ap = lambda k: selp_sb[:, 10 + 128 * k:10 + 128 * (k + 1)]

        def ksrc(c8, its, di):
            d = DELTAS[di]
            col = 512 + 512 * its - d
            if col >= 0:
                return kwin_sb[c8][:, col:col + 512]
            return kbig_t[buse_ord[(ITERS.index((c8, its)), di)] % 3][:]

        def vsrc(c8, its, di):
            d = DELTAS[di]
            col = 512 + 512 * its - d
            if col >= 0:
                return vwin_sb[c8][:, col:col + 512]
            return vbig_t[buse_ord[(ITERS.index((c8, its)), di)] % 3][:]

        ctxA = ExitStack()
        ps_sc = [ctxA.enter_context(nc.psum_tensor(f"ps_sc{i}", [128, 512], F32))
                 for i in range(4)]
        ps_z = ctxA.enter_context(nc.psum_tensor("ps_z", [2, 512], F32))
        ps_u = [ctxA.enter_context(nc.psum_tensor(f"ps_u{i}", [128, 512], F32))
                for i in range(2)]
        ps_rz = ctxA.enter_context(nc.psum_tensor("ps_rz", [128, 512], F32))
        with ctxA:
            with nc.Block() as block:
                @block.sync
                def _(sync):
                    nld = 0
                    for arr, src_ in ((x2_sb, x2_in), (q_sb, q_in),
                                      (kwin_sb, kwin_in), (vwin_sb, vwin_in)):
                        for c in range(NCH):
                            sync.dma_start(out=arr[c][:],
                                           in_=src_[128 * c:128 * (c + 1), :]).then_inc(s_ld, 16)
                            nld += 1
                    for t, src_ in ((sv4_sb, sv4_in), (selp_sb, selp_in),
                                    (e2t_sb, e2t_in), (bo_sb, bo_in)):
                        sync.dma_start(out=t[:], in_=src_).then_inc(s_ld, 16)
                        nld += 1
                    for bu_i, (n, di) in enumerate(big_uses):
                        c8, its = ITERS[n]
                        d = DELTAS[di]
                        slot = BIGIDX[(d, its)]
                        if bu_i >= 3:
                            pn, pdi = big_uses[bu_i - 3]
                            sync.wait_ge(s_pp, 14 * pn + pdi + 1)
                            sync.wait_ge(s_avv, 14 * pn + pdi + 1)
                        sync.dma_start(
                            out=kbig_t[bu_i % 3][:],
                            in_=kbig_in[128 * c8:128 * (c8 + 1),
                                        512 * slot:512 * (slot + 1)]).then_inc(s_kb, 16)
                        sync.dma_start(
                            out=vbig_t[bu_i % 3][:],
                            in_=vbig_in[128 * c8:128 * (c8 + 1),
                                        512 * slot:512 * (slot + 1)]).then_inc(s_vb, 16)
                        sync.wait_ge(s_kb, 16 * (bu_i + 1))
                        sync.wait_ge(s_vb, 16 * (bu_i + 1))

                @block.gpsimd
                def _(gpsimd):
                    pass

                @block.tensor
                def _(tensor):
                    tensor.wait_ge(s_ld, 16 * 36)
                    tensor.wait_ge(s_v5, 4)
                    for n, (c8, its) in enumerate(ITERS):
                        if n >= 1:
                            tensor.wait_ge(s_ep, n)
                        for di, d in enumerate(DELTAS):
                            t = 14 * n + di
                            b, k = di // 4, di % 4
                            tensor.wait_ge(s_pp, t + 1)
                            tensor.matmul(ps_sc[b][32 * k:32 * k + 2, :],
                                          e2_ap(), p_t[t % 2][:],
                                          start=True, stop=True,
                                          tile_position=(0, 32 * k)).then_inc(s_scm, 1)
                        for b in range(4):
                            tensor.wait_ge(s_u, 4 * n + b + 1)
                            ins = tensor.matmul(ps_z[0:2, :], zsel_ap(b),
                                                u_sb[b][:], start=(b == 0),
                                                stop=(b == 3))
                        ins.then_inc(s_z, 1)
                        for di, d in enumerate(DELTAS):
                            t = 14 * n + di
                            b, k = di // 4, di % 4
                            if t >= 2:
                                tensor.wait_ge(s_avv, t - 1)
                            tensor.matmul(ps_u[t % 2][:], selu_ap(k),
                                          u_sb[b][:], start=True,
                                          stop=True).then_inc(s_bc, 1)
                        tensor.wait_ge(s_rz, n + 1)
                        if n >= 1:
                            tensor.wait_ge(s_ep, n)
                        tensor.matmul(ps_rz[:], e2t_sb[0:2, :],
                                      rz_bf[0:2, :], start=True,
                                      stop=True).then_inc(s_bcz, 1)

                @block.scalar
                def _(scalar):
                    for n in range(len(ITERS)):
                        if n >= 1:
                            scalar.wait_ge(s_ep, n)
                        for b in range(4):
                            scalar.wait_ge(s_scm, 14 * n + min(4 * b + 4, 14))
                            if n >= 1:
                                scalar.wait_ge(s_u, 4 * n + b - 3)
                            scalar.activation(pexp_sb[b][:], ps_sc[b][:],
                                              AF.Exp,
                                              scale=0.125).then_inc(s_ex, 1)

                @block.vector
                def _(vector):
                    vector.wait_ge(s_ld, 16 * 36)
                    sv4v = sv4_sb[:].rearrange("p (c j) -> p c j", j=4)
                    vector.tensor_tensor(sv_sb[:], sv4v[:, :, 0],
                                         sv4v[:, :, 1], ALU.add).then_inc(s_v5, 1)
                    vector.wait_ge(s_v5, 1)
                    vector.tensor_tensor(sv_sb[:], sv_sb[:],
                                         sv4v[:, :, 2], ALU.add).then_inc(s_v5, 1)
                    vector.wait_ge(s_v5, 2)
                    vector.tensor_tensor(sv_sb[:], sv_sb[:],
                                         sv4v[:, :, 3], ALU.add).then_inc(s_v5, 1)
                    nv = 3
                    for b in range(4):
                        vector.memset(ps_sc[b][:], 0.0)
                    vector.memset(ps_z[:], 0.0)
                    vector.memset(ps_u[0][:], 0.0)
                    vector.memset(ps_u[1][:], 0.0)
                    vector.memset(ps_rz[:], 0.0).then_inc(s_v5, 1)
                    nv += 1
                    vector.wait_ge(s_ld, 16 * 36)
                    for n, (c8, its) in enumerate(ITERS):
                        for di, d in enumerate(DELTAS):
                            t = 14 * n + di
                            if t >= 2:
                                vector.wait_ge(s_scm, t - 1)
                            if (d, its) in BIGIDX:
                                vector.wait_ge(s_kb, 16 * (buse_ord[(n, di)] + 1))
                            vector.tensor_tensor(
                                p_t[t % 2][:],
                                q_sb[c8][:, 512 * its:512 * (its + 1)],
                                ksrc(c8, its, di), ALU.mult).then_inc(s_pp, 1)
                        for b in range(4):
                            vector.wait_ge(s_ex, 4 * n + b + 1)
                            vector.tensor_scalar(u_sb[b][:], pexp_sb[b][:], 1.0,
                                                 None,
                                                 ALU.subtract).then_inc(s_u, 1)
                        vector.wait_ge(s_z, n + 1)
                        vector.tensor_scalar(tz_f[:], ps_z[0:2, :], 8192.0,
                                             None, ALU.add).then_inc(s_v5, 1)
                        nv += 1
                        vector.wait_ge(s_v5, nv)
                        vector.reciprocal(rz_f[:], tz_f[:]).then_inc(s_v5, 1)
                        nv += 1
                        vector.wait_ge(s_v5, nv)
                        vector.tensor_copy(rz_bf[:], rz_f[:]).then_inc(s_rz, 1)
                        for di, d in enumerate(DELTAS):
                            t = 14 * n + di
                            vector.wait_ge(s_bc, t + 1)
                            if (d, its) in BIGIDX:
                                vector.wait_ge(s_vb, 16 * (buse_ord[(n, di)] + 1))
                            if di == 0:
                                vector.tensor_tensor(acc[0][:],
                                                     vsrc(c8, its, di),
                                                     ps_u[t % 2][:],
                                                     ALU.mult).then_inc(s_avv, 1)
                            else:
                                vector.tensor_tensor(t_av[t % 2][:],
                                                     vsrc(c8, its, di),
                                                     ps_u[t % 2][:],
                                                     ALU.mult).then_inc(s_avv, 1)
                                vector.wait_ge(s_avv, t + 1)
                                vector.tensor_tensor(acc[di % 2][:],
                                                     acc[(di - 1) % 2][:],
                                                     t_av[t % 2][:], ALU.add)
                        vector.wait_ge(s_bcz, n + 1)
                        vector.scalar_tensor_tensor(
                            q_sb[c8][:, 512 * its:512 * (its + 1)],
                            acc[(len(DELTAS) - 1) % 2][:],
                            sv_sb[:, c8:c8 + 1], ps_rz[:],
                            ALU.add, ALU.mult).then_inc(s_ep, 1)

        ctxB = ExitStack()
        ps_o = [ctxB.enter_context(nc.psum_tensor(f"ps_o{i}", [128, 512], F32))
                for i in range(2)]
        ln = LNEmit(nc, ctx, ctxB, "ln3")
        with ctxB:
            y_ap = lambda c, its: x2_sb[c][:, 512 * its:512 * (its + 1)]
            out_ap = lambda c, its: q_sb[c][:, 512 * its:512 * (its + 1)]

            def wait_y(eng, its, c):
                eng.wait_ge(s_y, 4 * NCH)

            with nc.Block() as block2:
                @block2.sync
                def _(sync):
                    for c in range(NCH):
                        if c > 0:
                            sync.wait_ge(s_cast, c)
                        sync.dma_start(out=wstage[:],
                                       in_=wo_in[128 * c:128 * (c + 1), :]).then_inc(s_ld, 16)
                    for c in range(NCH):
                        sync.wait_ge(ln.s_x1, 3 * NCH + c + 1)
                        sync.dma_start(out=x3_out[128 * c:128 * (c + 1), :],
                                       in_=q_sb[c][:]).then_inc(s_st, 16)
                    sync.wait_ge(s_st, 16 * NCH)

                @block2.vector
                def _(vector):
                    ln.vector_init(vector)
                    for c in range(NCH):
                        vector.wait_ge(s_ld, 16 * (36 + c + 1))
                        vector.tensor_copy(w_bf[c][:],
                                           wstage[:]).then_inc(s_cast, 1)
                    j = 0
                    for dc in range(NCH):
                        for its in range(4):
                            vector.wait_ge(s_o, j + 1)
                            vector.scalar_tensor_tensor(
                                x2_sb[dc][:, 512 * its:512 * (its + 1)],
                                ps_o[j % 2][:], bo_sb[:, dc:dc + 1],
                                x2_sb[dc][:, 512 * its:512 * (its + 1)],
                                ALU.add, ALU.add).then_inc(s_y, 1)
                            j += 1
                    ln.vector_ops(vector, y_ap, out_ap)

                @block2.scalar
                def _(scalar):
                    ln.scalar_ops(scalar)

                @block2.gpsimd
                def _(gpsimd):
                    ln.pool_ops(gpsimd, y_ap, wait_y)

                @block2.tensor
                def _(tensor):
                    tensor.wait_ge(s_ep, len(ITERS))
                    tensor.wait_ge(s_cast, NCH)
                    j = 0
                    for dc in range(NCH):
                        for its in range(4):
                            if j >= 2:
                                tensor.wait_ge(s_y, j - 1)
                            for hp in range(NCH):
                                ins = tensor.matmul(
                                    ps_o[j % 2][:],
                                    w_bf[hp][:, 128 * dc:128 * (dc + 1)],
                                    q_sb[hp][:, 512 * its:512 * (its + 1)],
                                    start=(hp == 0), stop=(hp == NCH - 1))
                            ins.then_inc(s_o, 1)
                            j += 1
                    tensor.wait_ge(ln.s_ini, 1)
                    ln.tensor_ops(tensor, y_ap, wait_y)
    return nc



def build_moe_gate():
    nc = bass.Bass("TRN2", target_bir_lowering=False, debug=False)
    x3_in = nc.dram_tensor("x3_fm", [DM, TPC], BF16, kind="ExternalInput").ap()
    wred_in = nc.dram_tensor("w_red", [DM, 16], F32, kind="ExternalInput").ap()
    wg_in = nc.dram_tensor("wg", [8, 16], F32, kind="ExternalInput").ap()
    iota8_in = nc.dram_tensor("iota8", [128, 8], F32, kind="ExternalInput").ap()
    eye_in = nc.dram_tensor("eye", [128, 128], F32, kind="ExternalInput").ap()
    eidx_out = nc.dram_tensor("eidx", [128, 16], F32, kind="ExternalOutput").ap()
    score_out = nc.dram_tensor("score", [128, 16], F32, kind="ExternalOutput").ap()
    x3tm_out = nc.dram_tensor("x3tm", [TPC, DM], BF16, kind="ExternalOutput").ap()

    ctx = ExitStack()
    sb, ps, sem = _mk(nc, ctx)
    with ctx:
        x3_sb = [sb(f"x3_{c}", [128, TPC], BF16) for c in range(NCH)]
        wred_sb = sb("wred_s", [128, 16 * NCH], F32)
        wred_bf = sb("wred_bf", [128, 16 * NCH], BF16)
        wg_sb = sb("wg_s", [8, 16], F32)
        iota8_sb = sb("iota8_s", [128, 8], F32)
        eye_sb = sb("eye_s", [128, 128], F32)
        eye_bf = sb("eye_bf", [128, 128], BF16)
        red_bf = sb("red_bf", [16, TPC], BF16)
        wg2 = sb("wg2", [8, 16], F32)
        n2 = sb("n2", [8, 1], F32)
        lnn = sb("lnn", [8, 1], F32)
        nrm = sb("nrm", [8, 1], F32)
        rn = sb("rn", [8, 1], F32)
        wgn_bf = sb("wgn_bf", [8, 16], BF16)
        wgnT = sb("wgnT", [16, 8], BF16)
        mx = sb("mx", [128, 1], F32)
        nmx = sb("nmx", [128, 1], F32)
        e_t = sb("e_t", [128, 8], BF16)
        ssum = sb("ssum", [128, 1], F32)
        onehot = sb("onehot", [128, 8], BF16)
        idxm = sb("idxm", [128, 8], F32)
        eidx_sb = sb("eidx_s", [128, 16], F32)
        score_sb = sb("score_s", [128, 16], F32)
        xtm_st = [sb(f"xtm{i}", [128, DM], BF16) for i in range(2)]

        s_ld = sem("s_ld"); s_v = sem("s_v"); s_mm = sem("s_mm")
        s_g = sem("s_g"); s_t = sem("s_t"); s_ts = sem("s_ts")
        s_st = sem("s_st"); s_sc = sem("s_sc"); s_rd = sem("s_rd")

        ctxA = ExitStack()
        ps_red = [ctxA.enter_context(nc.psum_tensor(f"ps_red{i}", [16, 512], F32))
                  for i in range(2)]
        ps_l = [ctxA.enter_context(nc.psum_tensor(f"ps_l{i}", [128, 8], F32))
                for i in range(2)]
        ps_wt = ctxA.enter_context(nc.psum_tensor("ps_wt", [16, 8], BF16))
        ps_tr = [ctxA.enter_context(nc.psum_tensor(f"ps_tr{i}", [128, 128], BF16))
                 for i in range(2)]
        with ctxA:
            with nc.Block() as block:
                @block.sync
                def _(sync):
                    for c in range(NCH):
                        sync.dma_start(out=x3_sb[c][:],
                                       in_=x3_in[128 * c:128 * (c + 1), :]).then_inc(s_ld, 16)
                    sync.dma_start(
                        out=wred_sb[:].rearrange("p (c g) -> p c g", g=16),
                        in_=wred_in.rearrange("(c p) g -> p c g", p=128)).then_inc(s_ld, 16)
                    sync.dma_start(out=wg_sb[:], in_=wg_in).then_inc(s_ld, 16)
                    sync.dma_start(out=iota8_sb[:], in_=iota8_in).then_inc(s_ld, 16)
                    sync.dma_start(out=eye_sb[:], in_=eye_in).then_inc(s_ld, 16)
                    sync.wait_ge(s_sc, 16)
                    sync.dma_start(out=eidx_out, in_=eidx_sb[:]).then_inc(s_st, 16)
                    sync.dma_start(out=score_out, in_=score_sb[:]).then_inc(s_st, 16)
                    sync.wait_ge(s_st, 32)
                    for jt in range(16):
                        sync.wait_ge(s_ts, 8 * (jt + 1))
                        sync.dma_start(out=x3tm_out[128 * jt:128 * (jt + 1), :],
                                       in_=xtm_st[jt % 2][:]).then_inc(s_st, 16)
                        sync.wait_ge(s_st, 16 * (jt + 3))

                @block.vector
                def _(vector):
                    nv = 0
                    vector.wait_ge(s_ld, 16 * 12)
                    vector.tensor_copy(wred_bf[:], wred_sb[:])
                    vector.tensor_copy(eye_bf[:], eye_sb[:]).then_inc(s_v, 1)
                    nv += 1
                    vector.wait_ge(s_g, 1)
                    vector.reduce_sum(n2[:], wg2[:], axis=AX.X).then_inc(s_v, 1)
                    nv += 1
                    vector.wait_ge(s_g, 2)
                    vector.tensor_scalar(rn[:], nrm[:], 1e-4, None,
                                         ALU.max).then_inc(s_v, 1)
                    nv += 1
                    vector.wait_ge(s_v, nv)
                    vector.reciprocal(rn[:], rn[:]).then_inc(s_v, 1)
                    nv += 1
                    vector.wait_ge(s_v, nv)
                    vector.tensor_scalar(wgn_bf[:], wg_sb[:], rn[:, 0:1], None,
                                         ALU.mult).then_inc(s_v, 1)
                    nv += 1
                    vector.wait_ge(s_mm, 1)
                    vector.tensor_copy(wgnT[:], ps_wt[:]).then_inc(s_v, 1)
                    nv += 1
                    for jt in range(16):
                        vector.wait_ge(s_mm, 1 + 4 + jt + 1)
                        vector.reduce_max(mx[:], ps_l[jt % 2][:], axis=AX.X)
                        vector.reduce_max(nmx[:], ps_l[jt % 2][:], axis=AX.X,
                                          negate=True).then_inc(s_v, 1)
                        nv += 1
                        vector.wait_ge(s_g, 2 + jt + 1)
                        vector.reciprocal(score_sb[:, jt:jt + 1],
                                          ssum[:]).then_inc(s_v, 1)
                        nv += 1
                        vector.wait_ge(s_v, nv)
                        vector.tensor_scalar(onehot[:], ps_l[jt % 2][:],
                                             mx[:, 0:1], None,
                                             ALU.is_ge).then_inc(s_v, 1)
                        nv += 1
                        vector.wait_ge(s_v, nv)
                        vector.tensor_tensor(idxm[:], onehot[:], iota8_sb[:],
                                             ALU.mult).then_inc(s_v, 1)
                        nv += 1
                        vector.wait_ge(s_v, nv)
                        vector.reduce_sum(eidx_sb[:, jt:jt + 1], idxm[:],
                                          axis=AX.X).then_inc(s_sc, 1)
                    for jt in range(16):
                        if jt >= 2:
                            vector.wait_ge(s_st, 16 * (2 + jt - 1))
                        for c in range(NCH):
                            vector.wait_ge(s_t, 8 * jt + c + 1)
                            vector.tensor_copy(
                                xtm_st[jt % 2][:, 128 * c:128 * (c + 1)],
                                ps_tr[(8 * jt + c) % 2][:]).then_inc(s_ts, 1)

                @block.scalar
                def _(scalar):
                    scalar.wait_ge(s_ld, 16 * 12)
                    scalar.activation(wg2[:], wg_sb[:], AF.Square).then_inc(s_g, 1)
                    scalar.wait_ge(s_v, 2)
                    scalar.activation(lnn[:], n2[:], AF.Ln)
                    scalar.activation(nrm[:], lnn[:], AF.Exp,
                                      scale=0.5).then_inc(s_g, 1)
                    for its in range(4):
                        scalar.wait_ge(s_mm, 1 + its + 1)
                        scalar.copy(red_bf[0:16, 512 * its:512 * (its + 1)],
                                    ps_red[its % 2][0:16, :]).then_inc(s_rd, 1)
                    for jt in range(16):
                        scalar.wait_ge(s_v, 6 + 4 * jt + 1)
                        scalar.activation(e_t[:], ps_l[jt % 2][:], AF.Exp,
                                          bias=nmx[:, 0:1],
                                          accum_out=ssum[:, 0:1]).then_inc(s_g, 1)

                @block.tensor
                def _(tensor):
                    tensor.wait_ge(s_v, 5)
                    tensor.transpose(ps_wt[:], wgn_bf[:],
                                     eye_bf[0:8, 0:8]).then_inc(s_mm, 1)
                    tensor.wait_ge(s_ld, 16 * 12)
                    tensor.wait_ge(s_v, 1)
                    for its in range(4):
                        if its >= 2:
                            tensor.wait_ge(s_rd, its - 1)
                        for c in range(NCH):
                            ins = tensor.matmul(
                                ps_red[its % 2][:],
                                wred_bf[:, 16 * c:16 * (c + 1)],
                                x3_sb[c][:, 512 * its:512 * (its + 1)],
                                start=(c == 0), stop=(c == NCH - 1))
                        ins.then_inc(s_mm, 1)
                    for jt in range(16):
                        its = jt // 4
                        tensor.wait_ge(s_rd, its + 1)
                        if jt >= 2:
                            tensor.wait_ge(s_sc, jt - 1)
                        tensor.matmul(ps_l[jt % 2][:],
                                      red_bf[0:16, 128 * jt:128 * (jt + 1)],
                                      wgnT[0:16, :], start=True,
                                      stop=True).then_inc(s_mm, 1)
                    for jt in range(16):
                        for c in range(NCH):
                            t = 8 * jt + c
                            if t >= 2:
                                tensor.wait_ge(s_ts, t - 1)
                            tensor.transpose(ps_tr[t % 2][:],
                                             x3_sb[c][:, 128 * jt:128 * (jt + 1)],
                                             eye_bf[:]).then_inc(s_t, 1)

                @block.gpsimd
                def _(gpsimd):
                    pass
    return nc



def build_moe_expert():
    nc = bass.Bass("TRN2", target_bir_lowering=False, debug=False)
    eidx_in = nc.dram_tensor("eidx_all", [128, 128], F32, kind="ExternalInput").ap()
    xtm_in = nc.dram_tensor("x3tm_all", [16384, DM], BF16, kind="ExternalInput").ap()
    w1_in = nc.dram_tensor("w1f", [8192, 4096], F32, kind="ExternalInput").ap()
    w2_in = nc.dram_tensor("w2f", [32768, 1024], F32, kind="ExternalInput").ap()
    b1_in = nc.dram_tensor("b1f", [8, 4096], F32, kind="ExternalInput").ap()
    b2_in = nc.dram_tensor("b2f", [8, 1024], F32, kind="ExternalInput").ap()
    rank_in = nc.dram_tensor("rankf", [1, 1], F32, kind="ExternalInput").ap()
    tri_in = nc.dram_tensor("tri128", [128, 128], F32, kind="ExternalInput").ap()
    thr_in = nc.dram_tensor("thr", [8, 32], F32, kind="ExternalInput").ap()
    eye_in = nc.dram_tensor("eye", [128, 128], F32, kind="ExternalInput").ap()
    slotb_in = nc.dram_tensor("slotbase", [1, 20], F32, kind="ExternalInput").ap()
    c8b_in = nc.dram_tensor("c8base128", [1, 8], F32, kind="ExternalInput").ap()
    j32_in = nc.dram_tensor("j32base128", [1, 32], F32, kind="ExternalInput").ap()
    iseg_in = nc.dram_tensor("iotaseg", [128, 3], F32, kind="ExternalInput").ap()
    z_out = nc.dram_tensor("z_slice", [2560, DM], BF16, kind="ExternalOutput").ap()
    post_out = nc.dram_tensor("posT", [128, 128], F32, kind="ExternalOutput").ap()

    d_sorted = nc.dram_tensor("d_sorted", [20480, DM], BF16, kind="Internal").ap()
    d_w1 = nc.dram_tensor("d_w1", [1024, 4096], F32, kind="Internal").ap()
    d_w2 = nc.dram_tensor("d_w2", [4096, 1024], F32, kind="Internal").ap()

    b1v = b1_in.rearrange("e (c p) -> (e c) p", p=128)
    b2v = b2_in.rearrange("e (c p) -> (e c) p", p=128)

    ctx = ExitStack()
    sb, ps, sem = _mk(nc, ctx)
    with ctx:
        em_sb = sb("em_s", [128, 128], F32)
        tri_sb = sb("tri_s", [128, 128], F32)
        tri_bf = sb("tri_bf", [128, 128], BF16)
        thr_sb = sb("thr_s", [8, 32], F32)
        eye_sb = sb("eye_s", [128, 128], F32)
        eye_bf = sb("eye_bf", [128, 128], BF16)
        rank_sb = sb("rank_s", [1, 1], F32)
        slotb_sb = sb("slotb_s", [1, 20], F32)
        c8b_sb = sb("c8b_s", [1, 8], F32)
        j32_sb = sb("j32_s", [1, 32], F32)
        iseg_sb = sb("iseg_s", [128, 3], F32)
        ones128 = sb("ones128", [128, 128], BF16)
        ones_col = sb("onesc", [128, 1], BF16)
        onescf = sb("onescf", [1, 128], F32)
        ones_row = sb("onesr", [1, 128], BF16)
        onesr8 = sb("onesr8", [1, 8], BF16)
        onesr20 = sb("onesr20", [1, 20], BF16)
        onesr32 = sb("onesr32", [1, 32], BF16)
        m_sb = [sb(f"m_{e}", [128, 128], BF16) for e in range(8)]
        scin = sb("scin", [128, 128], F32)
        wex = [sb(f"wex_{e}", [128, 128], F32) for e in range(8)]
        totb = sb("totb", [128, 8], F32)
        totb_bf = sb("totb_bf", [128, 8], BF16)
        tot_sb = sb("tot_s", [8, 1], F32)
        geb = sb("geb", [8, 32], BF16)
        blk8 = sb("blk8", [8, 1], F32)
        pad_f = sb("pad_f", [8, 1], F32)
        pad_bf = sb("pad_bf", [8, 1], BF16)
        go_f = sb("go_f", [8, 1], F32)
        go_bf = sb("go_bf", [8, 1], BF16)
        gorow_bf = sb("gorow_bf", [1, 8], BF16)
        gorow_f = sb("gorow_f", [1, 8], F32)
        bo_sb = sb("bo_s", [128, 8], F32)
        off_sb = sb("off_s", [128, 8], F32)
        ptmp = [sb(f"ptmp{i}", [128, 128], F32) for i in range(2)]
        pacc = [sb(f"pacc{i}", [128, 128], F32) for i in range(2)]
        posT_f = sb("posT_f", [128, 128], F32)
        posT_i = sb("posT_i", [128, 128], I32)
        zsrc = sb("zsrc", [128, DM], BF16)
        v1 = sb("v1", [1, 1], F32)
        flg = sb("flg", [1, 8], BF16)
        e_f = sb("e_f", [1, 5], F32)
        et_f = sb("et_f", [1, 5], F32)
        allidx = sb("allidx", [1, 230], F32)
        idx_f = sb("idx_f", [128, 230], F32)
        idx_i = sb("idx_i", [128, 230], I32)
        xsc = [sb(f"xsc{i}", [128, DM], BF16) for i in range(2)]
        xtm_blk = [sb(f"xtb{i}", [128, DM], BF16) for i in range(2)]
        xfm_blk = [sb(f"xfm{c}", [128, 512], BF16) for c in range(NCH)]
        wtmp = sb("wtmp", [128, 4096], F32)
        w1stage = [sb(f"w1s{i}", [128, 1024], F32) for i in range(2)]
        w1_bf = [sb(f"w1b{i}", [128, 1024], BF16) for i in range(2)]
        w2stage = sb("w2s", [128, 4096], F32)
        w2_bf = sb("w2b", [128, 4096], BF16)
        b1g = sb("b1g", [128, 128], F32)
        b2g = sb("b2g", [128, 128], F32)
        b1_pc = sb("b1pc", [128, 32], F32)
        b2_pc = sb("b2pc", [128, 8], F32)
        h_sb = sb("h_s", [128, 32 * 512], BF16)
        z_bf = [sb(f"zb{d}", [128, 512], BF16) for d in range(NCH)]
        ztm_st = [sb(f"ztm{i}", [128, DM], BF16) for i in range(2)]

        s_ld = sem("s_ld"); s_va = sem("s_va"); s_ta = sem("s_ta")
        s_ze = sem("s_ze"); s_sca = sem("s_sca"); s_xl2 = sem("s_xl2")
        s_tok = sem("s_tok"); s_tr = sem("s_tr"); s_xe = sem("s_xe")
        s_wf = sem("s_wf"); s_wf2 = sem("s_wf2"); s_ws = sem("s_ws")
        s_wsp = [sem("s_ws0"), sem("s_ws1")]; s_ws2 = sem("s_ws2")
        s_wc = sem("s_wc"); s_hm = sem("s_hm"); s_ge = sem("s_ge")
        s_z2 = sem("s_z2"); s_ev = sem("s_ev"); s_ztr = sem("s_ztr")
        s_zev = sem("s_zev"); s_st = sem("s_st"); s_bt = sem("s_bt")
        s_bte = sem("s_bte"); s_stp = sem("s_stp")

        marks = {}

        ctxA = ExitStack()
        ps_bo = ctxA.enter_context(nc.psum_tensor("ps_bo", [128, 8], F32))
        ps_te = ctxA.enter_context(nc.psum_tensor("ps_te", [8, 1], F32))
        ps_go = ctxA.enter_context(nc.psum_tensor("ps_go", [8, 1], F32))
        ps_got = ctxA.enter_context(nc.psum_tensor("ps_got", [1, 8], BF16))
        ps_gof = ctxA.enter_context(nc.psum_tensor("ps_gof", [1, 8], F32))
        ps_gob = ctxA.enter_context(nc.psum_tensor("ps_gob", [128, 8], F32))
        ps_pt = ctxA.enter_context(nc.psum_tensor("ps_pt", [128, 128], F32))
        ps_ib = ctxA.enter_context(nc.psum_tensor("ps_ib", [128, 230], F32))
        with ctxA:
            with nc.Block() as block:
                @block.sync
                def _(sync):
                    for t, src_ in ((em_sb, eidx_in), (tri_sb, tri_in),
                                    (thr_sb, thr_in), (eye_sb, eye_in),
                                    (rank_sb, rank_in), (slotb_sb, slotb_in),
                                    (c8b_sb, c8b_in), (j32_sb, j32_in),
                                    (iseg_sb, iseg_in)):
                        sync.dma_start(out=t[:], in_=src_).then_inc(s_ld, 16)
                    sync.wait_ge(s_va, 1)
                    for i in range(20480 // 128):
                        sync.dma_start(out=d_sorted[128 * i:128 * (i + 1), :],
                                       in_=zsrc[:]).then_inc(s_ze, 16)
                    sync.wait_ge(s_va, 200)
                    sync.dma_start(out=post_out, in_=posT_f[:]).then_inc(s_stp, 16)
                    sync.wait_ge(s_stp, 16)

                @block.vector
                def _(vector):
                    nv = 0
                    def step(ins, name=None):
                        nonlocal nv
                        ins.then_inc(s_va, 1)
                        nv += 1
                        if name:
                            marks[name] = nv
                        vector.wait_ge(s_va, nv)
                    vector.memset(zsrc[:], 0.0).then_inc(s_va, 1)
                    nv += 1
                    vector.memset(ones128[:], 1.0)
                    vector.memset(ones_col[:], 1.0)
                    vector.memset(onescf[:], 1.0)
                    vector.memset(ones_row[:], 1.0)
                    vector.memset(onesr8[:], 1.0)
                    vector.memset(onesr20[:], 1.0)
                    vector.memset(onesr32[:], 1.0)
                    vector.memset(b1g[:], 0.0)
                    vector.memset(b2g[:], 0.0)
                    vector.wait_ge(s_ld, 16 * 9)
                    vector.tensor_copy(tri_bf[:], tri_sb[:])
                    step(vector.tensor_copy(eye_bf[:], eye_sb[:]), "ini")
                    for e in range(8):
                        step(vector.tensor_scalar(m_sb[e][:], em_sb[:],
                                                  float(e), None, ALU.is_equal))
                        step(vector.tensor_tensor_scan(scin[:], ones128[:],
                                                       m_sb[e][:], 0.0,
                                                       ALU.mult, ALU.add))
                        step(vector.tensor_tensor(wex[e][:], scin[:], m_sb[e][:],
                                                  ALU.subtract))
                        step(vector.reduce_sum(totb[:, e:e + 1], m_sb[e][:],
                                               axis=AX.X))
                    step(vector.tensor_copy(totb_bf[:], totb[:]), "totb")
                    vector.wait_ge(s_ta, 2)
                    step(vector.tensor_copy(tot_sb[:], ps_te[:]))
                    step(vector.tensor_scalar(geb[:], thr_sb[:], tot_sb[:, 0:1],
                                              None, ALU.is_lt))
                    step(vector.reduce_sum(blk8[:], geb[:], axis=AX.X))
                    step(vector.tensor_scalar(pad_f[:], blk8[:], 512.0, None,
                                              ALU.mult))
                    step(vector.tensor_copy(pad_bf[:], pad_f[:]), "pad")
                    vector.wait_ge(s_ta, 3)
                    step(vector.tensor_copy(go_f[:], ps_go[:]))
                    step(vector.tensor_copy(go_bf[:], go_f[:]), "gobf")
                    vector.wait_ge(s_ta, 5)
                    step(vector.tensor_copy(gorow_bf[:], ps_got[:]))
                    step(vector.tensor_copy(gorow_f[:], ps_gof[:]), "gorow")
                    vector.wait_ge(s_ta, 6)
                    step(vector.tensor_copy(bo_sb[:], ps_bo[:]))
                    step(vector.tensor_tensor(off_sb[:], bo_sb[:], ps_gob[:],
                                              ALU.add))
                    for e in range(8):
                        step(vector.scalar_tensor_tensor(
                            ptmp[e % 2][:], wex[e][:], off_sb[:, e:e + 1],
                            m_sb[e][:], ALU.add, ALU.mult))
                        if e == 0:
                            step(vector.tensor_copy(pacc[0][:], ptmp[0][:]))
                        else:
                            step(vector.tensor_tensor(pacc[e % 2][:],
                                                      pacc[(e - 1) % 2][:],
                                                      ptmp[e % 2][:], ALU.add))
                    marks["pacc"] = nv
                    step(vector.tensor_scalar(v1[:], rank_sb[:], 2560.0, None,
                                              ALU.mult))
                    step(vector.scalar_tensor_tensor(
                        allidx[0:1, 200:220], onesr20[:], v1[0:1, 0:1],
                        slotb_sb[:], ALU.mult, ALU.add))
                    for k in range(5):
                        step(vector.tensor_scalar(flg[:], gorow_f[:],
                                                  allidx[0:1, 200 + 4 * k:
                                                         200 + 4 * k + 1],
                                                  None, ALU.is_le))
                        step(vector.reduce_sum(e_f[:, k:k + 1], flg[:],
                                               axis=AX.X))
                    step(vector.tensor_scalar(e_f[:], e_f[:], 1.0, None,
                                              ALU.subtract))
                    step(vector.tensor_scalar(et_f[:], e_f[:], 1024.0, None,
                                              ALU.mult))
                    for k in range(5):
                        step(vector.scalar_tensor_tensor(
                            allidx[0:1, 8 * k:8 * (k + 1)], onesr8[:],
                            et_f[0:1, k:k + 1], c8b_sb[:], ALU.mult, ALU.add))
                    step(vector.tensor_scalar(et_f[:], e_f[:], 4096.0, None,
                                              ALU.mult))
                    for k in range(5):
                        step(vector.scalar_tensor_tensor(
                            allidx[0:1, 40 + 32 * k:40 + 32 * (k + 1)],
                            onesr32[:], et_f[0:1, k:k + 1], j32_sb[:],
                            ALU.mult, ALU.add))
                    step(vector.tensor_scalar(allidx[0:1, 220:225], e_f[:],
                                              32.0, None, ALU.mult))
                    step(vector.tensor_scalar(allidx[0:1, 225:230], e_f[:],
                                              8.0, None, ALU.mult), "allidx")
                    vector.wait_ge(s_ta, 8)
                    step(vector.tensor_scalar(idx_f[:, 0:220], ps_ib[:, 0:220],
                                              iseg_sb[:, 0:1], None, ALU.add))
                    step(vector.tensor_scalar(idx_f[:, 220:225],
                                              ps_ib[:, 220:225],
                                              iseg_sb[:, 1:2], None, ALU.add))
                    step(vector.tensor_scalar(idx_f[:, 225:230],
                                              ps_ib[:, 225:230],
                                              iseg_sb[:, 2:3], None, ALU.add))
                    step(vector.tensor_copy(idx_i[:], idx_f[:]), "idx")
                    vector.wait_ge(s_ta, 7)
                    step(vector.tensor_copy(posT_f[:], ps_pt[:]))
                    step(vector.tensor_copy(posT_i[:], posT_f[:]), "post")
                    while nv < 200:
                        nv += 1
                        vector.sem_inc(s_va, 1)

                @block.tensor
                def _(tensor):
                    tensor.wait_ge(s_va, marks["totb"])
                    tensor.matmul(ps_bo[:], tri_bf[:], totb_bf[:],
                                  start=True, stop=True).then_inc(s_ta, 1)
                    tensor.matmul(ps_te[:], totb_bf[:], ones_col[:, 0:1],
                                  start=True, stop=True).then_inc(s_ta, 1)
                    tensor.wait_ge(s_va, marks["pad"])
                    tensor.matmul(ps_go[:], tri_bf[0:8, 0:8], pad_bf[:, 0:1],
                                  start=True, stop=True).then_inc(s_ta, 1)
                    tensor.wait_ge(s_va, marks["gobf"])
                    tensor.transpose(ps_got[:], go_bf[:, 0:1],
                                     eye_bf[0:8, 0:8]).then_inc(s_ta, 1)
                    tensor.transpose(ps_gof[:], go_f[:, 0:1],
                                     eye_sb[0:8, 0:8]).then_inc(s_ta, 1)
                    tensor.wait_ge(s_va, marks["gorow"])
                    tensor.matmul(ps_gob[:], ones_row[:], gorow_bf[:],
                                  start=True, stop=True).then_inc(s_ta, 1)
                    tensor.wait_ge(s_va, marks["pacc"])
                    tensor.transpose(ps_pt[:], pacc[7 % 2][:],
                                     eye_sb[:]).then_inc(s_ta, 1)
                    tensor.wait_ge(s_va, marks["allidx"])
                    tensor.matmul(ps_ib[:], onescf[:], allidx[:],
                                  start=True, stop=True).then_inc(s_ta, 1)

                @block.gpsimd
                def _(gpsimd):
                    gpsimd.wait_ge(s_ze, 16 * 160)
                    gpsimd.wait_ge(s_va, 200)
                    for j in range(128):
                        gpsimd.dma_start(out=xsc[j % 2][:],
                                         in_=xtm_in[128 * j:128 * (j + 1),
                                                    :]).then_inc(s_xl2, 16)
                        gpsimd.wait_ge(s_xl2, 16 * (j + 1))
                        gpsimd.indirect_dma_start(
                            out=d_sorted[:, :],
                            out_offset=bass.IndirectOffsetOnAxis(
                                ap=posT_i[:, j:j + 1], axis=0),
                            in_=xsc[j % 2][:],
                            in_offset=None).then_inc(s_sca, 16)
                    gpsimd.wait_ge(s_sca, 16 * 128)

        ctxB = ExitStack()
        ps_tr = [ctxB.enter_context(nc.psum_tensor(f"ps_tr{i}", [128, 128], BF16))
                 for i in range(2)]
        ps_h = [ctxB.enter_context(nc.psum_tensor(f"ps_h{i}", [128, 512], F32))
                for i in range(2)]
        ps_zz = [ctxB.enter_context(nc.psum_tensor(f"ps_zz{i}", [128, 512], F32))
                 for i in range(2)]
        ps_bt = ctxB.enter_context(nc.psum_tensor("ps_bt", [128, 128], F32))
        with ctxB:
            with nc.Block() as block2:
                @block2.gpsimd
                def _(gpsimd):
                    nf = 0
                    nf2 = 0
                    for k in range(5):
                        for m in range(4):
                            g = 4 * k + m
                            if g >= 2:
                                gpsimd.wait_ge(s_tr, 8 * (g - 1))
                            gpsimd.indirect_dma_start(
                                out=xtm_blk[m % 2][:], out_offset=None,
                                in_=d_sorted[:, :],
                                in_offset=bass.IndirectOffsetOnAxis(
                                    ap=idx_i[:, 200 + g:200 + g + 1],
                                    axis=0)).then_inc(s_tok, 16)
                            gpsimd.wait_ge(s_tok, 16 * (6 * k + m + 1))
                        if k >= 1:
                            gpsimd.wait_ge(s_bt, 2 * k)
                        gpsimd.indirect_dma_start(
                            out=b1g[:], out_offset=None, in_=b1v[:, :],
                            in_offset=bass.IndirectOffsetOnAxis(
                                ap=idx_i[:, 220 + k:220 + k + 1], axis=0),
                            bounds_check=256, oob_is_err=False).then_inc(s_tok, 16)
                        gpsimd.wait_ge(s_tok, 16 * (6 * k + 5))
                        gpsimd.indirect_dma_start(
                            out=b2g[:], out_offset=None, in_=b2v[:, :],
                            in_offset=bass.IndirectOffsetOnAxis(
                                ap=idx_i[:, 225 + k:225 + k + 1], axis=0),
                            bounds_check=64, oob_is_err=False).then_inc(s_tok, 16)
                        gpsimd.wait_ge(s_tok, 16 * (6 * k + 6))
                        if k >= 1:
                            gpsimd.wait_ge(s_wc, 40 * k)
                        for c8 in range(NCH):
                            gpsimd.indirect_dma_start(
                                out=wtmp[:, 0:4096], out_offset=None,
                                in_=w1_in[:, :],
                                in_offset=bass.IndirectOffsetOnAxis(
                                    ap=idx_i[:, 8 * k + c8:8 * k + c8 + 1],
                                    axis=0)).then_inc(s_wf, 16)
                            nf += 16
                            gpsimd.wait_ge(s_wf, nf)
                            gpsimd.dma_start(
                                out=d_w1[128 * c8:128 * (c8 + 1), :],
                                in_=wtmp[:, 0:4096]).then_inc(s_wf2, 16)
                            nf2 += 16
                            gpsimd.wait_ge(s_wf2, nf2)
                        for j in range(32):
                            gpsimd.indirect_dma_start(
                                out=wtmp[:, 0:1024], out_offset=None,
                                in_=w2_in[:, :],
                                in_offset=bass.IndirectOffsetOnAxis(
                                    ap=idx_i[:, 40 + 32 * k + j:
                                             40 + 32 * k + j + 1],
                                    axis=0)).then_inc(s_wf, 16)
                            nf += 16
                            gpsimd.wait_ge(s_wf, nf)
                            gpsimd.dma_start(
                                out=d_w2[128 * j:128 * (j + 1), :],
                                in_=wtmp[:, 0:1024]).then_inc(s_wf2, 16)
                            nf2 += 16
                            gpsimd.wait_ge(s_wf2, nf2)

                @block2.sync
                def _(sync):
                    for k in range(5):
                        for fc in range(32):
                            u = 32 * k + fc
                            if u >= 2:
                                k2, f2 = divmod(u - 2, 32)
                                sync.wait_ge(s_wc, 40 * k2 + f2 + 1)
                            sync.wait_ge(s_wf2, 16 * (40 * k + 8))
                            sync.dma_start(
                                out=w1stage[fc % 2][:].rearrange(
                                    "p (c f) -> p c f", f=128),
                                in_=d_w1.rearrange("(c p) f -> p c f", p=128)
                                [:, :, 128 * fc:128 * (fc + 1)]).then_inc(
                                    s_wsp[fc % 2], 16)
                        for dc in range(NCH):
                            if k > 0 or dc > 0:
                                sync.wait_ge(s_wc, 40 * k + 32 + dc)
                            sync.wait_ge(s_wf2, 16 * 40 * (k + 1))
                            sync.dma_start(
                                out=w2stage[:].rearrange("p (f w) -> p f w",
                                                         w=128),
                                in_=d_w2.rearrange("(f p) w -> p f w", p=128)
                                [:, :, 128 * dc:128 * (dc + 1)]).then_inc(s_ws2, 16)
                        for m in range(4):
                            sync.wait_ge(s_zev, 8 * (4 * k + m + 1))
                            sync.dma_start(
                                out=z_out[512 * k + 128 * m:
                                          512 * k + 128 * (m + 1), :],
                                in_=ztm_st[m % 2][:]).then_inc(s_st, 16)
                            sync.wait_ge(s_st, 16 * (4 * k + m + 1))
                    sync.wait_ge(s_st, 16 * 20)

                @block2.vector
                def _(vector):
                    for k in range(5):
                        for m in range(4):
                            for c8 in range(NCH):
                                t = 32 * k + 8 * m + c8
                                vector.wait_ge(s_tr, t + 1)
                                if k >= 1:
                                    vector.wait_ge(s_hm, 32 * k)
                                vector.tensor_copy(
                                    xfm_blk[c8][:, 128 * m:128 * (m + 1)],
                                    ps_tr[t % 2][:]).then_inc(s_xe, 1)
                        vector.wait_ge(s_bt, 2 * k + 1)
                        if k >= 1:
                            vector.wait_ge(s_ge, 32 * k)
                        vector.tensor_copy(b1_pc[:],
                                           ps_bt[:, 0:32]).then_inc(s_bte, 1)
                        vector.wait_ge(s_bt, 2 * k + 2)
                        if k >= 1:
                            vector.wait_ge(s_ev, 8 * k)
                        vector.tensor_copy(b2_pc[:],
                                           ps_bt[:, 0:8]).then_inc(s_bte, 1)
                        for fc in range(32):
                            u = 32 * k + fc
                            vector.wait_ge(s_wsp[fc % 2], 16 * (u // 2 + 1))
                            if u >= 2:
                                vector.wait_ge(s_hm, u - 1)
                            vector.tensor_copy(w1_bf[fc % 2][:],
                                               w1stage[fc % 2][:]).then_inc(s_wc, 1)
                        for dc in range(NCH):
                            vector.wait_ge(s_ws2, 16 * (8 * k + dc + 1))
                            if 8 * k + dc >= 1:
                                vector.wait_ge(s_z2, 8 * k + dc)
                            vector.tensor_copy(w2_bf[:],
                                               w2stage[:]).then_inc(s_wc, 1)
                        for m in range(4):
                            if 4 * k + m >= 2:
                                vector.wait_ge(s_st, 16 * (4 * k + m - 1))
                            for dc in range(NCH):
                                t = 32 * k + 8 * m + dc
                                vector.wait_ge(s_ztr, t + 1)
                                vector.tensor_copy(
                                    ztm_st[m % 2][:, 128 * dc:128 * (dc + 1)],
                                    ps_tr[t % 2][:]).then_inc(s_zev, 1)

                @block2.scalar
                def _(scalar):
                    for k in range(5):
                        for fc in range(32):
                            u = 32 * k + fc
                            scalar.wait_ge(s_hm, u + 1)
                            scalar.wait_ge(s_bte, 2 * k + 1)
                            if k >= 1:
                                scalar.wait_ge(s_z2, 8 * k)
                            scalar.activation(
                                h_sb[:, 512 * fc:512 * (fc + 1)],
                                ps_h[fc % 2][:], AF.Gelu,
                                bias=b1_pc[:, fc:fc + 1]).then_inc(s_ge, 1)
                        for dc in range(NCH):
                            scalar.wait_ge(s_z2, 8 * k + dc + 1)
                            scalar.wait_ge(s_bte, 2 * k + 2)
                            if k >= 1:
                                scalar.wait_ge(s_ztr, 32 * k)
                            scalar.activation(
                                z_bf[dc][:], ps_zz[dc % 2][:], AF.Identity,
                                bias=b2_pc[:, dc:dc + 1]).then_inc(s_ev, 1)

                @block2.tensor
                def _(tensor):
                    for k in range(5):
                        for m in range(4):
                            tensor.wait_ge(s_tok, 16 * (6 * k + m + 1))
                            if k >= 1:
                                tensor.wait_ge(s_zev, 32 * k)
                            for c8 in range(NCH):
                                t = 32 * k + 8 * m + c8
                                if t >= 2:
                                    tensor.wait_ge(s_xe, t - 1)
                                tensor.transpose(
                                    ps_tr[t % 2][:],
                                    xtm_blk[m % 2][:, 128 * c8:128 * (c8 + 1)],
                                    eye_bf[:]).then_inc(s_tr, 1)
                        tensor.wait_ge(s_tok, 16 * (6 * k + 6))
                        if k >= 1:
                            tensor.wait_ge(s_bte, 2 * k)
                        tensor.transpose(ps_bt[:], b1g[:],
                                         eye_sb[:]).then_inc(s_bt, 1)
                        tensor.wait_ge(s_bte, 2 * k + 1)
                        tensor.transpose(ps_bt[:], b2g[:],
                                         eye_sb[:]).then_inc(s_bt, 1)
                        for fc in range(32):
                            u = 32 * k + fc
                            tensor.wait_ge(s_wc, 40 * k + fc + 1)
                            tensor.wait_ge(s_xe, 32 * (k + 1))
                            if u >= 2:
                                tensor.wait_ge(s_ge, u - 1)
                            if k >= 1:
                                tensor.wait_ge(s_z2, 8 * k)
                            for c8 in range(NCH):
                                ins = tensor.matmul(
                                    ps_h[fc % 2][:],
                                    w1_bf[fc % 2][:, 128 * c8:128 * (c8 + 1)],
                                    xfm_blk[c8][:], start=(c8 == 0),
                                    stop=(c8 == NCH - 1))
                            ins.then_inc(s_hm, 1)
                        for dc in range(NCH):
                            zu = 8 * k + dc
                            tensor.wait_ge(s_wc, 40 * k + 32 + dc + 1)
                            tensor.wait_ge(s_ge, 32 * (k + 1))
                            if zu >= 2:
                                tensor.wait_ge(s_ev, zu - 1)
                            for fc in range(32):
                                ins = tensor.matmul(
                                    ps_zz[dc % 2][:],
                                    w2_bf[:, 128 * fc:128 * (fc + 1)],
                                    h_sb[:, 512 * fc:512 * (fc + 1)],
                                    start=(fc == 0), stop=(fc == 31))
                            ins.then_inc(s_z2, 1)
                        for m in range(4):
                            for dc in range(NCH):
                                t = 32 * k + 8 * m + dc
                                tensor.wait_ge(s_ev, 8 * k + dc + 1)
                                if t >= 2:
                                    tensor.wait_ge(s_zev, t - 1)
                                tensor.transpose(
                                    ps_tr[t % 2][:],
                                    z_bf[dc][:, 128 * m:128 * (m + 1)],
                                    eye_bf[:]).then_inc(s_ztr, 1)
    return nc



def build_moe_final():
    nc = bass.Bass("TRN2", target_bir_lowering=False, debug=False)
    z_in = nc.dram_tensor("z_all", [20480, DM], BF16, kind="ExternalInput").ap()
    pos_in = nc.dram_tensor("pos_own", [128, 16], F32, kind="ExternalInput").ap()
    sc_in = nc.dram_tensor("score", [128, 16], F32, kind="ExternalInput").ap()
    xtm_in = nc.dram_tensor("x3tm", [TPC, DM], BF16, kind="ExternalInput").ap()
    out_tm = nc.dram_tensor("out_tm", [TPC, DM], F32, kind="ExternalOutput").ap()

    ctx = ExitStack()
    sb, ps, sem = _mk(nc, ctx)
    with ctx:
        pos_sb = sb("pos_s", [128, 16], F32)
        pos_i = sb("pos_i", [128, 16], I32)
        sc_sb = sb("sc_s", [128, 16], F32)
        x3t = [sb(f"x3t{i}", [128, DM], BF16) for i in range(2)]
        zg = [sb(f"zg{i}", [128, DM], BF16) for i in range(2)]
        y_f = [sb(f"y_f{i}", [128, DM], F32) for i in range(2)]
        ysq = sb("ysq", [128, DM], F32)
        s1 = sb("s1", [128, 1], F32)
        s2 = sb("s2", [128, 1], F32)
        mu = sb("mu", [128, 1], F32)
        m2 = sb("m2", [128, 1], F32)
        mu2 = sb("mu2", [128, 1], F32)
        var = sb("var", [128, 1], F32)
        lnv = sb("lnv", [128, 1], F32)
        rstd = sb("rstd", [128, 1], F32)
        o_f = [sb(f"o_f{i}", [128, DM], F32) for i in range(2)]
        eps4 = sb("eps4", [128, 1], F32)

        s_ld = sem("s_ld"); s_gi = sem("s_gi"); s_zg = sem("s_zg")
        s_v = sem("s_v"); s_sc = sem("s_sc"); s_o = sem("s_o")
        s_st = sem("s_st"); s_xl = sem("s_xl")

        with nc.Block() as block:
            @block.sync
            def _(sync):
                for t, src_ in ((pos_sb, pos_in), (sc_sb, sc_in)):
                    sync.dma_start(out=t[:], in_=src_).then_inc(s_ld, 16)
                for j in range(18):
                    if j < 16:
                        if j >= 2:
                            sync.wait_ge(s_v, 8 * (j - 1))
                        sync.dma_start(out=x3t[j % 2][:],
                                       in_=xtm_in[128 * j:128 * (j + 1),
                                                  :]).then_inc(s_xl, 16)
                        sync.wait_ge(s_xl, 16 * (j + 1))
                    if j >= 2:
                        sync.wait_ge(s_v, 8 * (j - 2) + 7)
                        sync.dma_start(out=out_tm[128 * (j - 2):128 * (j - 1), :],
                                       in_=o_f[(j - 2) % 2][:]).then_inc(s_st, 16)
                        sync.wait_ge(s_st, 16 * (j - 1))
                sync.wait_ge(s_st, 16 * 16)

            @block.gpsimd
            def _(gpsimd):
                gpsimd.wait_ge(s_gi, 1)
                for j in range(16):
                    if j >= 2:
                        gpsimd.wait_ge(s_v, 8 * (j - 1) + 1)
                    gpsimd.indirect_dma_start(
                        out=zg[j % 2][:], out_offset=None,
                        in_=z_in[:, :],
                        in_offset=bass.IndirectOffsetOnAxis(
                            ap=pos_i[:, j:j + 1], axis=0)).then_inc(s_zg, 16)
                    gpsimd.wait_ge(s_zg, 16 * (j + 1))

            @block.vector
            def _(vector):
                vector.memset(eps4[:], 1e-5)
                vector.wait_ge(s_ld, 32)
                vector.tensor_copy(pos_i[:], pos_sb[:]).then_inc(s_gi, 1)
                nv = 0
                for j in range(16):
                    vector.wait_ge(s_zg, 16 * (j + 1))
                    vector.wait_ge(s_xl, 16 * (j + 1))
                    vector.scalar_tensor_tensor(
                        y_f[j % 2][:], zg[j % 2][:], sc_sb[:, j:j + 1],
                        x3t[j % 2][:], ALU.mult, ALU.add).then_inc(s_v, 1)
                    nv += 1
                    vector.wait_ge(s_v, nv)
                    vector.tensor_tensor(ysq[:], y_f[j % 2][:], y_f[j % 2][:],
                                         ALU.mult)
                    vector.reduce_sum(s1[:], y_f[j % 2][:],
                                      axis=AX.X).then_inc(s_v, 1)
                    nv += 1
                    vector.wait_ge(s_v, nv)
                    vector.reduce_sum(s2[:], ysq[:], axis=AX.X).then_inc(s_v, 1)
                    nv += 1
                    vector.wait_ge(s_v, nv)
                    vector.tensor_scalar(mu[:], s1[:], 1.0 / DM, None, ALU.mult)
                    vector.tensor_scalar(m2[:], s2[:], 1.0 / DM, None,
                                         ALU.mult).then_inc(s_v, 1)
                    nv += 1
                    vector.wait_ge(s_v, nv)
                    vector.tensor_tensor(mu2[:], mu[:], mu[:], ALU.mult).then_inc(s_v, 1)
                    nv += 1
                    vector.wait_ge(s_v, nv)
                    vector.tensor_tensor(var[:], m2[:], mu2[:],
                                         ALU.subtract).then_inc(s_v, 1)
                    nv += 1
                    if j >= 2:
                        vector.wait_ge(s_st, 16 * (j - 1))
                    vector.wait_ge(s_sc, j + 1)
                    vector.tensor_scalar(o_f[j % 2][:], y_f[j % 2][:],
                                         mu[:, 0:1], rstd[:, 0:1],
                                         ALU.subtract, ALU.mult).then_inc(s_v, 1)
                    nv += 1
                    vector.sem_inc(s_v, 1)
                    nv += 1

            @block.scalar
            def _(scalar):
                for j in range(16):
                    scalar.wait_ge(s_v, 8 * j + 6)
                    scalar.activation(lnv[:], var[:], AF.Ln,
                                      bias=eps4[:, 0:1])
                    scalar.activation(rstd[:], lnv[:], AF.Exp,
                                      scale=-0.5).then_inc(s_sc, 1)
    return nc



def _core_view(full):
    out = []
    for c in range(NCORE):
        b, cq = divmod(c, 4)
        out.append(np.ascontiguousarray(full[b, 2048 * cq:2048 * (cq + 1)]))
    return out


def _pc(v):
    return np.ascontiguousarray(np.asarray(v).reshape(8, 128).T)


def _asT(a):
    return np.ascontiguousarray(np.asarray(a).T)


def _l0attn_prep(inz):
    jj = np.arange(128)
    biasv = np.zeros((4, 128, NL0T), np.float32)
    maskv = np.zeros((4, 128, NL0M), np.float32)
    for t, (hp, h2, h, it, s) in enumerate(L0TILES):
        jpos = 128 * s + jj
        for cq in range(4):
            i0g = 2048 * cq + ITW * it
            if t in L0MASK:
                biasv[cq, :, t] = -SLOPE[h] * jpos
                maskv[cq, :, L0MASK[t]] = jpos - i0g
            else:
                if 128 * s > i0g + ITW - 1:
                    biasv[cq, :, t] = -1e30
                else:
                    biasv[cq, :, t] = -SLOPE[h] * jpos
    iota = np.broadcast_to(np.arange(ITW, dtype=np.float32)[None, :],
                           (128, ITW)).copy()
    sel = np.zeros((16, 8 * 128), np.float32)
    for hp in range(8):
        for m in range(128):
            sel[2 * hp + m // 64, 128 * hp + m] = 1.0
    return biasv, maskv, iota, sel.astype(BF)


def _l0_windows(kfm_g, vtm_g):
    kw = np.zeros((128, KW_TOT), BF)
    for hp in range(8):
        kw[:, int(KOFF[hp]):int(KOFF[hp]) + JPAIR[hp]] = \
            kfm_g[128 * hp:128 * (hp + 1), 0:JPAIR[hp]]
    vw = np.zeros((128, VW_TOT), BF)
    for h in range(16):
        for s in range(SH[h]):
            slot = int(VOFF[h]) + s
            vw[:, 65 * slot:65 * slot + 64] = \
                vtm_g[128 * s:128 * (s + 1), 64 * h:64 * (h + 1)]
            vw[:, 65 * slot + 64] = 1.0
    return kw, vw


def _l1_windows(kf, vf, cq):
    base = 2048 * cq
    kwin = np.zeros((1024, 2560), BF)
    vwin = np.zeros((1024, 2560), BF)
    lo = base - 512
    s = max(0, lo)
    kwin[:, s - lo:] = kf[:, s:base + 2048]
    vwin[:, s - lo:] = vf[:, s:base + 2048]
    kbig = np.zeros((1024, 512 * NBIG), BF)
    vbig = np.zeros((1024, 512 * NBIG), BF)
    for slot, (d, i) in enumerate(BIGSLOTS):
        g0 = base + 512 * i - d
        gs = max(0, g0)
        if g0 + 512 > 0:
            kbig[:, 512 * slot + (gs - g0):512 * (slot + 1)] = kf[:, gs:g0 + 512]
            vbig[:, 512 * slot + (gs - g0):512 * (slot + 1)] = vf[:, gs:g0 + 512]
    return kwin, vwin, kbig, vbig


def _l1_selectors():
    selp = np.zeros((128, 522), np.float32)
    selp[0:64, 0] = 1.0
    selp[64:128, 1] = 1.0
    for bnk in range(4):
        nslots = 4 if bnk < 3 else 2
        for kk in range(nslots):
            selp[32 * kk + 0, 2 + 2 * bnk + 0] = 1.0
            selp[32 * kk + 1, 2 + 2 * bnk + 1] = 1.0
    for kk in range(4):
        for m in range(128):
            selp[32 * kk + (m // 64), 10 + 128 * kk + m] = 1.0
    e2t = np.zeros((2, 128), np.float32)
    for m in range(128):
        e2t[m // 64, m] = 1.0
    return selp.astype(BF), e2t.astype(BF)


def kernel(**inputs):
    inz = {k: np.asarray(v) for k, v in inputs.items()}
    R = list(range(NCORE))
    x = inz["L0_attn_x"]
    xc = _core_view(x)

    nc = build_l0qkv()
    im = [{"x_fm": _asT(xc[c]), "Wq": inz["L0_attn_Wq"],
           "Wk": inz["L0_attn_Wk"], "Wv": inz["L0_attn_Wv"]} for c in R]
    r1 = run_bass_kernel_spmd(nc, im, core_ids=R)

    biasv, maskv, iota, selhp = _l0attn_prep(inz)
    bo_pc = _pc(inz["L0_attn_bo"])
    kv_b = []
    for b in range(2):
        kfm_g = np.concatenate([r1.results[4 * b + q]["k_fm"] for q in range(4)],
                               axis=1)
        vtm_g = np.concatenate([r1.results[4 * b + q]["v_tm"] for q in range(4)],
                               axis=0)
        kv_b.append(_l0_windows(kfm_g, vtm_g))
    nc = build_l0attn()
    im = []
    for c in R:
        b, cq = divmod(c, 4)
        im.append({"q_fm": r1.results[c]["q_fm"], "kwin": kv_b[b][0],
                   "vwin": kv_b[b][1],
                   "biasv": np.ascontiguousarray(biasv[cq]),
                   "maskv": np.ascontiguousarray(maskv[cq]),
                   "iota": iota, "selhp": selhp, "Wo": inz["L0_attn_Wo"],
                   "bo_pc": bo_pc, "x_fm": _asT(xc[c])})
    r2 = run_bass_kernel_spmd(nc, im, core_ids=R)

    nc = build_l0ffn()
    im = []
    for c in R:
        b, cq = divmod(c, 4)
        if cq == 0:
            halo = np.zeros((128, 8), np.float32)
        else:
            halo = _pc(r2.results[c - 1]["x1_fm"][:, -1].astype(np.float32))
        im.append({"x1_fm": r2.results[c]["x1_fm"], "halo_pc": halo,
                   "mur_pc": _pc(inz["L0_ffn_mu_r"]),
                   "muk_pc": _pc(inz["L0_ffn_mu_k"]),
                   "WrT": _asT(inz["L0_ffn_W_r"]),
                   "WkT": _asT(inz["L0_ffn_W_k"]),
                   "WvT": _asT(inz["L0_ffn_W_v"])})
    r3 = run_bass_kernel_spmd(nc, im, core_ids=R)

    nc = build_l1qkv()
    im = [{"x2_fm": r3.results[c]["x2_fm"],
           "WqT": _asT(inz["L1_attn_Wq"]), "WkT": _asT(inz["L1_attn_Wk"]),
           "WvT": _asT(inz["L1_attn_Wv"]), "bq_pc": _pc(inz["L1_attn_bq"]),
           "bk_pc": _pc(inz["L1_attn_bk"]), "bv_pc": _pc(inz["L1_attn_bv"])}
          for c in R]
    r4 = run_bass_kernel_spmd(nc, im, core_ids=R)

    selp, e2t = _l1_selectors()
    wins = {}
    sv4_b = {}
    for b in range(2):
        kf = np.concatenate([r4.results[4 * b + q]["k_fm"] for q in range(4)],
                            axis=1)
        vf = np.concatenate([r4.results[4 * b + q]["v_fm"] for q in range(4)],
                            axis=1)
        for cq in range(4):
            wins[(b, cq)] = _l1_windows(kf, vf, cq)
        sv4 = np.zeros((128, 32), np.float32)
        for q in range(4):
            sv4[:, np.arange(8) * 4 + q] = r4.results[4 * b + q]["sv_p"]
        sv4_b[b] = sv4
    nc = build_l1attn()
    im = []
    for c in R:
        b, cq = divmod(c, 4)
        kwin, vwin, kbig, vbig = wins[(b, cq)]
        im.append({"x2_fm": r3.results[c]["x2_fm"],
                   "q_fm": r4.results[c]["q_fm"], "kwin": kwin, "vwin": vwin,
                   "kbig": kbig, "vbig": vbig, "sv4": sv4_b[b], "selp": selp,
                   "e2t": e2t, "WoT": _asT(inz["L1_attn_Wo"]),
                   "bo_pc": _pc(inz["L1_attn_bo"])})
    r5 = run_bass_kernel_spmd(nc, im, core_ids=R)

    nc = build_moe_gate()
    iota8 = np.broadcast_to(np.arange(8, dtype=np.float32)[None, :],
                            (128, 8)).copy()
    eye = np.eye(128, dtype=np.float32)
    im = [{"x3_fm": r5.results[c]["x3_fm"], "w_red": inz["L1_ffn_w_red"],
           "wg": inz["L1_ffn_wg"], "iota8": iota8, "eye": eye} for c in R]
    r6a = run_bass_kernel_spmd(nc, im, core_ids=R)

    eidx_all = np.concatenate(
        [np.ascontiguousarray(r6a.results[c]["eidx"].T) for c in R],
        axis=0)
    x3tm_all = np.concatenate([r6a.results[c]["x3tm"] for c in R], axis=0)
    tri = np.triu(np.ones((128, 128), np.float32), 1)
    thr = np.broadcast_to((512.0 * np.arange(32, dtype=np.float32))[None, :],
                          (8, 32)).copy()
    slotbase = (512 * (np.arange(20) // 4) +
                128 * (np.arange(20) % 4)).astype(np.float32)[None, :]
    c8b = (128.0 * np.arange(8, dtype=np.float32))[None, :]
    j32b = (128.0 * np.arange(32, dtype=np.float32))[None, :]
    iseg = np.zeros((128, 3), np.float32)
    iseg[:, 0] = np.arange(128)
    iseg[:, 1] = np.where(np.arange(128) < 32, np.arange(128), 1e6)
    iseg[:, 2] = np.where(np.arange(128) < 8, np.arange(128), 1e6)
    w1f = np.ascontiguousarray(inz["L1_ffn_w1"].reshape(8192, 4096))
    w2f = np.ascontiguousarray(inz["L1_ffn_w2"].reshape(32768, 1024))
    nc = build_moe_expert()
    im = []
    for c in R:
        im.append({"eidx_all": eidx_all, "x3tm_all": x3tm_all, "w1f": w1f,
                   "w2f": w2f, "b1f": inz["L1_ffn_b1"], "b2f": inz["L1_ffn_b2"],
                   "rankf": np.array([[float(c)]], np.float32), "tri128": tri,
                   "thr": thr, "eye": eye, "slotbase": slotbase,
                   "c8base128": c8b, "j32base128": j32b, "iotaseg": iseg})
    r6b = run_bass_kernel_spmd(nc, im, core_ids=R)

    z_all = np.concatenate([r6b.results[c]["z_slice"] for c in R], axis=0)
    nc = build_moe_final()
    im = []
    for c in R:
        posT = r6b.results[c]["posT"]
        im.append({"z_all": z_all,
                   "pos_own": np.ascontiguousarray(posT[:, 16 * c:16 * (c + 1)]),
                   "score": r6a.results[c]["score"],
                   "x3tm": r6a.results[c]["x3tm"]})
    r6c = run_bass_kernel_spmd(nc, im, core_ids=R)

    out = np.empty((2, NSEQ, DM), np.float32)
    for c in R:
        b, cq = divmod(c, 4)
        out[b, 2048 * cq:2048 * (cq + 1)] = r6c.results[c]["out_tm"]
    return out


# version 04
# speedup vs baseline: 1.5715x; 1.5715x over previous
import math
from contextlib import ExitStack

import numpy as np
import ml_dtypes

import concourse.bass as bass
import concourse.mybir as mybir
from concourse.bass_utils import run_bass_kernel_spmd

F32 = mybir.dt.float32
BF16 = mybir.dt.bfloat16
I32 = mybir.dt.int32
AF = mybir.ActivationFunctionType
ALU = mybir.AluOpType
AX = mybir.AxisListType
BF = ml_dtypes.bfloat16

B, NSEQ, DM, NH, DH = 2, 8192, 1024, 16, 64
TPC = 2048
NCH = DM // 128
NCORE = 8
ITW = 1024
NIT = TPC // ITW

SLOPE = [2.0 ** (-0.5 * (h + 1)) for h in range(NH)]
THRESH = 30.0
JH = [min(NSEQ, int(math.ceil(THRESH / SLOPE[h] / 128)) * 128) for h in range(NH)]
SH = [JH[h] // 128 for h in range(NH)]
JPAIR = [max(JH[2 * p], JH[2 * p + 1]) for p in range(8)]
KOFF = np.concatenate([[0], np.cumsum(JPAIR)]).astype(int)
KW_TOT = int(KOFF[-1])
VOFF = np.concatenate([[0], np.cumsum(SH)]).astype(int)
NSLOT = int(VOFF[-1])
VW_TOT = 65 * NSLOT

L0TILES = []
for _hp in range(8):
    for _h2 in range(2):
        _h = 2 * _hp + _h2
        for _it in range(NIT):
            for _s in range(SH[_h]):
                L0TILES.append((_hp, _h2, _h, _it, _s))
NL0T = len(L0TILES)


def _is_partial(it, s):
    for cq in range(4):
        i0g = 2048 * cq + ITW * it
        if i0g - 127 <= 128 * s <= i0g + ITW - 1:
            return True
    return False

L0MASK = {}
for _t, (_hp, _h2, _h, _it, _s) in enumerate(L0TILES):
    if _is_partial(_it, _s):
        L0MASK[_t] = len(L0MASK)
NL0M = len(L0MASK)



def _mk(nc, ctx):
    sb = lambda name, shape, dt=F32: ctx.enter_context(nc.sbuf_tensor(name, shape, dt))
    ps = lambda name, shape, dt=F32: ctx.enter_context(nc.psum_tensor(name, shape, dt))
    sem = lambda name: ctx.enter_context(nc.semaphore(name))
    return sb, ps, sem



def build_l0qkv():
    nc = bass.Bass("TRN2", target_bir_lowering=False, debug=False)
    x_in = nc.dram_tensor("x_fm", [DM, TPC], F32, kind="ExternalInput").ap()
    w_in = {n: nc.dram_tensor(n, [DM, DM], F32, kind="ExternalInput").ap()
            for n in ("Wq", "Wk", "Wv")}
    q_out = nc.dram_tensor("q_fm", [DM, TPC], BF16, kind="ExternalOutput").ap()
    k_out = nc.dram_tensor("k_fm", [DM, TPC], BF16, kind="ExternalOutput").ap()
    v_out = nc.dram_tensor("v_tm", [TPC, DM], BF16, kind="ExternalOutput").ap()

    ctx = ExitStack()
    sb, ps, sem = _mk(nc, ctx)
    with ctx:
        xs = sb("xs", [128, TPC], F32)
        x_bf = [sb(f"x_bf{c}", [128, TPC], BF16) for c in range(NCH)]
        wstage = sb("wstage", [128, DM], F32)
        w_bf = {n: [sb(f"{n}_bf{c}", [128, DM], BF16) for c in range(NCH)]
                for n in ("Wq", "Wk", "Wv")}
        evac = [sb(f"evac{i}", [128, TPC], BF16) for i in range(2)]
        pst = [ps(f"pst{i}", [128, 512]) for i in range(4)]

        s_ld = sem("s_ld")
        s_ldw = sem("s_ldw")
        s_cast = sem("s_cast")
        s_mm = sem("s_mm")
        s_ev = sem("s_ev")
        s_st = sem("s_st")

        with nc.Block() as block:
            @block.sync
            def _(sync):
                nld = 0
                for c in range(NCH):
                    if c > 0:
                        sync.wait_ge(s_cast, c)
                    sync.dma_start(out=xs[:],
                                   in_=x_in[128 * c:128 * (c + 1), :]).then_inc(s_ld, 16)
                    nld += 16
                i = 0
                for n in ("Wq", "Wk", "Wv"):
                    for c in range(NCH):
                        sync.wait_ge(s_cast, NCH + i)
                        sync.dma_start(
                            out=wstage[:],
                            in_=w_in[n][128 * c:128 * (c + 1), :]).then_inc(s_ldw, 16)
                        i += 1
                g = 0
                for dst in (q_out, k_out):
                    for hp in range(NCH):
                        g += 1
                        sync.wait_ge(s_ev, g)
                        sync.dma_start(out=dst[128 * hp:128 * (hp + 1), :],
                                       in_=evac[(g - 1) % 2][:]).then_inc(s_st, 16)
                        sync.wait_ge(s_st, 16 * g)
                for jt in range(TPC // 128):
                    g += 1
                    sync.wait_ge(s_ev, g)
                    sync.dma_start(out=v_out[128 * jt:128 * (jt + 1), :],
                                   in_=evac[(g - 1) % 2][:, 0:DM]).then_inc(s_st, 16)
                    sync.wait_ge(s_st, 16 * g)

            @block.vector
            def _(vector):
                for c in range(NCH):
                    vector.wait_ge(s_ld, 16 * (c + 1))
                    vector.tensor_copy(x_bf[c][:], xs[:]).then_inc(s_cast, 1)
                i = 0
                for n in ("Wq", "Wk", "Wv"):
                    for c in range(NCH):
                        vector.wait_ge(s_ldw, 16 * (i + 1))
                        vector.tensor_copy(w_bf[n][c][:],
                                           wstage[:]).then_inc(s_cast, 1)
                        i += 1
                g = 0
                for pi in range(2):
                    for hp in range(NCH):
                        g += 1
                        vector.wait_ge(s_mm, g)
                        if g > 2:
                            vector.wait_ge(s_st, 16 * (g - 2))
                        for its in range(4):
                            ins = vector.tensor_copy(
                                evac[(g - 1) % 2][:, 512 * its:512 * (its + 1)],
                                pst[its][:])
                        ins.then_inc(s_ev, 1)
                for jt in range(TPC // 128):
                    g += 1
                    vector.wait_ge(s_mm, g)
                    if g > 2:
                        vector.wait_ge(s_st, 16 * (g - 2))
                    for dc in range(2):
                        ins = vector.tensor_copy(
                            evac[(g - 1) % 2][:, 512 * dc:512 * (dc + 1)],
                            pst[dc][:])
                    ins.then_inc(s_ev, 1)

            @block.tensor
            def _(tensor):
                tensor.wait_ge(s_cast, NCH + 24)
                g = 0
                for wn in ("Wq", "Wk"):
                    for hp in range(NCH):
                        g += 1
                        if g > 1:
                            tensor.wait_ge(s_ev, g - 1)
                        for its in range(4):
                            for c in range(NCH):
                                ins = tensor.matmul(
                                    pst[its][:],
                                    w_bf[wn][c][:, 128 * hp:128 * (hp + 1)],
                                    x_bf[c][:, 512 * its:512 * (its + 1)],
                                    start=(c == 0), stop=(c == NCH - 1))
                        ins.then_inc(s_mm, 1)
                for jt in range(TPC // 128):
                    g += 1
                    tensor.wait_ge(s_ev, g - 1)
                    for dc in range(2):
                        for c in range(NCH):
                            ins = tensor.matmul(
                                pst[dc][:],
                                x_bf[c][:, 128 * jt:128 * (jt + 1)],
                                w_bf["Wv"][c][:, 512 * dc:512 * (dc + 1)],
                                start=(c == 0), stop=(c == NCH - 1))
                    ins.then_inc(s_mm, 1)
    return nc



L0GROUPS = []
_cum = 0
for _hp in range(8):
    for _h2 in range(2):
        _h = 2 * _hp + _h2
        for _it in range(NIT):
            _cum += SH[_h]
            L0GROUPS.append((_hp, _h2, _h, _it, _cum - 1))
NL0G = len(L0GROUPS)
_L0MASK_ORD = {t: i for i, t in enumerate(sorted(L0MASK))}


def build_l0attn():
    nc = bass.Bass("TRN2", target_bir_lowering=False, debug=False)
    q_in = nc.dram_tensor("q_fm", [DM, TPC], BF16, kind="ExternalInput").ap()
    kwin_in = nc.dram_tensor("kwin", [128, KW_TOT], BF16, kind="ExternalInput").ap()
    vwin_in = nc.dram_tensor("vwin", [128, VW_TOT], BF16, kind="ExternalInput").ap()
    biasv_in = nc.dram_tensor("biasv", [128, NL0T], F32, kind="ExternalInput").ap()
    maskt_in = nc.dram_tensor("maskt", [128, NL0M * ITW], BF16,
                              kind="ExternalInput").ap()
    selhp_in = nc.dram_tensor("selhp", [16, 1024], BF16, kind="ExternalInput").ap()
    wo_in = nc.dram_tensor("Wo", [DM, DM], F32, kind="ExternalInput").ap()
    bo_in = nc.dram_tensor("bo_pc", [128, 8], F32, kind="ExternalInput").ap()
    x_in = nc.dram_tensor("x_fm", [DM, TPC], F32, kind="ExternalInput").ap()
    x1_out = nc.dram_tensor("x1_fm", [DM, TPC], BF16, kind="ExternalOutput").ap()
    d_l = nc.dram_tensor("d_l", [NL0G, ITW], F32, kind="Internal").ap()

    NPB = 2
    ctx = ExitStack()
    sb, ps, sem = _mk(nc, ctx)
    with ctx:
        q_sb = [sb(f"q{c}", [128, TPC], BF16) for c in range(NCH)]
        attn_sb = [sb(f"attn{c}", [128, TPC], BF16) for c in range(NCH)]
        kwin_sb = sb("kwin_s", [128, KW_TOT], BF16)
        vwin_sb = sb("vwin_s", [128, VW_TOT], BF16)
        biasv_sb = sb("biasv_s", [128, NL0T], F32)
        selhp_sb = sb("selhp_s", [16, 1024], BF16)
        bo_sb = sb("bo", [128, 8], F32)
        p_sb = [sb(f"p{i}", [128, ITW], BF16) for i in range(NPB)]
        pm_sb = [sb(f"pm{i}", [128, ITW], BF16) for i in range(2)]
        msk_sb = [sb(f"msk{i}", [128, ITW], BF16) for i in range(2)]
        l_stage = [sb(f"lst{i}", [1, ITW], F32) for i in range(2)]
        l_pm = sb("l_pm", [16, TPC], F32)
        r_bf = sb("r_bf", [16, TPC], BF16)
        r_f32 = sb("r_f32", [16, TPC], F32)
        wo_stage = sb("wo_stage", [128, DM], F32)
        wo_bf = [sb(f"wo_bf{i}", [128, DM], BF16) for i in range(2)]
        x_stage = [sb("xs0", [128, TPC], F32)]
        ones_col = sb("ones_col", [128, 1], BF16)
        ones_row = sb("ones_row", [1, 128], BF16)
        ysq_sb = [sb(f"ysq{i}", [128, 512], BF16) for i in range(2)]
        mu_f = sb("mu_f", [1, 512], F32)
        m2_f = sb("m2_f", [1, 512], F32)
        t_f = sb("t_f", [1, 512], F32)
        var_f = sb("var_f", [1, 512], F32)
        lnv_f = sb("lnv_f", [1, 512], F32)
        rstd_f = sb("rstd_f", [1, 512], F32)
        mu_bf = sb("mu_bf", [1, 512], BF16)
        rstd_bf = sb("rstd_bf", [1, 512], BF16)
        tnorm = sb("tnorm", [128, 512], BF16)
        eps_sb = sb("eps_sb", [1, 1], F32)

        s_ld = sem("s_ld"); s_mm = sem("s_mm"); s_exp = sem("s_exp")
        s_msk = sem("s_msk"); s_av = sem("s_av"); s_ev = sem("s_ev")
        s_ldma = sem("s_ldma"); s_r = sem("s_r"); s_bc = sem("s_bc")
        s_n = sem("s_n"); s_wo = sem("s_wo"); s_xl = sem("s_xl")
        s_o = sem("s_o"); s_y = sem("s_y"); s_sq = sem("s_sq")
        s_s = sem("s_s"); s_stat = sem("s_stat"); s_bcln = sem("s_bcln")
        s_x1 = sem("s_x1"); s_st = sem("s_st")
        s_v = sem("s_v"); s_mld = sem("s_mld"); s_scs = sem("s_scs")
        s_s2 = sem("s_s2")

        NLD0 = 8 + 5

        mask_list = sorted(L0MASK)

        ctxA = ExitStack()
        ps_sc = [ctxA.enter_context(nc.psum_tensor(f"ps_sc{i}", [128, ITW], F32))
                 for i in range(2)]
        ps_av = [ctxA.enter_context(nc.psum_tensor(f"ps_av{i}", [65, ITW], F32))
                 for i in range(2)]
        with ctxA:
            with nc.Block() as block:
                @block.sync
                def _(sync):
                    for c in range(NCH):
                        sync.dma_start(out=q_sb[c][:],
                                       in_=q_in[128 * c:128 * (c + 1), :]).then_inc(s_ld, 16)
                    for dst, src in ((kwin_sb, kwin_in), (vwin_sb, vwin_in),
                                     (biasv_sb, biasv_in), (selhp_sb, selhp_in),
                                     (bo_sb, bo_in)):
                        sync.dma_start(out=dst[:], in_=src).then_inc(s_ld, 16)
                    for g, (hp, h2, h, it, endt) in enumerate(L0GROUPS):
                        sync.wait_ge(s_ev, g + 1)
                        sync.dma_start(out=d_l[g:g + 1, :],
                                       in_=l_stage[g % 2][:]).then_inc(s_ldma, 16)
                        sync.wait_ge(s_ldma, 16 * (g + 1))
                    sync.wait_ge(s_ldma, 16 * NL0G)
                    sync.dma_start(
                        out=l_pm[:],
                        in_=d_l.rearrange("(h it) w -> h it w", it=NIT)).then_inc(
                            s_ldma, 16)

                @block.tensor
                def _(tensor):
                    tensor.wait_ge(s_ld, 16 * NLD0)
                    gidx = -1
                    for t, (hp, h2, h, it, s) in enumerate(L0TILES):
                        first = (s == 0); last = (s == SH[h] - 1)
                        if first:
                            gidx += 1
                        scb = t % 2
                        if t >= 2:
                            tensor.wait_ge(s_exp, t - 1)
                        lk = kwin_sb[64 * h2:64 * h2 + 64,
                                     int(KOFF[hp]) + 128 * s:int(KOFF[hp]) + 128 * (s + 1)]
                        rq = q_sb[hp]
                        tensor.matmul(ps_sc[scb][:, 0:512], lk,
                                      rq[64 * h2:64 * h2 + 64, ITW * it:ITW * it + 512])
                        tensor.matmul(ps_sc[scb][:, 512:1024], lk,
                                      rq[64 * h2:64 * h2 + 64,
                                         ITW * it + 512:ITW * it + 1024]).then_inc(s_mm, 1)
                        if first and gidx >= 2:
                            tensor.wait_ge(s_ev, gidx - 1)
                        if t in L0MASK:
                            tensor.wait_ge(s_msk, _L0MASK_ORD[t] + 1)
                            rhs_p = pm_sb[_L0MASK_ORD[t] % 2]
                        else:
                            tensor.wait_ge(s_exp, t + 1)
                            rhs_p = p_sb[t % NPB]
                        slot = int(VOFF[h]) + s
                        lv = vwin_sb[:, 65 * slot:65 * slot + 65]
                        tensor.matmul(ps_av[gidx % 2][0:65, 0:512], lv,
                                      rhs_p[:, 0:512], start=first, stop=last)
                        tensor.matmul(ps_av[gidx % 2][0:65, 512:1024], lv,
                                      rhs_p[:, 512:1024], start=first,
                                      stop=last).then_inc(s_av, 1)

                @block.scalar
                def _(scalar):
                    for t, (hp, h2, h, it, s) in enumerate(L0TILES):
                        scalar.wait_ge(s_mm, t + 1)
                        if t >= NPB:
                            scalar.wait_ge(s_av, t - NPB + 1)
                        scalar.activation(p_sb[t % NPB][:], ps_sc[t % 2][:],
                                          AF.Exp, bias=biasv_sb[:, t:t + 1],
                                          scale=0.125).then_inc(s_exp, 1)

                @block.gpsimd
                def _(gpsimd):
                    for mi, t in enumerate(mask_list):
                        if mi >= 2:
                            gpsimd.wait_ge(s_msk, mi - 1)
                        gpsimd.dma_start(
                            out=msk_sb[mi % 2][:],
                            in_=maskt_in[:, ITW * mi:ITW * (mi + 1)]).then_inc(
                                s_mld, 16)

                @block.vector
                def _(vector):
                    end2g = {endt: (g, hp, h2, h, it)
                             for g, (hp, h2, h, it, endt) in enumerate(L0GROUPS)}
                    for t in range(NL0T):
                        if t in L0MASK:
                            mi = _L0MASK_ORD[t]
                            vector.wait_ge(s_exp, t + 1)
                            vector.wait_ge(s_mld, 16 * (mi + 1))
                            if mi >= 2:
                                vector.wait_ge(s_av, mask_list[mi - 2] + 1)
                            vector.tensor_tensor(pm_sb[mi % 2][:],
                                                 p_sb[t % NPB][:],
                                                 msk_sb[mi % 2][:],
                                                 ALU.mult).then_inc(s_msk, 1)
                        if t in end2g:
                            g, hp, h2, h, it = end2g[t]
                            vector.wait_ge(s_av, t + 1)
                            if g >= 2:
                                vector.wait_ge(s_ldma, 16 * g)
                            vector.tensor_copy(l_stage[g % 2][:],
                                               ps_av[g % 2][64:65, :])
                            vector.tensor_copy(
                                attn_sb[hp][64 * h2:64 * h2 + 64,
                                            ITW * it:ITW * (it + 1)],
                                ps_av[g % 2][0:64, :]).then_inc(s_ev, 1)
                    vector.wait_ge(s_ldma, 16 * (NL0G + 1))
                    vector.reciprocal(r_f32[:], l_pm[:]).then_inc(s_v, 1)
                    vector.wait_ge(s_v, 1)
                    vector.tensor_copy(r_bf[:], r_f32[:]).then_inc(s_r, 1)

        ctxB = ExitStack()
        ps_br = [ctxB.enter_context(nc.psum_tensor(f"ps_br{i}", [128, 512], F32))
                 for i in range(2)]
        ps_o = [ctxB.enter_context(nc.psum_tensor(f"ps_o{i}", [128, 512], F32))
                for i in range(2)]
        ps_s1 = ctxB.enter_context(nc.psum_tensor("ps_s1", [1, 512], F32))
        ps_s2 = ctxB.enter_context(nc.psum_tensor("ps_s2", [1, 512], F32))
        ps_bm = ctxB.enter_context(nc.psum_tensor("ps_bm", [128, 512], F32))
        ps_brd = ctxB.enter_context(nc.psum_tensor("ps_brd", [128, 512], F32))
        with ctxB:
            with nc.Block() as block2:
                @block2.sync
                def _(sync):
                    for dc in range(NCH):
                        if dc >= 1:
                            sync.wait_ge(s_wo, dc)
                        sync.dma_start(
                            out=wo_stage[:].rearrange("p (c w) -> p c w", w=128),
                            in_=wo_in.rearrange("(c p) w -> p c w", p=128)
                            [:, :, 128 * dc:128 * (dc + 1)]).then_inc(s_ld, 16)
                        if dc >= 1:
                            sync.wait_ge(s_y, 4 * dc)
                        sync.dma_start(out=x_stage[0][:],
                                       in_=x_in[128 * dc:128 * (dc + 1), :]).then_inc(s_xl, 16)
                    for c in range(NCH):
                        sync.wait_ge(s_x1, 3 * NCH + c + 1)
                        sync.dma_start(out=x1_out[128 * c:128 * (c + 1), :],
                                       in_=q_sb[c][:]).then_inc(s_st, 16)
                    sync.wait_ge(s_st, 16 * NCH)

                @block2.vector
                def _(vector):
                    vector.memset(ones_col[:], 1.0)
                    vector.memset(ones_row[:], 1.0)
                    vector.memset(eps_sb[:], 1e-5)
                    k = 0
                    for hp in range(NCH):
                        for its in range(4):
                            vector.wait_ge(s_bc, k + 1)
                            vector.tensor_tensor(
                                q_sb[hp][:, 512 * its:512 * (its + 1)],
                                attn_sb[hp][:, 512 * its:512 * (its + 1)],
                                ps_br[k % 2][:], ALU.mult).then_inc(s_n, 1)
                            k += 1
                    j = 0
                    for dc in range(NCH):
                        vector.wait_ge(s_ld, 16 * (NLD0 + dc + 1))
                        vector.tensor_copy(wo_bf[dc % 2][:],
                                           wo_stage[:]).then_inc(s_wo, 1)
                        vector.wait_ge(s_xl, 16 * (dc + 1))
                        for its in range(4):
                            vector.wait_ge(s_o, j + 1)
                            vector.scalar_tensor_tensor(
                                attn_sb[dc][:, 512 * its:512 * (its + 1)],
                                ps_o[j % 2][:], bo_sb[:, dc:dc + 1],
                                x_stage[0][:, 512 * its:512 * (its + 1)],
                                ALU.add, ALU.add).then_inc(s_y, 1)
                            j += 1
                    vector.wait_ge(s_y, 4 * NCH)
                    nv = 0
                    for its in range(4):
                        vector.wait_ge(s_s, its + 1)
                        vector.wait_ge(s_s2, NCH * (its + 1))
                        vector.tensor_scalar(mu_f[:], ps_s1[:], 1.0 / DM, None,
                                             ALU.mult)
                        vector.tensor_scalar(m2_f[:], ps_s2[:], 1.0 / DM, None,
                                             ALU.mult).then_inc(s_v, 1)
                        nv += 1
                        vector.wait_ge(s_v, nv)
                        vector.tensor_tensor(t_f[:], mu_f[:], mu_f[:],
                                             ALU.mult).then_inc(s_v, 1)
                        nv += 1
                        vector.wait_ge(s_v, nv)
                        vector.tensor_tensor(var_f[:], m2_f[:], t_f[:],
                                             ALU.subtract).then_inc(s_v, 1)
                        nv += 1
                        vector.tensor_copy(mu_bf[:], mu_f[:]).then_inc(s_stat, 1)
                        vector.wait_ge(s_stat, 3 * its + 2)
                        vector.tensor_copy(rstd_bf[:], rstd_f[:]).then_inc(s_stat, 1)
                        for c in range(NCH):
                            vector.wait_ge(s_bcln, 2 * (its + 1))
                            vector.tensor_tensor(
                                tnorm[:], attn_sb[c][:, 512 * its:512 * (its + 1)],
                                ps_bm[:], ALU.subtract).then_inc(s_v, 1)
                            nv += 1
                            vector.wait_ge(s_v, nv)
                            vector.tensor_tensor(
                                q_sb[c][:, 512 * its:512 * (its + 1)],
                                tnorm[:], ps_brd[:], ALU.mult).then_inc(s_x1, 1)

                @block2.scalar
                def _(scalar):
                    for its in range(4):
                        scalar.wait_ge(s_stat, 3 * its + 1)
                        scalar.activation(lnv_f[:], var_f[:], AF.Ln,
                                          bias=eps_sb[0:1, 0:1]).then_inc(s_scs, 1)
                        scalar.wait_ge(s_scs, its + 1)
                        scalar.activation(rstd_f[:], lnv_f[:], AF.Exp,
                                          scale=-0.5).then_inc(s_stat, 1)

                @block2.gpsimd
                def _(gpsimd):
                    gq = 0
                    for its in range(4):
                        for c in range(NCH):
                            gpsimd.wait_ge(s_y, 4 * c + its + 1)
                            if gq >= 2:
                                gpsimd.wait_ge(s_s2, gq - 1)
                            gpsimd.tensor_tensor(
                                ysq_sb[gq % 2][:],
                                attn_sb[c][:, 512 * its:512 * (its + 1)],
                                attn_sb[c][:, 512 * its:512 * (its + 1)],
                                ALU.mult).then_inc(s_sq, 1)
                            gq += 1

                @block2.tensor
                def _(tensor):
                    tensor.wait_ge(s_r, 1)
                    k = 0
                    for hp in range(NCH):
                        for its in range(4):
                            if k >= 2:
                                tensor.wait_ge(s_n, k - 1)
                            tensor.matmul(ps_br[k % 2][:],
                                          selhp_sb[0:16, 128 * hp:128 * (hp + 1)],
                                          r_bf[0:16, 512 * its:512 * (its + 1)],
                                          start=True, stop=True).then_inc(s_bc, 1)
                            k += 1
                    tensor.wait_ge(s_n, 4 * NCH)
                    j = 0
                    for dc in range(NCH):
                        tensor.wait_ge(s_wo, dc + 1)
                        for its in range(4):
                            if j >= 2:
                                tensor.wait_ge(s_y, j - 1)
                            for hp in range(NCH):
                                ins = tensor.matmul(
                                    ps_o[j % 2][:],
                                    wo_bf[dc % 2][:, 128 * hp:128 * (hp + 1)],
                                    q_sb[hp][:, 512 * its:512 * (its + 1)],
                                    start=(hp == 0), stop=(hp == NCH - 1))
                            ins.then_inc(s_o, 1)
                            j += 1
                    gq = 0
                    for its in range(4):
                        if its >= 1:
                            tensor.wait_ge(s_stat, 3 * its)
                        for c in range(NCH):
                            ins = tensor.matmul(ps_s1[:], ones_col[:],
                                                attn_sb[c][:, 512 * its:512 * (its + 1)],
                                                start=(c == 0), stop=(c == NCH - 1))
                        ins.then_inc(s_s, 1)
                        for c in range(NCH):
                            tensor.wait_ge(s_sq, gq + 1)
                            ins = tensor.matmul(ps_s2[:], ones_col[:],
                                                ysq_sb[gq % 2][:],
                                                start=(c == 0), stop=(c == NCH - 1))
                            ins.then_inc(s_s2, 1)
                            gq += 1
                        tensor.wait_ge(s_stat, 3 * its + 1)
                        if its >= 1:
                            tensor.wait_ge(s_x1, 8 * its)
                        tensor.matmul(ps_bm[:], ones_row[:], mu_bf[:],
                                      start=True, stop=True)
                        tensor.wait_ge(s_stat, 3 * its + 3)
                        tensor.matmul(ps_brd[:], ones_row[:], rstd_bf[:],
                                      start=True, stop=True).then_inc(s_bcln, 2)
    return nc





class LNEmit:
    def __init__(self, nc, ctx, ctxP, name):
        sb = lambda n, shape, dt=F32: ctx.enter_context(
            nc.sbuf_tensor(f"{name}_{n}", shape, dt))
        self.ps_s1 = ctxP.enter_context(nc.psum_tensor(f"{name}_ps_s1", [1, 512], F32))
        self.ps_s2 = ctxP.enter_context(nc.psum_tensor(f"{name}_ps_s2", [1, 512], F32))
        self.ps_bm = ctxP.enter_context(nc.psum_tensor(f"{name}_ps_bm", [128, 512], F32))
        self.ps_brd = ctxP.enter_context(nc.psum_tensor(f"{name}_ps_brd", [128, 512], F32))
        self.mu_f = sb("mu_f", [1, 512]); self.m2_f = sb("m2_f", [1, 512])
        self.t_f = sb("t_f", [1, 512]); self.var_f = sb("var_f", [1, 512])
        self.lnv_f = sb("lnv_f", [1, 512]); self.rstd_f = sb("rstd_f", [1, 512])
        self.mu_bf = sb("mu_bf", [1, 512], BF16)
        self.rstd_bf = sb("rstd_bf", [1, 512], BF16)
        self.tnorm = sb("tnorm", [128, 512], BF16)
        self.ysq = [sb(f"ysq{i}", [128, 512], BF16) for i in range(2)]
        self.ones_col = sb("ones_col", [128, 1], BF16)
        self.ones_row = sb("ones_row", [1, 128], BF16)
        self.eps = sb("eps", [1, 1], F32)
        sem = lambda n: ctx.enter_context(nc.semaphore(f"{name}_{n}"))
        self.s_sq = sem("sq"); self.s_s = sem("s"); self.s_s2 = sem("s2")
        self.s_stat = sem("stat"); self.s_bcln = sem("bcln")
        self.s_x1 = sem("x1"); self.s_v = sem("v"); self.s_scs = sem("scs")
        self.s_ini = sem("ini")
        self.nv = 0

    def vector_init(self, vector):
        vector.memset(self.ones_col[:], 1.0)
        vector.memset(self.ones_row[:], 1.0)
        vector.memset(self.eps[:], 1e-5).then_inc(self.s_ini, 1)

    def pool_ops(self, gpsimd, y_ap, wait_y):
        gq = 0
        for its in range(4):
            for c in range(NCH):
                wait_y(gpsimd, its, c)
                if gq >= 2:
                    gpsimd.wait_ge(self.s_s2, gq - 1)
                gpsimd.tensor_tensor(self.ysq[gq % 2][:], y_ap(c, its),
                                     y_ap(c, its), ALU.mult).then_inc(self.s_sq, 1)
                gq += 1

    def tensor_ops(self, tensor, y_ap, wait_y):
        gq = 0
        for its in range(4):
            if its >= 1:
                tensor.wait_ge(self.s_stat, 3 * its)
            for c in range(NCH):
                wait_y(tensor, its, c)
                ins = tensor.matmul(self.ps_s1[:], self.ones_col[:],
                                    y_ap(c, its),
                                    start=(c == 0), stop=(c == NCH - 1))
            ins.then_inc(self.s_s, 1)
            for c in range(NCH):
                tensor.wait_ge(self.s_sq, gq + 1)
                tensor.matmul(self.ps_s2[:], self.ones_col[:],
                              self.ysq[gq % 2][:],
                              start=(c == 0),
                              stop=(c == NCH - 1)).then_inc(self.s_s2, 1)
                gq += 1
            tensor.wait_ge(self.s_stat, 3 * its + 1)
            if its >= 1:
                tensor.wait_ge(self.s_x1, 8 * its)
            tensor.matmul(self.ps_bm[:], self.ones_row[:], self.mu_bf[:],
                          start=True, stop=True)
            tensor.wait_ge(self.s_stat, 3 * its + 3)
            tensor.matmul(self.ps_brd[:], self.ones_row[:], self.rstd_bf[:],
                          start=True, stop=True).then_inc(self.s_bcln, 2)

    def scalar_ops(self, scalar):
        scalar.wait_ge(self.s_ini, 1)
        for its in range(4):
            scalar.wait_ge(self.s_stat, 3 * its + 1)
            scalar.activation(self.lnv_f[:], self.var_f[:], AF.Ln,
                              bias=self.eps[0:1, 0:1]).then_inc(self.s_scs, 1)
            scalar.wait_ge(self.s_scs, its + 1)
            scalar.activation(self.rstd_f[:], self.lnv_f[:], AF.Exp,
                              scale=-0.5).then_inc(self.s_stat, 1)

    def vector_ops(self, vector, y_ap, out_ap):
        for its in range(4):
            vector.wait_ge(self.s_s, its + 1)
            vector.wait_ge(self.s_s2, NCH * (its + 1))
            vector.tensor_scalar(self.mu_f[:], self.ps_s1[:], 1.0 / DM, None,
                                 ALU.mult)
            vector.tensor_scalar(self.m2_f[:], self.ps_s2[:], 1.0 / DM, None,
                                 ALU.mult).then_inc(self.s_v, 1)
            self.nv += 1
            vector.wait_ge(self.s_v, self.nv)
            vector.tensor_tensor(self.t_f[:], self.mu_f[:], self.mu_f[:],
                                 ALU.mult).then_inc(self.s_v, 1)
            self.nv += 1
            vector.wait_ge(self.s_v, self.nv)
            vector.tensor_tensor(self.var_f[:], self.m2_f[:], self.t_f[:],
                                 ALU.subtract).then_inc(self.s_v, 1)
            self.nv += 1
            vector.tensor_copy(self.mu_bf[:], self.mu_f[:]).then_inc(self.s_stat, 1)
            vector.wait_ge(self.s_stat, 3 * its + 2)
            vector.tensor_copy(self.rstd_bf[:],
                               self.rstd_f[:]).then_inc(self.s_stat, 1)
            for c in range(NCH):
                vector.wait_ge(self.s_bcln, 2 * (its + 1))
                vector.tensor_tensor(self.tnorm[:], y_ap(c, its),
                                     self.ps_bm[:], ALU.subtract).then_inc(self.s_v, 1)
                self.nv += 1
                vector.wait_ge(self.s_v, self.nv)
                vector.tensor_tensor(out_ap(c, its), self.tnorm[:],
                                     self.ps_brd[:], ALU.mult).then_inc(self.s_x1, 1)



def build_l0ffn():
    nc = bass.Bass("TRN2", target_bir_lowering=False, debug=False)
    x1_in = nc.dram_tensor("x1_fm", [DM, TPC], BF16, kind="ExternalInput").ap()
    halo_in = nc.dram_tensor("halo_pc", [128, 8], F32, kind="ExternalInput").ap()
    mur_in = nc.dram_tensor("mur_pc", [128, 8], F32, kind="ExternalInput").ap()
    muk_in = nc.dram_tensor("muk_pc", [128, 8], F32, kind="ExternalInput").ap()
    w_in = {n: nc.dram_tensor(n, [DM, DM], F32, kind="ExternalInput").ap()
            for n in ("WrT", "WkT", "WvT")}
    x2_out = nc.dram_tensor("x2_fm", [DM, TPC], BF16, kind="ExternalOutput").ap()

    ctx = ExitStack()
    sb, ps, sem = _mk(nc, ctx)
    with ctx:
        x1_sb = [sb(f"x1_{c}", [128, TPC], BF16) for c in range(NCH)]
        xr_sb = [sb(f"xr_{c}", [128, TPC], BF16) for c in range(NCH)]
        xk_sb = [sb(f"xk_{c}", [128, TPC], BF16) for c in range(NCH)]
        ka2_sb = [sb(f"ka2_{c}", [128, TPC], BF16) for c in range(NCH)]
        wA = [sb(f"wA{c}", [128, DM], BF16) for c in range(NCH)]
        wB = [sb(f"wB{c}", [128, DM], BF16) for c in range(NCH)]
        wstage = sb("wstage", [128, DM], F32)
        halo_sb = sb("halo_s", [128, 8], F32)
        mur_sb = sb("mur_s", [128, 8], F32)
        muk_sb = sb("muk_s", [128, 8], F32)
        xprev = sb("xprev", [128, TPC], BF16)
        dtmp = sb("dtmp", [128, TPC], BF16)
        ka_t = [sb(f"ka_t{i}", [128, 512], BF16) for i in range(2)]
        r_t = [sb(f"r_t{i}", [128, 512], BF16) for i in range(2)]
        t_t = [sb(f"t_t{i}", [128, 512], BF16) for i in range(2)]

        s_ld = sem("s_ld"); s_cast = sem("s_cast"); s_x = sem("s_x")
        s_ldw3 = sem("s_ldw3")
        s_mmk = sem("s_mmk"); s_rl = sem("s_rl"); s_ka2 = sem("s_ka2")
        s_mm2 = sem("s_mm2"); s_sg = sem("s_sg"); s_ff = sem("s_ff")
        s_y = sem("s_y"); s_st = sem("s_st"); s_v2 = sem("s_v2")

        ctxA = ExitStack()
        ps_k = [ctxA.enter_context(nc.psum_tensor(f"ps_k{i}", [128, 512], F32))
                for i in range(2)]
        ps_kv = [ctxA.enter_context(nc.psum_tensor(f"ps_kv{i}", [128, 512], F32))
                 for i in range(2)]
        ps_r = [ctxA.enter_context(nc.psum_tensor(f"ps_r{i}", [128, 512], F32))
                for i in range(2)]
        with ctxA:
            with nc.Block() as block:
                @block.sync
                def _(sync):
                    for c in range(NCH):
                        sync.dma_start(out=x1_sb[c][:],
                                       in_=x1_in[128 * c:128 * (c + 1), :]).then_inc(s_ld, 16)
                    for t, src_ in ((halo_sb, halo_in), (mur_sb, mur_in),
                                    (muk_sb, muk_in)):
                        sync.dma_start(out=t[:], in_=src_).then_inc(s_ld, 16)
                    i = 0
                    for wn in ("WkT", "WrT", "WvT"):
                        for c in range(NCH):
                            if i > 0:
                                sync.wait_ge(s_cast, i)
                            if wn == "WrT" and c == 0:
                                sync.wait_ge(s_mmk, 32)
                            sync.dma_start(out=wstage[:],
                                           in_=w_in[wn][128 * c:128 * (c + 1),
                                                        :]).then_inc(s_ldw3, 16)
                            i += 1

                @block.vector
                def _(vector):
                    i = 0
                    for c in range(NCH):
                        vector.wait_ge(s_ldw3, 16 * (i + 1))
                        vector.tensor_copy(wA[c][:],
                                           wstage[:]).then_inc(s_cast, 1)
                        i += 1
                    nv = 0
                    vector.wait_ge(s_ld, 16 * 11)
                    for c in range(NCH):
                        vector.tensor_copy(xprev[:, 1:TPC],
                                           x1_sb[c][:, 0:TPC - 1])
                        vector.tensor_copy(xprev[:, 0:1],
                                           halo_sb[:, c:c + 1]).then_inc(s_v2, 1)
                        nv += 1
                        vector.wait_ge(s_v2, nv)
                        vector.tensor_tensor(dtmp[:], x1_sb[c][:], xprev[:],
                                             ALU.subtract).then_inc(s_v2, 1)
                        nv += 1
                        vector.wait_ge(s_v2, nv)
                        vector.scalar_tensor_tensor(
                            xr_sb[c][:], dtmp[:], mur_sb[:, c:c + 1], xprev[:],
                            ALU.mult, ALU.add)
                        vector.scalar_tensor_tensor(
                            xk_sb[c][:], dtmp[:], muk_sb[:, c:c + 1], xprev[:],
                            ALU.mult, ALU.add).then_inc(s_x, 1)
                    for wn, dst in (("WrT", wA), ("WvT", wB)):
                        for c in range(NCH):
                            vector.wait_ge(s_ldw3, 16 * (i + 1))
                            vector.tensor_copy(dst[c][:],
                                               wstage[:]).then_inc(s_cast, 1)
                            i += 1
                    j = 0
                    for hp in range(NCH):
                        for its in range(4):
                            vector.wait_ge(s_sg, j + 1)
                            vector.scalar_tensor_tensor(
                                t_t[j % 2][:], ps_kv[j % 2][:], 1.0,
                                r_t[j % 2][:], ALU.mult, ALU.mult).then_inc(s_ff, 1)
                            vector.wait_ge(s_ff, j + 1)
                            vector.tensor_tensor(
                                x1_sb[hp][:, 512 * its:512 * (its + 1)],
                                x1_sb[hp][:, 512 * its:512 * (its + 1)],
                                t_t[j % 2][:], ALU.add).then_inc(s_y, 1)
                            j += 1

                @block.tensor
                def _(tensor):
                    j = 0
                    for hp in range(NCH):
                        for its in range(4):
                            tensor.wait_ge(s_cast, 8)
                            tensor.wait_ge(s_x, NCH)
                            if j >= 2:
                                tensor.wait_ge(s_rl, j - 1)
                            for c in range(NCH):
                                ins = tensor.matmul(
                                    ps_k[j % 2][:],
                                    wA[c][:, 128 * hp:128 * (hp + 1)],
                                    xk_sb[c][:, 512 * its:512 * (its + 1)],
                                    start=(c == 0), stop=(c == NCH - 1))
                            ins.then_inc(s_mmk, 1)
                            j += 1
                    tensor.wait_ge(s_cast, 24)
                    tensor.wait_ge(s_ka2, 32)
                    j = 0
                    for hp in range(NCH):
                        for its in range(4):
                            if j >= 2:
                                tensor.wait_ge(s_ff, j - 1)
                            for c in range(NCH):
                                ins = tensor.matmul(
                                    ps_kv[j % 2][:],
                                    wB[c][:, 128 * hp:128 * (hp + 1)],
                                    ka2_sb[c][:, 512 * its:512 * (its + 1)],
                                    start=(c == 0), stop=(c == NCH - 1))
                            ins.then_inc(s_mm2, 1)
                            if j >= 2:
                                tensor.wait_ge(s_sg, j - 1)
                            for c in range(NCH):
                                ins = tensor.matmul(
                                    ps_r[j % 2][:],
                                    wA[c][:, 128 * hp:128 * (hp + 1)],
                                    xr_sb[c][:, 512 * its:512 * (its + 1)],
                                    start=(c == 0), stop=(c == NCH - 1))
                            ins.then_inc(s_mm2, 1)
                            j += 1

                @block.scalar
                def _(scalar):
                    for j in range(32):
                        scalar.wait_ge(s_mmk, j + 1)
                        if j >= 2:
                            scalar.wait_ge(s_ka2, j - 1)
                        scalar.activation(ka_t[j % 2][:], ps_k[j % 2][:],
                                          AF.Relu).then_inc(s_rl, 1)
                    for j in range(32):
                        scalar.wait_ge(s_mm2, 2 * j + 2)
                        if j >= 2:
                            scalar.wait_ge(s_ff, j - 1)
                        scalar.activation(r_t[j % 2][:], ps_r[j % 2][:],
                                          AF.Sigmoid).then_inc(s_sg, 1)

                @block.gpsimd
                def _(gpsimd):
                    for j in range(32):
                        hp, its = divmod(j, 4)
                        gpsimd.wait_ge(s_rl, j + 1)
                        gpsimd.tensor_tensor(
                            ka2_sb[hp][:, 512 * its:512 * (its + 1)],
                            ka_t[j % 2][:], ka_t[j % 2][:],
                            ALU.mult).then_inc(s_ka2, 1)

        ctxB = ExitStack()
        ln = LNEmit(nc, ctx, ctxB, "ln2")
        with ctxB:
            y_ap = lambda c, its: x1_sb[c][:, 512 * its:512 * (its + 1)]
            out_ap = lambda c, its: xr_sb[c][:, 512 * its:512 * (its + 1)]

            def wait_y(eng, its, c):
                eng.wait_ge(s_y, 4 * NCH)

            with nc.Block() as block2:
                @block2.sync
                def _(sync):
                    for c in range(NCH):
                        sync.wait_ge(ln.s_x1, 3 * NCH + c + 1)
                        sync.dma_start(out=x2_out[128 * c:128 * (c + 1), :],
                                       in_=xr_sb[c][:]).then_inc(s_st, 16)
                    sync.wait_ge(s_st, 16 * NCH)

                @block2.vector
                def _(vector):
                    ln.vector_init(vector)
                    ln.vector_ops(vector, y_ap, out_ap)

                @block2.scalar
                def _(scalar):
                    ln.scalar_ops(scalar)

                @block2.gpsimd
                def _(gpsimd):
                    ln.pool_ops(gpsimd, y_ap, wait_y)

                @block2.tensor
                def _(tensor):
                    tensor.wait_ge(ln.s_ini, 1)
                    ln.tensor_ops(tensor, y_ap, wait_y)
    return nc



def build_l1qkv():
    nc = bass.Bass("TRN2", target_bir_lowering=False, debug=False)
    x2_in = nc.dram_tensor("x2_fm", [DM, TPC], BF16, kind="ExternalInput").ap()
    w_in = {n: nc.dram_tensor(n, [DM, DM], F32, kind="ExternalInput").ap()
            for n in ("WqT", "WkT", "WvT")}
    b_in = {n: nc.dram_tensor(n, [128, 8], F32, kind="ExternalInput").ap()
            for n in ("bq_pc", "bk_pc", "bv_pc")}
    q_out = nc.dram_tensor("q_fm", [DM, TPC], BF16, kind="ExternalOutput").ap()
    k_out = nc.dram_tensor("k_fm", [DM, TPC], BF16, kind="ExternalOutput").ap()
    v_out = nc.dram_tensor("v_fm", [DM, TPC], BF16, kind="ExternalOutput").ap()
    sv_out = nc.dram_tensor("sv_p", [128, 8], F32, kind="ExternalOutput").ap()

    ctx = ExitStack()
    sb, ps, sem = _mk(nc, ctx)
    with ctx:
        x_sb = [sb(f"x_{c}", [128, TPC], BF16) for c in range(NCH)]
        wstage = sb("wstage", [128, DM], F32)
        w_bf = [sb(f"w_bf{c}", [128, DM], BF16) for c in range(NCH)]
        bias_sb = {n: sb(n + "_s", [128, 8], F32) for n in ("bq", "bk", "bv")}
        evac = [sb(f"evac{i}", [128, TPC], BF16) for i in range(2)]
        sv_sb = sb("sv_s", [128, 8], F32)
        pst = [ps(f"pst{i}", [128, 512]) for i in range(4)]

        s_ld = sem("s_ld"); s_cast = sem("s_cast"); s_mm = sem("s_mm")
        s_ev = sem("s_ev"); s_st = sem("s_st"); s_sv = sem("s_sv")
        s_ldw4 = sem("s_ldw4")

        wnames = ("WqT", "WkT", "WvT")
        bnames = ("bq", "bk", "bv")
        outs = (q_out, k_out, v_out)

        with nc.Block() as block:
            @block.sync
            def _(sync):
                for c in range(NCH):
                    sync.dma_start(out=x_sb[c][:],
                                   in_=x2_in[128 * c:128 * (c + 1), :]).then_inc(s_ld, 16)
                for n in bnames:
                    sync.dma_start(out=bias_sb[n][:],
                                   in_=b_in[n + "_pc"]).then_inc(s_ld, 16)
                i = 0
                for wi, wn in enumerate(wnames):
                    for c in range(NCH):
                        if i > 0:
                            sync.wait_ge(s_cast, i)
                        if c == 0 and wi > 0:
                            sync.wait_ge(s_mm, 8 * wi)
                        sync.dma_start(out=wstage[:],
                                       in_=w_in[wn][128 * c:128 * (c + 1),
                                                    :]).then_inc(s_ldw4, 16)
                        i += 1

            @block.gpsimd
            def _(gpsimd):
                g = 0
                for wi in range(3):
                    for hp in range(NCH):
                        g += 1
                        gpsimd.wait_ge(s_ev, g)
                        gpsimd.dma_start(out=outs[wi][128 * hp:128 * (hp + 1), :],
                                         in_=evac[(g - 1) % 2][:]).then_inc(s_st, 16)
                        gpsimd.wait_ge(s_st, 16 * g)
                gpsimd.wait_ge(s_sv, NCH)
                gpsimd.dma_start(out=sv_out, in_=sv_sb[:]).then_inc(s_st, 16)
                gpsimd.wait_ge(s_st, 16 * (g + 1))

            @block.vector
            def _(vector):
                i = 0
                for wi in range(3):
                    for c in range(NCH):
                        vector.wait_ge(s_ldw4, 16 * (i + 1))
                        vector.tensor_copy(w_bf[c][:],
                                           wstage[:]).then_inc(s_cast, 1)
                        i += 1
                for hp in range(NCH):
                    vector.wait_ge(s_ev, 16 + hp + 1)
                    vector.reduce_sum(sv_sb[:, hp:hp + 1],
                                      evac[(16 + hp) % 2][:],
                                      axis=AX.X).then_inc(s_sv, 1)

            @block.scalar
            def _(scalar):
                scalar.wait_ge(s_ld, 16 * 11)
                g = 0
                for wi in range(3):
                    for hp in range(NCH):
                        g += 1
                        scalar.wait_ge(s_mm, g)
                        if g > 2:
                            scalar.wait_ge(s_st, 16 * (g - 2))
                        for its in range(4):
                            ins = scalar.activation(
                                evac[(g - 1) % 2][:, 512 * its:512 * (its + 1)],
                                pst[its][:], AF.Identity,
                                bias=bias_sb[bnames[wi]][:, hp:hp + 1])
                        ins.then_inc(s_ev, 1)

            @block.tensor
            def _(tensor):
                tensor.wait_ge(s_ld, 16 * 11)
                g = 0
                for wi in range(3):
                    for hp in range(NCH):
                        tensor.wait_ge(s_cast, 8 * wi + 8)
                        g += 1
                        if g > 1:
                            tensor.wait_ge(s_ev, g - 1)
                        for its in range(4):
                            for c in range(NCH):
                                ins = tensor.matmul(
                                    pst[its][:],
                                    w_bf[c][:, 128 * hp:128 * (hp + 1)],
                                    x_sb[c][:, 512 * its:512 * (its + 1)],
                                    start=(c == 0), stop=(c == NCH - 1))
                        ins.then_inc(s_mm, 1)
    return nc



DELTAS = [0, 1, 2, 4, 8, 16, 32, 64, 128, 256, 512, 1024, 2048, 4096]
BIGSLOTS = [(d, i) for d in (1024, 2048, 4096) for i in range(4)
            if 512 + 512 * i - d < 0]
BIGIDX = {s: i for i, s in enumerate(BIGSLOTS)}
NBIG = len(BIGSLOTS)


def build_l1attn():
    nc = bass.Bass("TRN2", target_bir_lowering=False, debug=False)
    x2_in = nc.dram_tensor("x2_fm", [DM, TPC], BF16, kind="ExternalInput").ap()
    q_in = nc.dram_tensor("q_fm", [DM, TPC], BF16, kind="ExternalInput").ap()
    kwin_in = nc.dram_tensor("kwin", [DM, 2560], BF16, kind="ExternalInput").ap()
    vwin_in = nc.dram_tensor("vwin", [DM, 2560], BF16, kind="ExternalInput").ap()
    kbig_in = nc.dram_tensor("kbig", [DM, 512 * NBIG], BF16, kind="ExternalInput").ap()
    vbig_in = nc.dram_tensor("vbig", [DM, 512 * NBIG], BF16, kind="ExternalInput").ap()
    sv4_in = nc.dram_tensor("sv4", [128, 32], F32, kind="ExternalInput").ap()
    selp_in = nc.dram_tensor("selp", [128, 522], BF16, kind="ExternalInput").ap()
    e2t_in = nc.dram_tensor("e2t", [2, 128], BF16, kind="ExternalInput").ap()
    wo_in = nc.dram_tensor("WoT", [DM, DM], F32, kind="ExternalInput").ap()
    bo_in = nc.dram_tensor("bo_pc", [128, 8], F32, kind="ExternalInput").ap()
    x3_out = nc.dram_tensor("x3_fm", [DM, TPC], BF16, kind="ExternalOutput").ap()

    ctx = ExitStack()
    sb, ps, sem = _mk(nc, ctx)
    with ctx:
        x2_sb = [sb(f"x2_{c}", [128, TPC], BF16) for c in range(NCH)]
        q_sb = [sb(f"q_{c}", [128, TPC], BF16) for c in range(NCH)]
        kwin_sb = [sb(f"kw_{c}", [128, 2560], BF16) for c in range(NCH)]
        vwin_sb = [sb(f"vw_{c}", [128, 2560], BF16) for c in range(NCH)]
        kbig_t = [sb(f"kb_{i}", [128, 512], BF16) for i in range(3)]
        vbig_t = [sb(f"vb_{i}", [128, 512], BF16) for i in range(3)]
        sv4_sb = sb("sv4_s", [128, 32], F32)
        sv_sb = sb("sv_s", [128, 8], F32)
        selp_sb = sb("selp_s", [128, 522], BF16)
        e2t_sb = sb("e2t_s", [2, 128], BF16)
        wstage = sb("wstage", [128, DM], F32)
        w_bf = [sb(f"w_bf{c}", [128, DM], BF16) for c in range(NCH)]
        bo_sb = sb("bo_s", [128, 8], F32)
        p_t = [sb(f"p_t{i}", [128, 512], BF16) for i in range(2)]
        pexp_sb = [sb(f"pexp{i}", [128, 512], BF16) for i in range(4)]
        u_sb = [sb(f"u{i}", [128, 512], BF16) for i in range(4)]
        tz_f = sb("tz_f", [2, 512], F32)
        rz_f = sb("rz_f", [2, 512], F32)
        rz_bf = sb("rz_bf", [2, 512], BF16)
        t_av = [sb(f"t_av{i}", [128, 512], BF16) for i in range(2)]
        acc = [sb(f"acc{i}", [128, 512], BF16) for i in range(2)]

        s_ld = sem("s_ld"); s_cast = sem("s_cast"); s_pp = sem("s_pp")
        s_scm = sem("s_scm"); s_ex = sem("s_ex"); s_u = sem("s_u")
        s_z = sem("s_z"); s_bc = sem("s_bc"); s_avv = sem("s_avv")
        s_rz = sem("s_rz"); s_bcz = sem("s_bcz"); s_ep = sem("s_ep")
        s_kb = sem("s_kb"); s_vb = sem("s_vb"); s_v5 = sem("s_v5")
        s_o = sem("s_o"); s_y = sem("s_y"); s_st = sem("s_st")

        ITERS = [(c, i) for c in range(NCH) for i in range(4)]
        big_uses = []
        for n, (c8, its) in enumerate(ITERS):
            for di, d in enumerate(DELTAS):
                if (d, its) in BIGIDX:
                    big_uses.append((n, di))
        buse_ord = {u: i for i, u in enumerate(big_uses)}
        NLD0 = 8 + 8 + 8 + 8 + 5

        e2_ap = lambda: selp_sb[:, 0:2]
        zsel_ap = lambda b: selp_sb[:, 2 + 2 * b:2 + 2 * b + 2]
        selu_ap = lambda k: selp_sb[:, 10 + 128 * k:10 + 128 * (k + 1)]

        def ksrc(c8, its, di):
            d = DELTAS[di]
            col = 512 + 512 * its - d
            if col >= 0:
                return kwin_sb[c8][:, col:col + 512]
            return kbig_t[buse_ord[(ITERS.index((c8, its)), di)] % 3][:]

        def vsrc(c8, its, di):
            d = DELTAS[di]
            col = 512 + 512 * its - d
            if col >= 0:
                return vwin_sb[c8][:, col:col + 512]
            return vbig_t[buse_ord[(ITERS.index((c8, its)), di)] % 3][:]

        ctxA = ExitStack()
        ps_sc = [ctxA.enter_context(nc.psum_tensor(f"ps_sc{i}", [128, 512], F32))
                 for i in range(4)]
        ps_z = ctxA.enter_context(nc.psum_tensor("ps_z", [2, 512], F32))
        ps_u = [ctxA.enter_context(nc.psum_tensor(f"ps_u{i}", [128, 512], F32))
                for i in range(2)]
        ps_rz = ctxA.enter_context(nc.psum_tensor("ps_rz", [128, 512], F32))
        with ctxA:
            with nc.Block() as block:
                @block.sync
                def _(sync):
                    nld = 0
                    for arr, src_ in ((x2_sb, x2_in), (q_sb, q_in),
                                      (kwin_sb, kwin_in), (vwin_sb, vwin_in)):
                        for c in range(NCH):
                            sync.dma_start(out=arr[c][:],
                                           in_=src_[128 * c:128 * (c + 1), :]).then_inc(s_ld, 16)
                            nld += 1
                    for t, src_ in ((sv4_sb, sv4_in), (selp_sb, selp_in),
                                    (e2t_sb, e2t_in), (bo_sb, bo_in)):
                        sync.dma_start(out=t[:], in_=src_).then_inc(s_ld, 16)
                        nld += 1
                    for bu_i, (n, di) in enumerate(big_uses):
                        c8, its = ITERS[n]
                        d = DELTAS[di]
                        slot = BIGIDX[(d, its)]
                        if bu_i >= 3:
                            pn, pdi = big_uses[bu_i - 3]
                            sync.wait_ge(s_pp, 14 * pn + pdi + 1)
                            sync.wait_ge(s_avv, 14 * pn + pdi + 1)
                        sync.dma_start(
                            out=kbig_t[bu_i % 3][:],
                            in_=kbig_in[128 * c8:128 * (c8 + 1),
                                        512 * slot:512 * (slot + 1)]).then_inc(s_kb, 16)
                        sync.dma_start(
                            out=vbig_t[bu_i % 3][:],
                            in_=vbig_in[128 * c8:128 * (c8 + 1),
                                        512 * slot:512 * (slot + 1)]).then_inc(s_vb, 16)
                        sync.wait_ge(s_kb, 16 * (bu_i + 1))
                        sync.wait_ge(s_vb, 16 * (bu_i + 1))

                @block.gpsimd
                def _(gpsimd):
                    pass

                @block.tensor
                def _(tensor):
                    tensor.wait_ge(s_ld, 16 * 36)
                    tensor.wait_ge(s_v5, 4)
                    for n, (c8, its) in enumerate(ITERS):
                        if n >= 1:
                            tensor.wait_ge(s_ep, n)
                        for di, d in enumerate(DELTAS):
                            t = 14 * n + di
                            b, k = di // 4, di % 4
                            tensor.wait_ge(s_pp, t + 1)
                            tensor.matmul(ps_sc[b][32 * k:32 * k + 2, :],
                                          e2_ap(), p_t[t % 2][:],
                                          start=True, stop=True,
                                          tile_position=(0, 32 * k)).then_inc(s_scm, 1)
                        for b in range(4):
                            tensor.wait_ge(s_u, 4 * n + b + 1)
                            ins = tensor.matmul(ps_z[0:2, :], zsel_ap(b),
                                                u_sb[b][:], start=(b == 0),
                                                stop=(b == 3))
                        ins.then_inc(s_z, 1)
                        for di, d in enumerate(DELTAS):
                            t = 14 * n + di
                            b, k = di // 4, di % 4
                            if t >= 2:
                                tensor.wait_ge(s_avv, t - 1)
                            tensor.matmul(ps_u[t % 2][:], selu_ap(k),
                                          u_sb[b][:], start=True,
                                          stop=True).then_inc(s_bc, 1)
                        tensor.wait_ge(s_rz, n + 1)
                        if n >= 1:
                            tensor.wait_ge(s_ep, n)
                        tensor.matmul(ps_rz[:], e2t_sb[0:2, :],
                                      rz_bf[0:2, :], start=True,
                                      stop=True).then_inc(s_bcz, 1)

                @block.scalar
                def _(scalar):
                    for n in range(len(ITERS)):
                        if n >= 1:
                            scalar.wait_ge(s_ep, n)
                        for b in range(4):
                            scalar.wait_ge(s_scm, 14 * n + min(4 * b + 4, 14))
                            if n >= 1:
                                scalar.wait_ge(s_u, 4 * n + b - 3)
                            scalar.activation(pexp_sb[b][:], ps_sc[b][:],
                                              AF.Exp,
                                              scale=0.125).then_inc(s_ex, 1)

                @block.vector
                def _(vector):
                    vector.wait_ge(s_ld, 16 * 36)
                    sv4v = sv4_sb[:].rearrange("p (c j) -> p c j", j=4)
                    vector.tensor_tensor(sv_sb[:], sv4v[:, :, 0],
                                         sv4v[:, :, 1], ALU.add).then_inc(s_v5, 1)
                    vector.wait_ge(s_v5, 1)
                    vector.tensor_tensor(sv_sb[:], sv_sb[:],
                                         sv4v[:, :, 2], ALU.add).then_inc(s_v5, 1)
                    vector.wait_ge(s_v5, 2)
                    vector.tensor_tensor(sv_sb[:], sv_sb[:],
                                         sv4v[:, :, 3], ALU.add).then_inc(s_v5, 1)
                    nv = 3
                    for b in range(4):
                        vector.memset(ps_sc[b][:], 0.0)
                    vector.memset(ps_z[:], 0.0)
                    vector.memset(ps_u[0][:], 0.0)
                    vector.memset(ps_u[1][:], 0.0)
                    vector.memset(ps_rz[:], 0.0).then_inc(s_v5, 1)
                    nv += 1
                    vector.wait_ge(s_ld, 16 * 36)
                    for n, (c8, its) in enumerate(ITERS):
                        for di, d in enumerate(DELTAS):
                            t = 14 * n + di
                            if t >= 2:
                                vector.wait_ge(s_scm, t - 1)
                            if (d, its) in BIGIDX:
                                vector.wait_ge(s_kb, 16 * (buse_ord[(n, di)] + 1))
                            vector.tensor_tensor(
                                p_t[t % 2][:],
                                q_sb[c8][:, 512 * its:512 * (its + 1)],
                                ksrc(c8, its, di), ALU.mult).then_inc(s_pp, 1)
                        for b in range(4):
                            vector.wait_ge(s_ex, 4 * n + b + 1)
                            vector.tensor_scalar(u_sb[b][:], pexp_sb[b][:], 1.0,
                                                 None,
                                                 ALU.subtract).then_inc(s_u, 1)
                        vector.wait_ge(s_z, n + 1)
                        vector.tensor_scalar(tz_f[:], ps_z[0:2, :], 8192.0,
                                             None, ALU.add).then_inc(s_v5, 1)
                        nv += 1
                        vector.wait_ge(s_v5, nv)
                        vector.reciprocal(rz_f[:], tz_f[:]).then_inc(s_v5, 1)
                        nv += 1
                        vector.wait_ge(s_v5, nv)
                        vector.tensor_copy(rz_bf[:], rz_f[:]).then_inc(s_rz, 1)
                        for di, d in enumerate(DELTAS):
                            t = 14 * n + di
                            vector.wait_ge(s_bc, t + 1)
                            if (d, its) in BIGIDX:
                                vector.wait_ge(s_vb, 16 * (buse_ord[(n, di)] + 1))
                            if di == 0:
                                vector.tensor_tensor(acc[0][:],
                                                     vsrc(c8, its, di),
                                                     ps_u[t % 2][:],
                                                     ALU.mult).then_inc(s_avv, 1)
                            else:
                                vector.tensor_tensor(t_av[t % 2][:],
                                                     vsrc(c8, its, di),
                                                     ps_u[t % 2][:],
                                                     ALU.mult).then_inc(s_avv, 1)
                                vector.wait_ge(s_avv, t + 1)
                                vector.tensor_tensor(acc[di % 2][:],
                                                     acc[(di - 1) % 2][:],
                                                     t_av[t % 2][:], ALU.add)
                        vector.wait_ge(s_bcz, n + 1)
                        vector.scalar_tensor_tensor(
                            q_sb[c8][:, 512 * its:512 * (its + 1)],
                            acc[(len(DELTAS) - 1) % 2][:],
                            sv_sb[:, c8:c8 + 1], ps_rz[:],
                            ALU.add, ALU.mult).then_inc(s_ep, 1)

        ctxB = ExitStack()
        ps_o = [ctxB.enter_context(nc.psum_tensor(f"ps_o{i}", [128, 512], F32))
                for i in range(2)]
        ln = LNEmit(nc, ctx, ctxB, "ln3")
        with ctxB:
            y_ap = lambda c, its: x2_sb[c][:, 512 * its:512 * (its + 1)]
            out_ap = lambda c, its: q_sb[c][:, 512 * its:512 * (its + 1)]

            def wait_y(eng, its, c):
                eng.wait_ge(s_y, 4 * NCH)

            with nc.Block() as block2:
                @block2.sync
                def _(sync):
                    for c in range(NCH):
                        if c > 0:
                            sync.wait_ge(s_cast, c)
                        sync.dma_start(out=wstage[:],
                                       in_=wo_in[128 * c:128 * (c + 1), :]).then_inc(s_ld, 16)
                    for c in range(NCH):
                        sync.wait_ge(ln.s_x1, 3 * NCH + c + 1)
                        sync.dma_start(out=x3_out[128 * c:128 * (c + 1), :],
                                       in_=q_sb[c][:]).then_inc(s_st, 16)
                    sync.wait_ge(s_st, 16 * NCH)

                @block2.vector
                def _(vector):
                    ln.vector_init(vector)
                    for c in range(NCH):
                        vector.wait_ge(s_ld, 16 * (36 + c + 1))
                        vector.tensor_copy(w_bf[c][:],
                                           wstage[:]).then_inc(s_cast, 1)
                    j = 0
                    for dc in range(NCH):
                        for its in range(4):
                            vector.wait_ge(s_o, j + 1)
                            vector.scalar_tensor_tensor(
                                x2_sb[dc][:, 512 * its:512 * (its + 1)],
                                ps_o[j % 2][:], bo_sb[:, dc:dc + 1],
                                x2_sb[dc][:, 512 * its:512 * (its + 1)],
                                ALU.add, ALU.add).then_inc(s_y, 1)
                            j += 1
                    ln.vector_ops(vector, y_ap, out_ap)

                @block2.scalar
                def _(scalar):
                    ln.scalar_ops(scalar)

                @block2.gpsimd
                def _(gpsimd):
                    ln.pool_ops(gpsimd, y_ap, wait_y)

                @block2.tensor
                def _(tensor):
                    tensor.wait_ge(s_ep, len(ITERS))
                    tensor.wait_ge(s_cast, NCH)
                    j = 0
                    for dc in range(NCH):
                        for its in range(4):
                            if j >= 2:
                                tensor.wait_ge(s_y, j - 1)
                            for hp in range(NCH):
                                ins = tensor.matmul(
                                    ps_o[j % 2][:],
                                    w_bf[hp][:, 128 * dc:128 * (dc + 1)],
                                    q_sb[hp][:, 512 * its:512 * (its + 1)],
                                    start=(hp == 0), stop=(hp == NCH - 1))
                            ins.then_inc(s_o, 1)
                            j += 1
                    tensor.wait_ge(ln.s_ini, 1)
                    ln.tensor_ops(tensor, y_ap, wait_y)
    return nc



def build_moe_gate():
    nc = bass.Bass("TRN2", target_bir_lowering=False, debug=False)
    x3_in = nc.dram_tensor("x3_fm", [DM, TPC], BF16, kind="ExternalInput").ap()
    wred_in = nc.dram_tensor("w_red", [DM, 16], F32, kind="ExternalInput").ap()
    wg_in = nc.dram_tensor("wg", [8, 16], F32, kind="ExternalInput").ap()
    iota8_in = nc.dram_tensor("iota8", [128, 8], F32, kind="ExternalInput").ap()
    eye_in = nc.dram_tensor("eye", [128, 128], F32, kind="ExternalInput").ap()
    eidx_out = nc.dram_tensor("eidx", [128, 16], F32, kind="ExternalOutput").ap()
    score_out = nc.dram_tensor("score", [128, 16], F32, kind="ExternalOutput").ap()
    x3tm_out = nc.dram_tensor("x3tm", [TPC, DM], BF16, kind="ExternalOutput").ap()

    ctx = ExitStack()
    sb, ps, sem = _mk(nc, ctx)
    with ctx:
        x3_sb = [sb(f"x3_{c}", [128, TPC], BF16) for c in range(NCH)]
        wred_sb = sb("wred_s", [128, 16 * NCH], F32)
        wred_bf = sb("wred_bf", [128, 16 * NCH], BF16)
        wg_sb = sb("wg_s", [8, 16], F32)
        iota8_sb = sb("iota8_s", [128, 8], F32)
        eye_sb = sb("eye_s", [128, 128], F32)
        eye_bf = sb("eye_bf", [128, 128], BF16)
        red_bf = sb("red_bf", [16, TPC], BF16)
        wg2 = sb("wg2", [8, 16], F32)
        n2 = sb("n2", [8, 1], F32)
        lnn = sb("lnn", [8, 1], F32)
        nrm = sb("nrm", [8, 1], F32)
        rn = sb("rn", [8, 1], F32)
        wgn_bf = sb("wgn_bf", [8, 16], BF16)
        wgnT = sb("wgnT", [16, 8], BF16)
        mx = sb("mx", [128, 1], F32)
        nmx = sb("nmx", [128, 1], F32)
        e_t = sb("e_t", [128, 8], BF16)
        ssum = sb("ssum", [128, 1], F32)
        onehot = sb("onehot", [128, 8], BF16)
        idxm = sb("idxm", [128, 8], F32)
        eidx_sb = sb("eidx_s", [128, 16], F32)
        score_sb = sb("score_s", [128, 16], F32)
        xtm_st = [sb(f"xtm{i}", [128, DM], BF16) for i in range(2)]

        s_ld = sem("s_ld"); s_v = sem("s_v"); s_mm = sem("s_mm")
        s_g = sem("s_g"); s_t = sem("s_t"); s_ts = sem("s_ts")
        s_st = sem("s_st"); s_sc = sem("s_sc"); s_rd = sem("s_rd")

        ctxA = ExitStack()
        ps_red = [ctxA.enter_context(nc.psum_tensor(f"ps_red{i}", [16, 512], F32))
                  for i in range(2)]
        ps_l = [ctxA.enter_context(nc.psum_tensor(f"ps_l{i}", [128, 8], F32))
                for i in range(2)]
        ps_wt = ctxA.enter_context(nc.psum_tensor("ps_wt", [16, 8], BF16))
        ps_tr = [ctxA.enter_context(nc.psum_tensor(f"ps_tr{i}", [128, 128], BF16))
                 for i in range(2)]
        with ctxA:
            with nc.Block() as block:
                @block.sync
                def _(sync):
                    for c in range(NCH):
                        sync.dma_start(out=x3_sb[c][:],
                                       in_=x3_in[128 * c:128 * (c + 1), :]).then_inc(s_ld, 16)
                    sync.dma_start(
                        out=wred_sb[:].rearrange("p (c g) -> p c g", g=16),
                        in_=wred_in.rearrange("(c p) g -> p c g", p=128)).then_inc(s_ld, 16)
                    sync.dma_start(out=wg_sb[:], in_=wg_in).then_inc(s_ld, 16)
                    sync.dma_start(out=iota8_sb[:], in_=iota8_in).then_inc(s_ld, 16)
                    sync.dma_start(out=eye_sb[:], in_=eye_in).then_inc(s_ld, 16)
                    sync.wait_ge(s_sc, 16)
                    sync.dma_start(out=eidx_out, in_=eidx_sb[:]).then_inc(s_st, 16)
                    sync.dma_start(out=score_out, in_=score_sb[:]).then_inc(s_st, 16)
                    sync.wait_ge(s_st, 32)
                    for jt in range(16):
                        sync.wait_ge(s_ts, 8 * (jt + 1))
                        sync.dma_start(out=x3tm_out[128 * jt:128 * (jt + 1), :],
                                       in_=xtm_st[jt % 2][:]).then_inc(s_st, 16)
                        sync.wait_ge(s_st, 16 * (jt + 3))

                @block.vector
                def _(vector):
                    nv = 0
                    vector.wait_ge(s_ld, 16 * 12)
                    vector.tensor_copy(wred_bf[:], wred_sb[:])
                    vector.tensor_copy(eye_bf[:], eye_sb[:]).then_inc(s_v, 1)
                    nv += 1
                    vector.wait_ge(s_g, 1)
                    vector.reduce_sum(n2[:], wg2[:], axis=AX.X).then_inc(s_v, 1)
                    nv += 1
                    vector.wait_ge(s_g, 2)
                    vector.tensor_scalar(rn[:], nrm[:], 1e-4, None,
                                         ALU.max).then_inc(s_v, 1)
                    nv += 1
                    vector.wait_ge(s_v, nv)
                    vector.reciprocal(rn[:], rn[:]).then_inc(s_v, 1)
                    nv += 1
                    vector.wait_ge(s_v, nv)
                    vector.tensor_scalar(wgn_bf[:], wg_sb[:], rn[:, 0:1], None,
                                         ALU.mult).then_inc(s_v, 1)
                    nv += 1
                    vector.wait_ge(s_mm, 1)
                    vector.tensor_copy(wgnT[:], ps_wt[:]).then_inc(s_v, 1)
                    nv += 1
                    for jt in range(16):
                        vector.wait_ge(s_mm, 1 + 4 + jt + 1)
                        vector.reduce_max(mx[:], ps_l[jt % 2][:], axis=AX.X)
                        vector.reduce_max(nmx[:], ps_l[jt % 2][:], axis=AX.X,
                                          negate=True).then_inc(s_v, 1)
                        nv += 1
                        vector.wait_ge(s_g, 2 + jt + 1)
                        vector.reciprocal(score_sb[:, jt:jt + 1],
                                          ssum[:]).then_inc(s_v, 1)
                        nv += 1
                        vector.wait_ge(s_v, nv)
                        vector.tensor_scalar(onehot[:], ps_l[jt % 2][:],
                                             mx[:, 0:1], None,
                                             ALU.is_ge).then_inc(s_v, 1)
                        nv += 1
                        vector.wait_ge(s_v, nv)
                        vector.tensor_tensor(idxm[:], onehot[:], iota8_sb[:],
                                             ALU.mult).then_inc(s_v, 1)
                        nv += 1
                        vector.wait_ge(s_v, nv)
                        vector.reduce_sum(eidx_sb[:, jt:jt + 1], idxm[:],
                                          axis=AX.X).then_inc(s_sc, 1)
                    for jt in range(16):
                        if jt >= 2:
                            vector.wait_ge(s_st, 16 * (2 + jt - 1))
                        for c in range(NCH):
                            vector.wait_ge(s_t, 8 * jt + c + 1)
                            vector.tensor_copy(
                                xtm_st[jt % 2][:, 128 * c:128 * (c + 1)],
                                ps_tr[(8 * jt + c) % 2][:]).then_inc(s_ts, 1)

                @block.scalar
                def _(scalar):
                    scalar.wait_ge(s_ld, 16 * 12)
                    scalar.activation(wg2[:], wg_sb[:], AF.Square).then_inc(s_g, 1)
                    scalar.wait_ge(s_v, 2)
                    scalar.activation(lnn[:], n2[:], AF.Ln)
                    scalar.activation(nrm[:], lnn[:], AF.Exp,
                                      scale=0.5).then_inc(s_g, 1)
                    for its in range(4):
                        scalar.wait_ge(s_mm, 1 + its + 1)
                        scalar.copy(red_bf[0:16, 512 * its:512 * (its + 1)],
                                    ps_red[its % 2][0:16, :]).then_inc(s_rd, 1)
                    for jt in range(16):
                        scalar.wait_ge(s_v, 6 + 4 * jt + 1)
                        scalar.activation(e_t[:], ps_l[jt % 2][:], AF.Exp,
                                          bias=nmx[:, 0:1],
                                          accum_out=ssum[:, 0:1]).then_inc(s_g, 1)

                @block.tensor
                def _(tensor):
                    tensor.wait_ge(s_v, 5)
                    tensor.transpose(ps_wt[:], wgn_bf[:],
                                     eye_bf[0:8, 0:8]).then_inc(s_mm, 1)
                    tensor.wait_ge(s_ld, 16 * 12)
                    tensor.wait_ge(s_v, 1)
                    for its in range(4):
                        if its >= 2:
                            tensor.wait_ge(s_rd, its - 1)
                        for c in range(NCH):
                            ins = tensor.matmul(
                                ps_red[its % 2][:],
                                wred_bf[:, 16 * c:16 * (c + 1)],
                                x3_sb[c][:, 512 * its:512 * (its + 1)],
                                start=(c == 0), stop=(c == NCH - 1))
                        ins.then_inc(s_mm, 1)
                    for jt in range(16):
                        its = jt // 4
                        tensor.wait_ge(s_rd, its + 1)
                        if jt >= 2:
                            tensor.wait_ge(s_sc, jt - 1)
                        tensor.matmul(ps_l[jt % 2][:],
                                      red_bf[0:16, 128 * jt:128 * (jt + 1)],
                                      wgnT[0:16, :], start=True,
                                      stop=True).then_inc(s_mm, 1)
                    for jt in range(16):
                        for c in range(NCH):
                            t = 8 * jt + c
                            if t >= 2:
                                tensor.wait_ge(s_ts, t - 1)
                            tensor.transpose(ps_tr[t % 2][:],
                                             x3_sb[c][:, 128 * jt:128 * (jt + 1)],
                                             eye_bf[:]).then_inc(s_t, 1)

                @block.gpsimd
                def _(gpsimd):
                    pass
    return nc



def build_moe_expert():
    nc = bass.Bass("TRN2", target_bir_lowering=False, debug=False)
    eidx_in = nc.dram_tensor("eidx_all", [128, 128], F32, kind="ExternalInput").ap()
    xtm_in = nc.dram_tensor("x3tm_all", [16384, DM], BF16, kind="ExternalInput").ap()
    w1_in = nc.dram_tensor("w1f", [8192, 4096], F32, kind="ExternalInput").ap()
    w2_in = nc.dram_tensor("w2f", [32768, 1024], F32, kind="ExternalInput").ap()
    b1_in = nc.dram_tensor("b1f", [8, 4096], F32, kind="ExternalInput").ap()
    b2_in = nc.dram_tensor("b2f", [8, 1024], F32, kind="ExternalInput").ap()
    rank_in = nc.dram_tensor("rankf", [1, 1], F32, kind="ExternalInput").ap()
    tri_in = nc.dram_tensor("tri128", [128, 128], F32, kind="ExternalInput").ap()
    thr_in = nc.dram_tensor("thr", [8, 32], F32, kind="ExternalInput").ap()
    eye_in = nc.dram_tensor("eye", [128, 128], F32, kind="ExternalInput").ap()
    slotb_in = nc.dram_tensor("slotbase", [1, 20], F32, kind="ExternalInput").ap()
    c8b_in = nc.dram_tensor("c8base128", [1, 8], F32, kind="ExternalInput").ap()
    j32_in = nc.dram_tensor("j32base128", [1, 32], F32, kind="ExternalInput").ap()
    iseg_in = nc.dram_tensor("iotaseg", [128, 3], F32, kind="ExternalInput").ap()
    z_out = nc.dram_tensor("z_slice", [2560, DM], BF16, kind="ExternalOutput").ap()
    post_out = nc.dram_tensor("posT", [128, 128], F32, kind="ExternalOutput").ap()

    d_sorted = nc.dram_tensor("d_sorted", [20480, DM], BF16, kind="Internal").ap()
    d_w1 = nc.dram_tensor("d_w1", [1024, 4096], F32, kind="Internal").ap()
    d_w2 = nc.dram_tensor("d_w2", [4096, 1024], F32, kind="Internal").ap()

    b1v = b1_in.rearrange("e (c p) -> (e c) p", p=128)
    b2v = b2_in.rearrange("e (c p) -> (e c) p", p=128)

    ctx = ExitStack()
    sb, ps, sem = _mk(nc, ctx)
    with ctx:
        em_sb = sb("em_s", [128, 128], F32)
        tri_sb = sb("tri_s", [128, 128], F32)
        tri_bf = sb("tri_bf", [128, 128], BF16)
        thr_sb = sb("thr_s", [8, 32], F32)
        eye_sb = sb("eye_s", [128, 128], F32)
        eye_bf = sb("eye_bf", [128, 128], BF16)
        rank_sb = sb("rank_s", [1, 1], F32)
        slotb_sb = sb("slotb_s", [1, 20], F32)
        c8b_sb = sb("c8b_s", [1, 8], F32)
        j32_sb = sb("j32_s", [1, 32], F32)
        iseg_sb = sb("iseg_s", [128, 3], F32)
        ones128 = sb("ones128", [128, 128], BF16)
        ones_col = sb("onesc", [128, 1], BF16)
        onescf = sb("onescf", [1, 128], F32)
        ones_row = sb("onesr", [1, 128], BF16)
        onesr8 = sb("onesr8", [1, 8], BF16)
        onesr20 = sb("onesr20", [1, 20], BF16)
        onesr32 = sb("onesr32", [1, 32], BF16)
        m_sb = [sb(f"m_{e}", [128, 128], BF16) for e in range(8)]
        scin = sb("scin", [128, 128], F32)
        wex = [sb(f"wex_{e}", [128, 128], F32) for e in range(8)]
        totb = sb("totb", [128, 8], F32)
        totb_bf = sb("totb_bf", [128, 8], BF16)
        tot_sb = sb("tot_s", [8, 1], F32)
        geb = sb("geb", [8, 32], BF16)
        blk8 = sb("blk8", [8, 1], F32)
        pad_f = sb("pad_f", [8, 1], F32)
        pad_bf = sb("pad_bf", [8, 1], BF16)
        go_f = sb("go_f", [8, 1], F32)
        go_bf = sb("go_bf", [8, 1], BF16)
        gorow_bf = sb("gorow_bf", [1, 8], BF16)
        gorow_f = sb("gorow_f", [1, 8], F32)
        bo_sb = sb("bo_s", [128, 8], F32)
        off_sb = sb("off_s", [128, 8], F32)
        ptmp = [sb(f"ptmp{i}", [128, 128], F32) for i in range(2)]
        pacc = [sb(f"pacc{i}", [128, 128], F32) for i in range(2)]
        posT_f = sb("posT_f", [128, 128], F32)
        posT_i = sb("posT_i", [128, 128], I32)
        zsrc = sb("zsrc", [128, DM], BF16)
        v1 = sb("v1", [1, 1], F32)
        flg = sb("flg", [1, 8], BF16)
        e_f = sb("e_f", [1, 5], F32)
        et_f = sb("et_f", [1, 5], F32)
        allidx = sb("allidx", [1, 230], F32)
        idx_f = sb("idx_f", [128, 230], F32)
        idx_i = sb("idx_i", [128, 230], I32)
        xsc = [sb(f"xsc{i}", [128, DM], BF16) for i in range(2)]
        xtm_blk = [sb(f"xtb{i}", [128, DM], BF16) for i in range(2)]
        xfm_blk = [sb(f"xfm{c}", [128, 512], BF16) for c in range(NCH)]
        wtmp2 = [sb(f"wtmp{i}", [128, 4096], F32) for i in range(2)]
        w1stage = [sb(f"w1s{i}", [128, 1024], F32) for i in range(2)]
        w1_bf = [sb(f"w1b{i}", [128, 1024], BF16) for i in range(2)]
        w2stage = sb("w2s", [128, 4096], F32)
        w2_bf = sb("w2b", [128, 4096], BF16)
        b1g = sb("b1g", [128, 128], F32)
        b2g = sb("b2g", [128, 128], F32)
        b1_pc = sb("b1pc", [128, 32], F32)
        b2_pc = sb("b2pc", [128, 8], F32)
        h_sb = sb("h_s", [128, 32 * 512], BF16)
        z_bf = [sb(f"zb{d}", [128, 512], BF16) for d in range(NCH)]
        ztm_st = [sb(f"ztm{i}", [128, DM], BF16) for i in range(2)]

        s_ld = sem("s_ld"); s_va = sem("s_va"); s_ta = sem("s_ta")
        s_ze = sem("s_ze"); s_sca = sem("s_sca"); s_xl2 = sem("s_xl2")
        s_tok = sem("s_tok"); s_tr = sem("s_tr"); s_xe = sem("s_xe")
        s_wf = sem("s_wf"); s_wf2 = sem("s_wf2"); s_ws = sem("s_ws")
        s_wsp = [sem("s_ws0"), sem("s_ws1")]; s_ws2 = sem("s_ws2")
        s_wc = sem("s_wc"); s_hm = sem("s_hm"); s_ge = sem("s_ge")
        s_z2 = sem("s_z2"); s_ev = sem("s_ev"); s_ztr = sem("s_ztr")
        s_zev = sem("s_zev"); s_st = sem("s_st"); s_bt = sem("s_bt")
        s_bte = sem("s_bte"); s_stp = sem("s_stp")

        marks = {}

        ctxA = ExitStack()
        ps_bo = ctxA.enter_context(nc.psum_tensor("ps_bo", [128, 8], F32))
        ps_te = ctxA.enter_context(nc.psum_tensor("ps_te", [8, 1], F32))
        ps_go = ctxA.enter_context(nc.psum_tensor("ps_go", [8, 1], F32))
        ps_got = ctxA.enter_context(nc.psum_tensor("ps_got", [1, 8], BF16))
        ps_gof = ctxA.enter_context(nc.psum_tensor("ps_gof", [1, 8], F32))
        ps_gob = ctxA.enter_context(nc.psum_tensor("ps_gob", [128, 8], F32))
        ps_pt = ctxA.enter_context(nc.psum_tensor("ps_pt", [128, 128], F32))
        ps_ib = ctxA.enter_context(nc.psum_tensor("ps_ib", [128, 230], F32))
        with ctxA:
            with nc.Block() as block:
                @block.sync
                def _(sync):
                    for t, src_ in ((em_sb, eidx_in), (tri_sb, tri_in),
                                    (thr_sb, thr_in), (eye_sb, eye_in),
                                    (rank_sb, rank_in), (slotb_sb, slotb_in),
                                    (c8b_sb, c8b_in), (j32_sb, j32_in),
                                    (iseg_sb, iseg_in)):
                        sync.dma_start(out=t[:], in_=src_).then_inc(s_ld, 16)
                    sync.wait_ge(s_va, 1)
                    for i in range(20480 // 128):
                        sync.dma_start(out=d_sorted[128 * i:128 * (i + 1), :],
                                       in_=zsrc[:]).then_inc(s_ze, 16)
                    sync.wait_ge(s_va, 200)
                    sync.dma_start(out=post_out, in_=posT_f[:]).then_inc(s_stp, 16)
                    sync.wait_ge(s_stp, 16)
                    for j in range(128):
                        if j >= 2:
                            sync.wait_ge(s_sca, 16 * (j - 1))
                        sync.dma_start(out=xsc[j % 2][:],
                                       in_=xtm_in[128 * j:128 * (j + 1),
                                                  :]).then_inc(s_xl2, 16)

                @block.vector
                def _(vector):
                    nv = 0
                    def step(ins, name=None):
                        nonlocal nv
                        ins.then_inc(s_va, 1)
                        nv += 1
                        if name:
                            marks[name] = nv
                        vector.wait_ge(s_va, nv)
                    vector.memset(zsrc[:], 0.0).then_inc(s_va, 1)
                    nv += 1
                    vector.memset(ones128[:], 1.0)
                    vector.memset(ones_col[:], 1.0)
                    vector.memset(onescf[:], 1.0)
                    vector.memset(ones_row[:], 1.0)
                    vector.memset(onesr8[:], 1.0)
                    vector.memset(onesr20[:], 1.0)
                    vector.memset(onesr32[:], 1.0)
                    vector.memset(b1g[:], 0.0)
                    vector.memset(b2g[:], 0.0)
                    vector.wait_ge(s_ld, 16 * 9)
                    vector.tensor_copy(tri_bf[:], tri_sb[:])
                    step(vector.tensor_copy(eye_bf[:], eye_sb[:]), "ini")
                    for e in range(8):
                        step(vector.tensor_scalar(m_sb[e][:], em_sb[:],
                                                  float(e), None, ALU.is_equal))
                        step(vector.tensor_tensor_scan(scin[:], ones128[:],
                                                       m_sb[e][:], 0.0,
                                                       ALU.mult, ALU.add))
                        step(vector.tensor_tensor(wex[e][:], scin[:], m_sb[e][:],
                                                  ALU.subtract))
                        step(vector.reduce_sum(totb[:, e:e + 1], m_sb[e][:],
                                               axis=AX.X))
                    step(vector.tensor_copy(totb_bf[:], totb[:]), "totb")
                    vector.wait_ge(s_ta, 2)
                    step(vector.tensor_copy(tot_sb[:], ps_te[:]))
                    step(vector.tensor_scalar(geb[:], thr_sb[:], tot_sb[:, 0:1],
                                              None, ALU.is_lt))
                    step(vector.reduce_sum(blk8[:], geb[:], axis=AX.X))
                    step(vector.tensor_scalar(pad_f[:], blk8[:], 512.0, None,
                                              ALU.mult))
                    step(vector.tensor_copy(pad_bf[:], pad_f[:]), "pad")
                    vector.wait_ge(s_ta, 3)
                    step(vector.tensor_copy(go_f[:], ps_go[:]))
                    step(vector.tensor_copy(go_bf[:], go_f[:]), "gobf")
                    vector.wait_ge(s_ta, 5)
                    step(vector.tensor_copy(gorow_bf[:], ps_got[:]))
                    step(vector.tensor_copy(gorow_f[:], ps_gof[:]), "gorow")
                    vector.wait_ge(s_ta, 6)
                    step(vector.tensor_copy(bo_sb[:], ps_bo[:]))
                    step(vector.tensor_tensor(off_sb[:], bo_sb[:], ps_gob[:],
                                              ALU.add))
                    for e in range(8):
                        step(vector.scalar_tensor_tensor(
                            ptmp[e % 2][:], wex[e][:], off_sb[:, e:e + 1],
                            m_sb[e][:], ALU.add, ALU.mult))
                        if e == 0:
                            step(vector.tensor_copy(pacc[0][:], ptmp[0][:]))
                        else:
                            step(vector.tensor_tensor(pacc[e % 2][:],
                                                      pacc[(e - 1) % 2][:],
                                                      ptmp[e % 2][:], ALU.add))
                    marks["pacc"] = nv
                    step(vector.tensor_scalar(v1[:], rank_sb[:], 2560.0, None,
                                              ALU.mult))
                    step(vector.scalar_tensor_tensor(
                        allidx[0:1, 200:220], onesr20[:], v1[0:1, 0:1],
                        slotb_sb[:], ALU.mult, ALU.add))
                    for k in range(5):
                        step(vector.tensor_scalar(flg[:], gorow_f[:],
                                                  allidx[0:1, 200 + 4 * k:
                                                         200 + 4 * k + 1],
                                                  None, ALU.is_le))
                        step(vector.reduce_sum(e_f[:, k:k + 1], flg[:],
                                               axis=AX.X))
                    step(vector.tensor_scalar(e_f[:], e_f[:], 1.0, None,
                                              ALU.subtract))
                    step(vector.tensor_scalar(et_f[:], e_f[:], 1024.0, None,
                                              ALU.mult))
                    for k in range(5):
                        step(vector.scalar_tensor_tensor(
                            allidx[0:1, 8 * k:8 * (k + 1)], onesr8[:],
                            et_f[0:1, k:k + 1], c8b_sb[:], ALU.mult, ALU.add))
                    step(vector.tensor_scalar(et_f[:], e_f[:], 4096.0, None,
                                              ALU.mult))
                    for k in range(5):
                        step(vector.scalar_tensor_tensor(
                            allidx[0:1, 40 + 32 * k:40 + 32 * (k + 1)],
                            onesr32[:], et_f[0:1, k:k + 1], j32_sb[:],
                            ALU.mult, ALU.add))
                    step(vector.tensor_scalar(allidx[0:1, 220:225], e_f[:],
                                              32.0, None, ALU.mult))
                    step(vector.tensor_scalar(allidx[0:1, 225:230], e_f[:],
                                              8.0, None, ALU.mult), "allidx")
                    vector.wait_ge(s_ta, 8)
                    step(vector.tensor_scalar(idx_f[:, 0:220], ps_ib[:, 0:220],
                                              iseg_sb[:, 0:1], None, ALU.add))
                    step(vector.tensor_scalar(idx_f[:, 220:225],
                                              ps_ib[:, 220:225],
                                              iseg_sb[:, 1:2], None, ALU.add))
                    step(vector.tensor_scalar(idx_f[:, 225:230],
                                              ps_ib[:, 225:230],
                                              iseg_sb[:, 2:3], None, ALU.add))
                    step(vector.tensor_copy(idx_i[:], idx_f[:]), "idx")
                    vector.wait_ge(s_ta, 7)
                    step(vector.tensor_copy(posT_f[:], ps_pt[:]))
                    step(vector.tensor_copy(posT_i[:], posT_f[:]), "post")
                    while nv < 200:
                        nv += 1
                        vector.sem_inc(s_va, 1)

                @block.tensor
                def _(tensor):
                    tensor.wait_ge(s_va, marks["totb"])
                    tensor.matmul(ps_bo[:], tri_bf[:], totb_bf[:],
                                  start=True, stop=True).then_inc(s_ta, 1)
                    tensor.matmul(ps_te[:], totb_bf[:], ones_col[:, 0:1],
                                  start=True, stop=True).then_inc(s_ta, 1)
                    tensor.wait_ge(s_va, marks["pad"])
                    tensor.matmul(ps_go[:], tri_bf[0:8, 0:8], pad_bf[:, 0:1],
                                  start=True, stop=True).then_inc(s_ta, 1)
                    tensor.wait_ge(s_va, marks["gobf"])
                    tensor.transpose(ps_got[:], go_bf[:, 0:1],
                                     eye_bf[0:8, 0:8]).then_inc(s_ta, 1)
                    tensor.transpose(ps_gof[:], go_f[:, 0:1],
                                     eye_sb[0:8, 0:8]).then_inc(s_ta, 1)
                    tensor.wait_ge(s_va, marks["gorow"])
                    tensor.matmul(ps_gob[:], ones_row[:], gorow_bf[:],
                                  start=True, stop=True).then_inc(s_ta, 1)
                    tensor.wait_ge(s_va, marks["pacc"])
                    tensor.transpose(ps_pt[:], pacc[7 % 2][:],
                                     eye_sb[:]).then_inc(s_ta, 1)
                    tensor.wait_ge(s_va, marks["allidx"])
                    tensor.matmul(ps_ib[:], onescf[:], allidx[:],
                                  start=True, stop=True).then_inc(s_ta, 1)

                @block.gpsimd
                def _(gpsimd):
                    gpsimd.wait_ge(s_ze, 16 * 160)
                    gpsimd.wait_ge(s_va, 200)
                    for j in range(128):
                        gpsimd.wait_ge(s_xl2, 16 * (j + 1))
                        gpsimd.indirect_dma_start(
                            out=d_sorted[:, :],
                            out_offset=bass.IndirectOffsetOnAxis(
                                ap=posT_i[:, j:j + 1], axis=0),
                            in_=xsc[j % 2][:],
                            in_offset=None).then_inc(s_sca, 16)
                    gpsimd.wait_ge(s_sca, 16 * 128)

        ctxB = ExitStack()
        ps_tr = [ctxB.enter_context(nc.psum_tensor(f"ps_tr{i}", [128, 128], BF16))
                 for i in range(2)]
        ps_h = [ctxB.enter_context(nc.psum_tensor(f"ps_h{i}", [128, 512], F32))
                for i in range(2)]
        ps_zz = [ctxB.enter_context(nc.psum_tensor(f"ps_zz{i}", [128, 512], F32))
                 for i in range(2)]
        ps_bt = ctxB.enter_context(nc.psum_tensor("ps_bt", [128, 128], F32))
        with ctxB:
            with nc.Block() as block2:
                @block2.gpsimd
                def _(gpsimd):
                    nf = 0
                    nf2 = 0
                    for k in range(5):
                        for m in range(4):
                            g = 4 * k + m
                            if g >= 2:
                                gpsimd.wait_ge(s_tr, 8 * (g - 1))
                            gpsimd.indirect_dma_start(
                                out=xtm_blk[m % 2][:], out_offset=None,
                                in_=d_sorted[:, :],
                                in_offset=bass.IndirectOffsetOnAxis(
                                    ap=idx_i[:, 200 + g:200 + g + 1],
                                    axis=0)).then_inc(s_tok, 16)
                            gpsimd.wait_ge(s_tok, 16 * (6 * k + m + 1))
                        if k >= 1:
                            gpsimd.wait_ge(s_bt, 2 * k)
                        gpsimd.indirect_dma_start(
                            out=b1g[:], out_offset=None, in_=b1v[:, :],
                            in_offset=bass.IndirectOffsetOnAxis(
                                ap=idx_i[:, 220 + k:220 + k + 1], axis=0),
                            bounds_check=256, oob_is_err=False).then_inc(s_tok, 16)
                        gpsimd.wait_ge(s_tok, 16 * (6 * k + 5))
                        gpsimd.indirect_dma_start(
                            out=b2g[:], out_offset=None, in_=b2v[:, :],
                            in_offset=bass.IndirectOffsetOnAxis(
                                ap=idx_i[:, 225 + k:225 + k + 1], axis=0),
                            bounds_check=64, oob_is_err=False).then_inc(s_tok, 16)
                        gpsimd.wait_ge(s_tok, 16 * (6 * k + 6))
                        for c8 in range(NCH):
                            wi = 40 * k + c8
                            if wi >= 2:
                                gpsimd.wait_ge(s_wf2, 16 * (wi - 1))
                            gpsimd.indirect_dma_start(
                                out=wtmp2[wi % 2][:, 0:4096], out_offset=None,
                                in_=w1_in[:, :],
                                in_offset=bass.IndirectOffsetOnAxis(
                                    ap=idx_i[:, 8 * k + c8:8 * k + c8 + 1],
                                    axis=0)).then_inc(s_wf, 16)
                        for j in range(32):
                            wi = 40 * k + 8 + j
                            if wi >= 2:
                                gpsimd.wait_ge(s_wf2, 16 * (wi - 1))
                            gpsimd.indirect_dma_start(
                                out=wtmp2[wi % 2][:, 0:1024], out_offset=None,
                                in_=w2_in[:, :],
                                in_offset=bass.IndirectOffsetOnAxis(
                                    ap=idx_i[:, 40 + 32 * k + j:
                                             40 + 32 * k + j + 1],
                                    axis=0)).then_inc(s_wf, 16)

                @block2.sync
                def _(sync):
                    for k in range(5):
                        for fc in range(32):
                            u = 32 * k + fc
                            if u >= 2:
                                k2, f2 = divmod(u - 2, 32)
                                sync.wait_ge(s_wc, 40 * k2 + f2 + 1)
                            sync.wait_ge(s_wf2, 16 * (40 * k + 8))
                            sync.dma_start(
                                out=w1stage[fc % 2][:].rearrange(
                                    "p (c f) -> p c f", f=128),
                                in_=d_w1.rearrange("(c p) f -> p c f", p=128)
                                [:, :, 128 * fc:128 * (fc + 1)]).then_inc(
                                    s_wsp[fc % 2], 16)
                        for dc in range(NCH):
                            if k > 0 or dc > 0:
                                sync.wait_ge(s_wc, 40 * k + 32 + dc)
                            sync.wait_ge(s_wf2, 16 * 40 * (k + 1))
                            sync.dma_start(
                                out=w2stage[:].rearrange("p (f w) -> p f w",
                                                         w=128),
                                in_=d_w2.rearrange("(f p) w -> p f w", p=128)
                                [:, :, 128 * dc:128 * (dc + 1)]).then_inc(s_ws2, 16)
                        for m in range(4):
                            sync.wait_ge(s_zev, 8 * (4 * k + m + 1))
                            sync.dma_start(
                                out=z_out[512 * k + 128 * m:
                                          512 * k + 128 * (m + 1), :],
                                in_=ztm_st[m % 2][:]).then_inc(s_st, 16)
                            sync.wait_ge(s_st, 16 * (4 * k + m + 1))
                    sync.wait_ge(s_st, 16 * 20)

                @block2.vector
                def _(vector):
                    for k in range(5):
                        for m in range(4):
                            for c8 in range(NCH):
                                t = 32 * k + 8 * m + c8
                                vector.wait_ge(s_tr, t + 1)
                                if k >= 1:
                                    vector.wait_ge(s_hm, 32 * k)
                                vector.tensor_copy(
                                    xfm_blk[c8][:, 128 * m:128 * (m + 1)],
                                    ps_tr[t % 2][:]).then_inc(s_xe, 1)
                        vector.wait_ge(s_bt, 2 * k + 1)
                        if k >= 1:
                            vector.wait_ge(s_ge, 32 * k)
                        vector.tensor_copy(b1_pc[:],
                                           ps_bt[:, 0:32]).then_inc(s_bte, 1)
                        vector.wait_ge(s_bt, 2 * k + 2)
                        if k >= 1:
                            vector.wait_ge(s_ev, 8 * k)
                        vector.tensor_copy(b2_pc[:],
                                           ps_bt[:, 0:8]).then_inc(s_bte, 1)
                        for fc in range(32):
                            u = 32 * k + fc
                            vector.wait_ge(s_wsp[fc % 2], 16 * (u // 2 + 1))
                            if u >= 2:
                                vector.wait_ge(s_hm, u - 1)
                            vector.tensor_copy(w1_bf[fc % 2][:],
                                               w1stage[fc % 2][:]).then_inc(s_wc, 1)
                        for dc in range(NCH):
                            vector.wait_ge(s_ws2, 16 * (8 * k + dc + 1))
                            if 8 * k + dc >= 1:
                                vector.wait_ge(s_z2, 8 * k + dc)
                            vector.tensor_copy(w2_bf[:],
                                               w2stage[:]).then_inc(s_wc, 1)
                        for m in range(4):
                            if 4 * k + m >= 2:
                                vector.wait_ge(s_st, 16 * (4 * k + m - 1))
                            for dc in range(NCH):
                                t = 32 * k + 8 * m + dc
                                vector.wait_ge(s_ztr, t + 1)
                                vector.tensor_copy(
                                    ztm_st[m % 2][:, 128 * dc:128 * (dc + 1)],
                                    ps_tr[t % 2][:]).then_inc(s_zev, 1)

                @block2.scalar
                def _(scalar):
                    for k in range(5):
                        if k >= 1:
                            scalar.wait_ge(s_wc, 40 * k)
                        for c8 in range(NCH):
                            wi = 40 * k + c8
                            scalar.wait_ge(s_wf, 16 * (wi + 1))
                            scalar.dma_start(
                                out=d_w1[128 * c8:128 * (c8 + 1), :],
                                in_=wtmp2[wi % 2][:, 0:4096]).then_inc(s_wf2, 16)
                        for j in range(32):
                            wi = 40 * k + 8 + j
                            scalar.wait_ge(s_wf, 16 * (wi + 1))
                            scalar.dma_start(
                                out=d_w2[128 * j:128 * (j + 1), :],
                                in_=wtmp2[wi % 2][:, 0:1024]).then_inc(s_wf2, 16)
                        for fc in range(32):
                            u = 32 * k + fc
                            scalar.wait_ge(s_hm, u + 1)
                            scalar.wait_ge(s_bte, 2 * k + 1)
                            if k >= 1:
                                scalar.wait_ge(s_z2, 8 * k)
                            scalar.activation(
                                h_sb[:, 512 * fc:512 * (fc + 1)],
                                ps_h[fc % 2][:], AF.Gelu,
                                bias=b1_pc[:, fc:fc + 1]).then_inc(s_ge, 1)
                        for dc in range(NCH):
                            scalar.wait_ge(s_z2, 8 * k + dc + 1)
                            scalar.wait_ge(s_bte, 2 * k + 2)
                            if k >= 1:
                                scalar.wait_ge(s_ztr, 32 * k)
                            scalar.activation(
                                z_bf[dc][:], ps_zz[dc % 2][:], AF.Identity,
                                bias=b2_pc[:, dc:dc + 1]).then_inc(s_ev, 1)

                @block2.tensor
                def _(tensor):
                    for k in range(5):
                        for m in range(4):
                            tensor.wait_ge(s_tok, 16 * (6 * k + m + 1))
                            if k >= 1:
                                tensor.wait_ge(s_zev, 32 * k)
                            for c8 in range(NCH):
                                t = 32 * k + 8 * m + c8
                                if t >= 2:
                                    tensor.wait_ge(s_xe, t - 1)
                                tensor.transpose(
                                    ps_tr[t % 2][:],
                                    xtm_blk[m % 2][:, 128 * c8:128 * (c8 + 1)],
                                    eye_bf[:]).then_inc(s_tr, 1)
                        tensor.wait_ge(s_tok, 16 * (6 * k + 6))
                        if k >= 1:
                            tensor.wait_ge(s_bte, 2 * k)
                        tensor.transpose(ps_bt[:], b1g[:],
                                         eye_sb[:]).then_inc(s_bt, 1)
                        tensor.wait_ge(s_bte, 2 * k + 1)
                        tensor.transpose(ps_bt[:], b2g[:],
                                         eye_sb[:]).then_inc(s_bt, 1)
                        for fc in range(32):
                            u = 32 * k + fc
                            tensor.wait_ge(s_wc, 40 * k + fc + 1)
                            tensor.wait_ge(s_xe, 32 * (k + 1))
                            if u >= 2:
                                tensor.wait_ge(s_ge, u - 1)
                            if k >= 1:
                                tensor.wait_ge(s_z2, 8 * k)
                            for c8 in range(NCH):
                                ins = tensor.matmul(
                                    ps_h[fc % 2][:],
                                    w1_bf[fc % 2][:, 128 * c8:128 * (c8 + 1)],
                                    xfm_blk[c8][:], start=(c8 == 0),
                                    stop=(c8 == NCH - 1))
                            ins.then_inc(s_hm, 1)
                        for dc in range(NCH):
                            zu = 8 * k + dc
                            tensor.wait_ge(s_wc, 40 * k + 32 + dc + 1)
                            tensor.wait_ge(s_ge, 32 * (k + 1))
                            if zu >= 2:
                                tensor.wait_ge(s_ev, zu - 1)
                            for fc in range(32):
                                ins = tensor.matmul(
                                    ps_zz[dc % 2][:],
                                    w2_bf[:, 128 * fc:128 * (fc + 1)],
                                    h_sb[:, 512 * fc:512 * (fc + 1)],
                                    start=(fc == 0), stop=(fc == 31))
                            ins.then_inc(s_z2, 1)
                        for m in range(4):
                            for dc in range(NCH):
                                t = 32 * k + 8 * m + dc
                                tensor.wait_ge(s_ev, 8 * k + dc + 1)
                                if t >= 2:
                                    tensor.wait_ge(s_zev, t - 1)
                                tensor.transpose(
                                    ps_tr[t % 2][:],
                                    z_bf[dc][:, 128 * m:128 * (m + 1)],
                                    eye_bf[:]).then_inc(s_ztr, 1)
    return nc



def build_moe_final():
    nc = bass.Bass("TRN2", target_bir_lowering=False, debug=False)
    z_in = nc.dram_tensor("z_all", [20480, DM], BF16, kind="ExternalInput").ap()
    pos_in = nc.dram_tensor("pos_own", [128, 16], F32, kind="ExternalInput").ap()
    sc_in = nc.dram_tensor("score", [128, 16], F32, kind="ExternalInput").ap()
    xtm_in = nc.dram_tensor("x3tm", [TPC, DM], BF16, kind="ExternalInput").ap()
    out_tm = nc.dram_tensor("out_tm", [TPC, DM], F32, kind="ExternalOutput").ap()

    ctx = ExitStack()
    sb, ps, sem = _mk(nc, ctx)
    with ctx:
        pos_sb = sb("pos_s", [128, 16], F32)
        pos_i = sb("pos_i", [128, 16], I32)
        sc_sb = sb("sc_s", [128, 16], F32)
        x3t = [sb(f"x3t{i}", [128, DM], BF16) for i in range(2)]
        zg = [sb(f"zg{i}", [128, DM], BF16) for i in range(2)]
        y_f = [sb(f"y_f{i}", [128, DM], F32) for i in range(2)]
        ysq = sb("ysq", [128, DM], F32)
        s1 = sb("s1", [128, 1], F32)
        s2 = sb("s2", [128, 1], F32)
        mu = sb("mu", [128, 1], F32)
        m2 = sb("m2", [128, 1], F32)
        mu2 = sb("mu2", [128, 1], F32)
        var = sb("var", [128, 1], F32)
        lnv = sb("lnv", [128, 1], F32)
        rstd = sb("rstd", [128, 1], F32)
        o_f = [sb(f"o_f{i}", [128, DM], F32) for i in range(2)]
        eps4 = sb("eps4", [128, 1], F32)

        s_ld = sem("s_ld"); s_gi = sem("s_gi"); s_zg = sem("s_zg")
        s_v = sem("s_v"); s_sc = sem("s_sc"); s_o = sem("s_o")
        s_st = sem("s_st"); s_xl = sem("s_xl")

        with nc.Block() as block:
            @block.sync
            def _(sync):
                for t, src_ in ((pos_sb, pos_in), (sc_sb, sc_in)):
                    sync.dma_start(out=t[:], in_=src_).then_inc(s_ld, 16)
                for j in range(18):
                    if j < 16:
                        if j >= 2:
                            sync.wait_ge(s_v, 8 * (j - 1))
                        sync.dma_start(out=x3t[j % 2][:],
                                       in_=xtm_in[128 * j:128 * (j + 1),
                                                  :]).then_inc(s_xl, 16)
                        sync.wait_ge(s_xl, 16 * (j + 1))
                    if j >= 2:
                        sync.wait_ge(s_v, 8 * (j - 2) + 7)
                        sync.dma_start(out=out_tm[128 * (j - 2):128 * (j - 1), :],
                                       in_=o_f[(j - 2) % 2][:]).then_inc(s_st, 16)
                        sync.wait_ge(s_st, 16 * (j - 1))
                sync.wait_ge(s_st, 16 * 16)

            @block.gpsimd
            def _(gpsimd):
                gpsimd.wait_ge(s_gi, 1)
                for j in range(16):
                    if j >= 2:
                        gpsimd.wait_ge(s_v, 8 * (j - 1) + 1)
                    gpsimd.indirect_dma_start(
                        out=zg[j % 2][:], out_offset=None,
                        in_=z_in[:, :],
                        in_offset=bass.IndirectOffsetOnAxis(
                            ap=pos_i[:, j:j + 1], axis=0)).then_inc(s_zg, 16)
                    gpsimd.wait_ge(s_zg, 16 * (j + 1))

            @block.vector
            def _(vector):
                vector.memset(eps4[:], 1e-5)
                vector.wait_ge(s_ld, 32)
                vector.tensor_copy(pos_i[:], pos_sb[:]).then_inc(s_gi, 1)
                nv = 0
                for j in range(16):
                    vector.wait_ge(s_zg, 16 * (j + 1))
                    vector.wait_ge(s_xl, 16 * (j + 1))
                    vector.scalar_tensor_tensor(
                        y_f[j % 2][:], zg[j % 2][:], sc_sb[:, j:j + 1],
                        x3t[j % 2][:], ALU.mult, ALU.add).then_inc(s_v, 1)
                    nv += 1
                    vector.wait_ge(s_v, nv)
                    vector.tensor_tensor(ysq[:], y_f[j % 2][:], y_f[j % 2][:],
                                         ALU.mult)
                    vector.reduce_sum(s1[:], y_f[j % 2][:],
                                      axis=AX.X).then_inc(s_v, 1)
                    nv += 1
                    vector.wait_ge(s_v, nv)
                    vector.reduce_sum(s2[:], ysq[:], axis=AX.X).then_inc(s_v, 1)
                    nv += 1
                    vector.wait_ge(s_v, nv)
                    vector.tensor_scalar(mu[:], s1[:], 1.0 / DM, None, ALU.mult)
                    vector.tensor_scalar(m2[:], s2[:], 1.0 / DM, None,
                                         ALU.mult).then_inc(s_v, 1)
                    nv += 1
                    vector.wait_ge(s_v, nv)
                    vector.tensor_tensor(mu2[:], mu[:], mu[:], ALU.mult).then_inc(s_v, 1)
                    nv += 1
                    vector.wait_ge(s_v, nv)
                    vector.tensor_tensor(var[:], m2[:], mu2[:],
                                         ALU.subtract).then_inc(s_v, 1)
                    nv += 1
                    if j >= 2:
                        vector.wait_ge(s_st, 16 * (j - 1))
                    vector.wait_ge(s_sc, j + 1)
                    vector.tensor_scalar(o_f[j % 2][:], y_f[j % 2][:],
                                         mu[:, 0:1], rstd[:, 0:1],
                                         ALU.subtract, ALU.mult).then_inc(s_v, 1)
                    nv += 1
                    vector.sem_inc(s_v, 1)
                    nv += 1

            @block.scalar
            def _(scalar):
                for j in range(16):
                    scalar.wait_ge(s_v, 8 * j + 6)
                    scalar.activation(lnv[:], var[:], AF.Ln,
                                      bias=eps4[:, 0:1])
                    scalar.activation(rstd[:], lnv[:], AF.Exp,
                                      scale=-0.5).then_inc(s_sc, 1)
    return nc



def _core_view(full):
    out = []
    for c in range(NCORE):
        b, cq = divmod(c, 4)
        out.append(np.ascontiguousarray(full[b, 2048 * cq:2048 * (cq + 1)]))
    return out


def _pc(v):
    return np.ascontiguousarray(np.asarray(v).reshape(8, 128).T)


def _asT(a):
    return np.ascontiguousarray(np.asarray(a).T)


def _l0attn_prep(inz):
    jj = np.arange(128)
    ii = np.arange(ITW)
    biasv = np.zeros((4, 128, NL0T), np.float32)
    maskt = np.zeros((4, 128, NL0M * ITW), BF)
    for t, (hp, h2, h, it, s) in enumerate(L0TILES):
        jpos = 128 * s + jj
        for cq in range(4):
            i0g = 2048 * cq + ITW * it
            if t in L0MASK:
                biasv[cq, :, t] = -SLOPE[h] * jpos
                mi = _L0MASK_ORD[t]
                m = (jpos[:, None] - i0g <= ii[None, :])
                maskt[cq, :, ITW * mi:ITW * (mi + 1)] = m.astype(BF)
            else:
                if 128 * s > i0g + ITW - 1:
                    biasv[cq, :, t] = -1e30
                else:
                    biasv[cq, :, t] = -SLOPE[h] * jpos
    sel = np.zeros((16, 8 * 128), np.float32)
    for hp in range(8):
        for m in range(128):
            sel[2 * hp + m // 64, 128 * hp + m] = 1.0
    return biasv, maskt, sel.astype(BF)


def _l0_windows(kfm_g, vtm_g):
    kw = np.zeros((128, KW_TOT), BF)
    for hp in range(8):
        kw[:, int(KOFF[hp]):int(KOFF[hp]) + JPAIR[hp]] = \
            kfm_g[128 * hp:128 * (hp + 1), 0:JPAIR[hp]]
    vw = np.zeros((128, VW_TOT), BF)
    for h in range(16):
        for s in range(SH[h]):
            slot = int(VOFF[h]) + s
            vw[:, 65 * slot:65 * slot + 64] = \
                vtm_g[128 * s:128 * (s + 1), 64 * h:64 * (h + 1)]
            vw[:, 65 * slot + 64] = 1.0
    return kw, vw


def _l1_windows(kf, vf, cq):
    base = 2048 * cq
    kwin = np.zeros((1024, 2560), BF)
    vwin = np.zeros((1024, 2560), BF)
    lo = base - 512
    s = max(0, lo)
    kwin[:, s - lo:] = kf[:, s:base + 2048]
    vwin[:, s - lo:] = vf[:, s:base + 2048]
    kbig = np.zeros((1024, 512 * NBIG), BF)
    vbig = np.zeros((1024, 512 * NBIG), BF)
    for slot, (d, i) in enumerate(BIGSLOTS):
        g0 = base + 512 * i - d
        gs = max(0, g0)
        if g0 + 512 > 0:
            kbig[:, 512 * slot + (gs - g0):512 * (slot + 1)] = kf[:, gs:g0 + 512]
            vbig[:, 512 * slot + (gs - g0):512 * (slot + 1)] = vf[:, gs:g0 + 512]
    return kwin, vwin, kbig, vbig


def _l1_selectors():
    selp = np.zeros((128, 522), np.float32)
    selp[0:64, 0] = 1.0
    selp[64:128, 1] = 1.0
    for bnk in range(4):
        nslots = 4 if bnk < 3 else 2
        for kk in range(nslots):
            selp[32 * kk + 0, 2 + 2 * bnk + 0] = 1.0
            selp[32 * kk + 1, 2 + 2 * bnk + 1] = 1.0
    for kk in range(4):
        for m in range(128):
            selp[32 * kk + (m // 64), 10 + 128 * kk + m] = 1.0
    e2t = np.zeros((2, 128), np.float32)
    for m in range(128):
        e2t[m // 64, m] = 1.0
    return selp.astype(BF), e2t.astype(BF)


def kernel(**inputs):
    inz = {k: np.asarray(v) for k, v in inputs.items()}
    R = list(range(NCORE))
    x = inz["L0_attn_x"]
    xc = _core_view(x)

    nc = build_l0qkv()
    im = [{"x_fm": _asT(xc[c]), "Wq": inz["L0_attn_Wq"],
           "Wk": inz["L0_attn_Wk"], "Wv": inz["L0_attn_Wv"]} for c in R]
    r1 = run_bass_kernel_spmd(nc, im, core_ids=R)

    biasv, maskt, selhp = _l0attn_prep(inz)
    bo_pc = _pc(inz["L0_attn_bo"])
    kv_b = []
    for b in range(2):
        kfm_g = np.concatenate([r1.results[4 * b + q]["k_fm"] for q in range(4)],
                               axis=1)
        vtm_g = np.concatenate([r1.results[4 * b + q]["v_tm"] for q in range(4)],
                               axis=0)
        kv_b.append(_l0_windows(kfm_g, vtm_g))
    nc = build_l0attn()
    im = []
    for c in R:
        b, cq = divmod(c, 4)
        im.append({"q_fm": r1.results[c]["q_fm"], "kwin": kv_b[b][0],
                   "vwin": kv_b[b][1],
                   "biasv": np.ascontiguousarray(biasv[cq]),
                   "maskt": np.ascontiguousarray(maskt[cq]),
                   "selhp": selhp, "Wo": inz["L0_attn_Wo"],
                   "bo_pc": bo_pc, "x_fm": _asT(xc[c])})
    r2 = run_bass_kernel_spmd(nc, im, core_ids=R)

    nc = build_l0ffn()
    im = []
    for c in R:
        b, cq = divmod(c, 4)
        if cq == 0:
            halo = np.zeros((128, 8), np.float32)
        else:
            halo = _pc(r2.results[c - 1]["x1_fm"][:, -1].astype(np.float32))
        im.append({"x1_fm": r2.results[c]["x1_fm"], "halo_pc": halo,
                   "mur_pc": _pc(inz["L0_ffn_mu_r"]),
                   "muk_pc": _pc(inz["L0_ffn_mu_k"]),
                   "WrT": _asT(inz["L0_ffn_W_r"]),
                   "WkT": _asT(inz["L0_ffn_W_k"]),
                   "WvT": _asT(inz["L0_ffn_W_v"])})
    r3 = run_bass_kernel_spmd(nc, im, core_ids=R)

    nc = build_l1qkv()
    im = [{"x2_fm": r3.results[c]["x2_fm"],
           "WqT": _asT(inz["L1_attn_Wq"]), "WkT": _asT(inz["L1_attn_Wk"]),
           "WvT": _asT(inz["L1_attn_Wv"]), "bq_pc": _pc(inz["L1_attn_bq"]),
           "bk_pc": _pc(inz["L1_attn_bk"]), "bv_pc": _pc(inz["L1_attn_bv"])}
          for c in R]
    r4 = run_bass_kernel_spmd(nc, im, core_ids=R)

    selp, e2t = _l1_selectors()
    wins = {}
    sv4_b = {}
    for b in range(2):
        kf = np.concatenate([r4.results[4 * b + q]["k_fm"] for q in range(4)],
                            axis=1)
        vf = np.concatenate([r4.results[4 * b + q]["v_fm"] for q in range(4)],
                            axis=1)
        for cq in range(4):
            wins[(b, cq)] = _l1_windows(kf, vf, cq)
        sv4 = np.zeros((128, 32), np.float32)
        for q in range(4):
            sv4[:, np.arange(8) * 4 + q] = r4.results[4 * b + q]["sv_p"]
        sv4_b[b] = sv4
    nc = build_l1attn()
    im = []
    for c in R:
        b, cq = divmod(c, 4)
        kwin, vwin, kbig, vbig = wins[(b, cq)]
        im.append({"x2_fm": r3.results[c]["x2_fm"],
                   "q_fm": r4.results[c]["q_fm"], "kwin": kwin, "vwin": vwin,
                   "kbig": kbig, "vbig": vbig, "sv4": sv4_b[b], "selp": selp,
                   "e2t": e2t, "WoT": _asT(inz["L1_attn_Wo"]),
                   "bo_pc": _pc(inz["L1_attn_bo"])})
    r5 = run_bass_kernel_spmd(nc, im, core_ids=R)

    nc = build_moe_gate()
    iota8 = np.broadcast_to(np.arange(8, dtype=np.float32)[None, :],
                            (128, 8)).copy()
    eye = np.eye(128, dtype=np.float32)
    im = [{"x3_fm": r5.results[c]["x3_fm"], "w_red": inz["L1_ffn_w_red"],
           "wg": inz["L1_ffn_wg"], "iota8": iota8, "eye": eye} for c in R]
    r6a = run_bass_kernel_spmd(nc, im, core_ids=R)

    eidx_all = np.concatenate(
        [np.ascontiguousarray(r6a.results[c]["eidx"].T) for c in R],
        axis=0)
    x3tm_all = np.concatenate([r6a.results[c]["x3tm"] for c in R], axis=0)
    tri = np.triu(np.ones((128, 128), np.float32), 1)
    thr = np.broadcast_to((512.0 * np.arange(32, dtype=np.float32))[None, :],
                          (8, 32)).copy()
    slotbase = (512 * (np.arange(20) // 4) +
                128 * (np.arange(20) % 4)).astype(np.float32)[None, :]
    c8b = (128.0 * np.arange(8, dtype=np.float32))[None, :]
    j32b = (128.0 * np.arange(32, dtype=np.float32))[None, :]
    iseg = np.zeros((128, 3), np.float32)
    iseg[:, 0] = np.arange(128)
    iseg[:, 1] = np.where(np.arange(128) < 32, np.arange(128), 1e6)
    iseg[:, 2] = np.where(np.arange(128) < 8, np.arange(128), 1e6)
    w1f = np.ascontiguousarray(inz["L1_ffn_w1"].reshape(8192, 4096))
    w2f = np.ascontiguousarray(inz["L1_ffn_w2"].reshape(32768, 1024))
    nc = build_moe_expert()
    im = []
    for c in R:
        im.append({"eidx_all": eidx_all, "x3tm_all": x3tm_all, "w1f": w1f,
                   "w2f": w2f, "b1f": inz["L1_ffn_b1"], "b2f": inz["L1_ffn_b2"],
                   "rankf": np.array([[float(c)]], np.float32), "tri128": tri,
                   "thr": thr, "eye": eye, "slotbase": slotbase,
                   "c8base128": c8b, "j32base128": j32b, "iotaseg": iseg})
    r6b = run_bass_kernel_spmd(nc, im, core_ids=R)

    z_all = np.concatenate([r6b.results[c]["z_slice"] for c in R], axis=0)
    nc = build_moe_final()
    im = []
    for c in R:
        posT = r6b.results[c]["posT"]
        im.append({"z_all": z_all,
                   "pos_own": np.ascontiguousarray(posT[:, 16 * c:16 * (c + 1)]),
                   "score": r6a.results[c]["score"],
                   "x3tm": r6a.results[c]["x3tm"]})
    r6c = run_bass_kernel_spmd(nc, im, core_ids=R)

    out = np.empty((2, NSEQ, DM), np.float32)
    for c in R:
        b, cq = divmod(c, 4)
        out[b, 2048 * cq:2048 * (cq + 1)] = r6c.results[c]["out_tm"]
    return out


# version 05
# speedup vs baseline: 1.6947x; 1.0784x over previous
import math
from contextlib import ExitStack

import numpy as np
import ml_dtypes

import concourse.bass as bass
import concourse.mybir as mybir
from concourse.bass_utils import run_bass_kernel_spmd

F32 = mybir.dt.float32
BF16 = mybir.dt.bfloat16
I32 = mybir.dt.int32
AF = mybir.ActivationFunctionType
ALU = mybir.AluOpType
AX = mybir.AxisListType
BF = ml_dtypes.bfloat16

B, NSEQ, DM, NH, DH = 2, 8192, 1024, 16, 64
TPC = 2048
NCH = DM // 128
NCORE = 8
ITW = 1024
NIT = TPC // ITW

SLOPE = [2.0 ** (-0.5 * (h + 1)) for h in range(NH)]
THRESH = 30.0
JH = [min(NSEQ, int(math.ceil(THRESH / SLOPE[h] / 128)) * 128) for h in range(NH)]
SH = [JH[h] // 128 for h in range(NH)]
JPAIR = [max(JH[2 * p], JH[2 * p + 1]) for p in range(8)]
KOFF = np.concatenate([[0], np.cumsum(JPAIR)]).astype(int)
KW_TOT = int(KOFF[-1])
VOFF = np.concatenate([[0], np.cumsum(SH)]).astype(int)
NSLOT = int(VOFF[-1])
VW_TOT = 65 * NSLOT

L0TILES = []
for _hp in range(8):
    for _h2 in range(2):
        _h = 2 * _hp + _h2
        for _it in range(NIT):
            for _s in range(SH[_h]):
                L0TILES.append((_hp, _h2, _h, _it, _s))
NL0T = len(L0TILES)


def _is_partial(it, s):
    for cq in range(4):
        i0g = 2048 * cq + ITW * it
        if i0g - 127 <= 128 * s <= i0g + ITW - 1:
            return True
    return False

L0MASK = {}
for _t, (_hp, _h2, _h, _it, _s) in enumerate(L0TILES):
    if _is_partial(_it, _s):
        L0MASK[_t] = len(L0MASK)
NL0M = len(L0MASK)



def _mk(nc, ctx):
    sb = lambda name, shape, dt=F32: ctx.enter_context(nc.sbuf_tensor(name, shape, dt))
    ps = lambda name, shape, dt=F32: ctx.enter_context(nc.psum_tensor(name, shape, dt))
    sem = lambda name: ctx.enter_context(nc.semaphore(name))
    return sb, ps, sem



def build_l0qkv():
    nc = bass.Bass("TRN2", target_bir_lowering=False, debug=False)
    x_in = nc.dram_tensor("x_fm", [DM, TPC], F32, kind="ExternalInput").ap()
    w_in = {n: nc.dram_tensor(n, [DM, DM], F32, kind="ExternalInput").ap()
            for n in ("Wq", "Wk", "Wv")}
    q_out = nc.dram_tensor("q_fm", [DM, TPC], BF16, kind="ExternalOutput").ap()
    k_out = nc.dram_tensor("k_fm", [DM, TPC], BF16, kind="ExternalOutput").ap()
    v_out = nc.dram_tensor("v_tm", [TPC, DM], BF16, kind="ExternalOutput").ap()

    ctx = ExitStack()
    sb, ps, sem = _mk(nc, ctx)
    with ctx:
        xs = sb("xs", [128, TPC], F32)
        x_bf = [sb(f"x_bf{c}", [128, TPC], BF16) for c in range(NCH)]
        wstage = sb("wstage", [128, DM], F32)
        w_bf = {n: [sb(f"{n}_bf{c}", [128, DM], BF16) for c in range(NCH)]
                for n in ("Wq", "Wk", "Wv")}
        evac = [sb(f"evac{i}", [128, TPC], BF16) for i in range(2)]
        pst = [ps(f"pst{i}", [128, 512]) for i in range(4)]

        s_ld = sem("s_ld")
        s_ldw = sem("s_ldw")
        s_cast = sem("s_cast")
        s_mm = sem("s_mm")
        s_ev = sem("s_ev")
        s_st = sem("s_st")

        with nc.Block() as block:
            @block.sync
            def _(sync):
                nld = 0
                for c in range(NCH):
                    if c > 0:
                        sync.wait_ge(s_cast, c)
                    sync.dma_start(out=xs[:],
                                   in_=x_in[128 * c:128 * (c + 1), :]).then_inc(s_ld, 16)
                    nld += 16
                i = 0
                for n in ("Wq", "Wk", "Wv"):
                    for c in range(NCH):
                        sync.wait_ge(s_cast, NCH + i)
                        sync.dma_start(
                            out=wstage[:],
                            in_=w_in[n][128 * c:128 * (c + 1), :]).then_inc(s_ldw, 16)
                        i += 1
                g = 0
                for dst in (q_out, k_out):
                    for hp in range(NCH):
                        g += 1
                        sync.wait_ge(s_ev, g)
                        sync.dma_start(out=dst[128 * hp:128 * (hp + 1), :],
                                       in_=evac[(g - 1) % 2][:]).then_inc(s_st, 16)
                        sync.wait_ge(s_st, 16 * g)
                for jt in range(TPC // 128):
                    g += 1
                    sync.wait_ge(s_ev, g)
                    sync.dma_start(out=v_out[128 * jt:128 * (jt + 1), :],
                                   in_=evac[(g - 1) % 2][:, 0:DM]).then_inc(s_st, 16)
                    sync.wait_ge(s_st, 16 * g)

            @block.vector
            def _(vector):
                for c in range(NCH):
                    vector.wait_ge(s_ld, 16 * (c + 1))
                    vector.tensor_copy(x_bf[c][:], xs[:]).then_inc(s_cast, 1)
                i = 0
                for n in ("Wq", "Wk", "Wv"):
                    for c in range(NCH):
                        vector.wait_ge(s_ldw, 16 * (i + 1))
                        vector.tensor_copy(w_bf[n][c][:],
                                           wstage[:]).then_inc(s_cast, 1)
                        i += 1
                g = 0
                for pi in range(2):
                    for hp in range(NCH):
                        g += 1
                        vector.wait_ge(s_mm, g)
                        if g > 2:
                            vector.wait_ge(s_st, 16 * (g - 2))
                        for its in range(4):
                            ins = vector.tensor_copy(
                                evac[(g - 1) % 2][:, 512 * its:512 * (its + 1)],
                                pst[its][:])
                        ins.then_inc(s_ev, 1)
                for jt in range(TPC // 128):
                    g += 1
                    vector.wait_ge(s_mm, g)
                    if g > 2:
                        vector.wait_ge(s_st, 16 * (g - 2))
                    for dc in range(2):
                        ins = vector.tensor_copy(
                            evac[(g - 1) % 2][:, 512 * dc:512 * (dc + 1)],
                            pst[dc][:])
                    ins.then_inc(s_ev, 1)

            @block.tensor
            def _(tensor):
                tensor.wait_ge(s_cast, NCH + 24)
                g = 0
                for wn in ("Wq", "Wk"):
                    for hp in range(NCH):
                        g += 1
                        if g > 1:
                            tensor.wait_ge(s_ev, g - 1)
                        for its in range(4):
                            for c in range(NCH):
                                ins = tensor.matmul(
                                    pst[its][:],
                                    w_bf[wn][c][:, 128 * hp:128 * (hp + 1)],
                                    x_bf[c][:, 512 * its:512 * (its + 1)],
                                    start=(c == 0), stop=(c == NCH - 1))
                        ins.then_inc(s_mm, 1)
                for jt in range(TPC // 128):
                    g += 1
                    tensor.wait_ge(s_ev, g - 1)
                    for dc in range(2):
                        for c in range(NCH):
                            ins = tensor.matmul(
                                pst[dc][:],
                                x_bf[c][:, 128 * jt:128 * (jt + 1)],
                                w_bf["Wv"][c][:, 512 * dc:512 * (dc + 1)],
                                start=(c == 0), stop=(c == NCH - 1))
                    ins.then_inc(s_mm, 1)
    return nc



L0GROUPS = []
_cum = 0
for _hp in range(8):
    for _h2 in range(2):
        _h = 2 * _hp + _h2
        for _it in range(NIT):
            _cum += SH[_h]
            L0GROUPS.append((_hp, _h2, _h, _it, _cum - 1))
NL0G = len(L0GROUPS)
_L0MASK_ORD = {t: i for i, t in enumerate(sorted(L0MASK))}


def build_l0attn():
    nc = bass.Bass("TRN2", target_bir_lowering=False, debug=False)
    q_in = nc.dram_tensor("q_fm", [DM, TPC], BF16, kind="ExternalInput").ap()
    kwin_in = nc.dram_tensor("kwin", [128, KW_TOT], BF16, kind="ExternalInput").ap()
    vwin_in = nc.dram_tensor("vwin", [128, VW_TOT], BF16, kind="ExternalInput").ap()
    biasv_in = nc.dram_tensor("biasv", [128, NL0T], F32, kind="ExternalInput").ap()
    maskt_in = nc.dram_tensor("maskt", [128, NL0M * ITW], BF16,
                              kind="ExternalInput").ap()
    selhp_in = nc.dram_tensor("selhp", [16, 1024], BF16, kind="ExternalInput").ap()
    wo_in = nc.dram_tensor("Wo", [DM, DM], F32, kind="ExternalInput").ap()
    bo_in = nc.dram_tensor("bo_pc", [128, 8], F32, kind="ExternalInput").ap()
    x_in = nc.dram_tensor("x_fm", [DM, TPC], F32, kind="ExternalInput").ap()
    x1_out = nc.dram_tensor("x1_fm", [DM, TPC], BF16, kind="ExternalOutput").ap()
    d_l = nc.dram_tensor("d_l", [NL0G, ITW], F32, kind="Internal").ap()

    NPB = 2
    ctx = ExitStack()
    sb, ps, sem = _mk(nc, ctx)
    with ctx:
        q_sb = [sb(f"q{c}", [128, TPC], BF16) for c in range(NCH)]
        attn_sb = [sb(f"attn{c}", [128, TPC], BF16) for c in range(NCH)]
        kwin_sb = sb("kwin_s", [128, KW_TOT], BF16)
        vwin_sb = sb("vwin_s", [128, VW_TOT], BF16)
        biasv_sb = sb("biasv_s", [128, NL0T], F32)
        selhp_sb = sb("selhp_s", [16, 1024], BF16)
        bo_sb = sb("bo", [128, 8], F32)
        p_sb = [sb(f"p{i}", [128, ITW], BF16) for i in range(NPB)]
        pm_sb = [sb(f"pm{i}", [128, ITW], BF16) for i in range(2)]
        msk_sb = [sb(f"msk{i}", [128, ITW], BF16) for i in range(2)]
        l_stage = [sb(f"lst{i}", [1, ITW], F32) for i in range(2)]
        l_pm = sb("l_pm", [16, TPC], F32)
        r_bf = sb("r_bf", [16, TPC], BF16)
        r_f32 = sb("r_f32", [16, TPC], F32)
        wo_stage = sb("wo_stage", [128, DM], F32)
        wo_bf = [sb(f"wo_bf{i}", [128, DM], BF16) for i in range(2)]
        x_stage = [sb("xs0", [128, TPC], F32)]
        ones_col = sb("ones_col", [128, 1], BF16)
        ones_row = sb("ones_row", [1, 128], BF16)
        ysq_sb = [sb(f"ysq{i}", [128, 512], BF16) for i in range(2)]
        mu_f = sb("mu_f", [1, 512], F32)
        m2_f = sb("m2_f", [1, 512], F32)
        t_f = sb("t_f", [1, 512], F32)
        var_f = sb("var_f", [1, 512], F32)
        lnv_f = sb("lnv_f", [1, 512], F32)
        rstd_f = sb("rstd_f", [1, 512], F32)
        mu_bf = sb("mu_bf", [1, 512], BF16)
        rstd_bf = sb("rstd_bf", [1, 512], BF16)
        tnorm = sb("tnorm", [128, 512], BF16)
        eps_sb = sb("eps_sb", [1, 1], F32)

        s_ld = sem("s_ld"); s_mm = sem("s_mm"); s_exp = sem("s_exp")
        s_msk = sem("s_msk"); s_av = sem("s_av"); s_ev = sem("s_ev")
        s_ldma = sem("s_ldma"); s_r = sem("s_r"); s_bc = sem("s_bc")
        s_n = sem("s_n"); s_wo = sem("s_wo"); s_xl = sem("s_xl")
        s_o = sem("s_o"); s_y = sem("s_y"); s_sq = sem("s_sq")
        s_s = sem("s_s"); s_stat = sem("s_stat"); s_bcln = sem("s_bcln")
        s_x1 = sem("s_x1"); s_st = sem("s_st")
        s_v = sem("s_v"); s_mld = sem("s_mld"); s_scs = sem("s_scs")
        s_s2 = sem("s_s2")

        NLD0 = 8 + 5

        mask_list = sorted(L0MASK)

        ctxA = ExitStack()
        ps_sc = [ctxA.enter_context(nc.psum_tensor(f"ps_sc{i}", [128, ITW], F32))
                 for i in range(2)]
        ps_av = [ctxA.enter_context(nc.psum_tensor(f"ps_av{i}", [65, ITW], F32))
                 for i in range(2)]
        with ctxA:
            with nc.Block() as block:
                @block.sync
                def _(sync):
                    for c in range(NCH):
                        sync.dma_start(out=q_sb[c][:],
                                       in_=q_in[128 * c:128 * (c + 1), :]).then_inc(s_ld, 16)
                    for dst, src in ((kwin_sb, kwin_in), (vwin_sb, vwin_in),
                                     (biasv_sb, biasv_in), (selhp_sb, selhp_in),
                                     (bo_sb, bo_in)):
                        sync.dma_start(out=dst[:], in_=src).then_inc(s_ld, 16)
                    for g, (hp, h2, h, it, endt) in enumerate(L0GROUPS):
                        sync.wait_ge(s_ev, g + 1)
                        sync.dma_start(out=d_l[g:g + 1, :],
                                       in_=l_stage[g % 2][:]).then_inc(s_ldma, 16)
                        sync.wait_ge(s_ldma, 16 * (g + 1))
                    sync.wait_ge(s_ldma, 16 * NL0G)
                    sync.dma_start(
                        out=l_pm[:],
                        in_=d_l.rearrange("(h it) w -> h it w", it=NIT)).then_inc(
                            s_ldma, 16)

                @block.tensor
                def _(tensor):
                    tensor.wait_ge(s_ld, 16 * NLD0)
                    gidx = -1
                    for t, (hp, h2, h, it, s) in enumerate(L0TILES):
                        first = (s == 0); last = (s == SH[h] - 1)
                        if first:
                            gidx += 1
                        scb = t % 2
                        if t >= 2:
                            tensor.wait_ge(s_exp, t - 1)
                        lk = kwin_sb[64 * h2:64 * h2 + 64,
                                     int(KOFF[hp]) + 128 * s:int(KOFF[hp]) + 128 * (s + 1)]
                        rq = q_sb[hp]
                        tensor.matmul(ps_sc[scb][:, 0:512], lk,
                                      rq[64 * h2:64 * h2 + 64, ITW * it:ITW * it + 512])
                        tensor.matmul(ps_sc[scb][:, 512:1024], lk,
                                      rq[64 * h2:64 * h2 + 64,
                                         ITW * it + 512:ITW * it + 1024]).then_inc(s_mm, 1)
                        if first and gidx >= 2:
                            tensor.wait_ge(s_ev, gidx - 1)
                        if t in L0MASK:
                            tensor.wait_ge(s_msk, _L0MASK_ORD[t] + 1)
                            rhs_p = pm_sb[_L0MASK_ORD[t] % 2]
                        else:
                            tensor.wait_ge(s_exp, t + 1)
                            rhs_p = p_sb[t % NPB]
                        slot = int(VOFF[h]) + s
                        lv = vwin_sb[:, 65 * slot:65 * slot + 65]
                        tensor.matmul(ps_av[gidx % 2][0:65, 0:512], lv,
                                      rhs_p[:, 0:512], start=first, stop=last)
                        tensor.matmul(ps_av[gidx % 2][0:65, 512:1024], lv,
                                      rhs_p[:, 512:1024], start=first,
                                      stop=last).then_inc(s_av, 1)

                @block.scalar
                def _(scalar):
                    for t, (hp, h2, h, it, s) in enumerate(L0TILES):
                        scalar.wait_ge(s_mm, t + 1)
                        if t >= NPB:
                            scalar.wait_ge(s_av, t - NPB + 1)
                        scalar.activation(p_sb[t % NPB][:], ps_sc[t % 2][:],
                                          AF.Exp, bias=biasv_sb[:, t:t + 1],
                                          scale=0.125).then_inc(s_exp, 1)

                @block.gpsimd
                def _(gpsimd):
                    for mi, t in enumerate(mask_list):
                        if mi >= 2:
                            gpsimd.wait_ge(s_msk, mi - 1)
                        gpsimd.dma_start(
                            out=msk_sb[mi % 2][:],
                            in_=maskt_in[:, ITW * mi:ITW * (mi + 1)]).then_inc(
                                s_mld, 16)

                @block.vector
                def _(vector):
                    end2g = {endt: (g, hp, h2, h, it)
                             for g, (hp, h2, h, it, endt) in enumerate(L0GROUPS)}
                    for t in range(NL0T):
                        if t in L0MASK:
                            mi = _L0MASK_ORD[t]
                            vector.wait_ge(s_exp, t + 1)
                            vector.wait_ge(s_mld, 16 * (mi + 1))
                            if mi >= 2:
                                vector.wait_ge(s_av, mask_list[mi - 2] + 1)
                            vector.tensor_tensor(pm_sb[mi % 2][:],
                                                 p_sb[t % NPB][:],
                                                 msk_sb[mi % 2][:],
                                                 ALU.mult).then_inc(s_msk, 1)
                        if t in end2g:
                            g, hp, h2, h, it = end2g[t]
                            vector.wait_ge(s_av, t + 1)
                            if g >= 2:
                                vector.wait_ge(s_ldma, 16 * g)
                            vector.tensor_copy(l_stage[g % 2][:],
                                               ps_av[g % 2][64:65, :])
                            vector.tensor_copy(
                                attn_sb[hp][64 * h2:64 * h2 + 64,
                                            ITW * it:ITW * (it + 1)],
                                ps_av[g % 2][0:64, :]).then_inc(s_ev, 1)
                    vector.wait_ge(s_ldma, 16 * (NL0G + 1))
                    vector.reciprocal(r_f32[:], l_pm[:]).then_inc(s_v, 1)
                    vector.wait_ge(s_v, 1)
                    vector.tensor_copy(r_bf[:], r_f32[:]).then_inc(s_r, 1)

        ctxB = ExitStack()
        ps_br = [ctxB.enter_context(nc.psum_tensor(f"ps_br{i}", [128, 512], F32))
                 for i in range(2)]
        ps_o = [ctxB.enter_context(nc.psum_tensor(f"ps_o{i}", [128, 512], F32))
                for i in range(2)]
        ps_s1 = ctxB.enter_context(nc.psum_tensor("ps_s1", [1, 512], F32))
        ps_s2 = ctxB.enter_context(nc.psum_tensor("ps_s2", [1, 512], F32))
        ps_bm = ctxB.enter_context(nc.psum_tensor("ps_bm", [128, 512], F32))
        ps_brd = ctxB.enter_context(nc.psum_tensor("ps_brd", [128, 512], F32))
        with ctxB:
            with nc.Block() as block2:
                @block2.sync
                def _(sync):
                    for dc in range(NCH):
                        if dc >= 1:
                            sync.wait_ge(s_wo, dc)
                        sync.dma_start(
                            out=wo_stage[:].rearrange("p (c w) -> p c w", w=128),
                            in_=wo_in.rearrange("(c p) w -> p c w", p=128)
                            [:, :, 128 * dc:128 * (dc + 1)]).then_inc(s_ld, 16)
                        if dc >= 1:
                            sync.wait_ge(s_y, 4 * dc)
                        sync.dma_start(out=x_stage[0][:],
                                       in_=x_in[128 * dc:128 * (dc + 1), :]).then_inc(s_xl, 16)
                    for c in range(NCH):
                        sync.wait_ge(s_x1, 3 * NCH + c + 1)
                        sync.dma_start(out=x1_out[128 * c:128 * (c + 1), :],
                                       in_=q_sb[c][:]).then_inc(s_st, 16)
                    sync.wait_ge(s_st, 16 * NCH)

                @block2.vector
                def _(vector):
                    vector.memset(ones_col[:], 1.0)
                    vector.memset(ones_row[:], 1.0)
                    vector.memset(eps_sb[:], 1e-5)
                    k = 0
                    for hp in range(NCH):
                        for its in range(4):
                            vector.wait_ge(s_bc, k + 1)
                            vector.tensor_tensor(
                                q_sb[hp][:, 512 * its:512 * (its + 1)],
                                attn_sb[hp][:, 512 * its:512 * (its + 1)],
                                ps_br[k % 2][:], ALU.mult).then_inc(s_n, 1)
                            k += 1
                    j = 0
                    for dc in range(NCH):
                        vector.wait_ge(s_ld, 16 * (NLD0 + dc + 1))
                        vector.tensor_copy(wo_bf[dc % 2][:],
                                           wo_stage[:]).then_inc(s_wo, 1)
                        vector.wait_ge(s_xl, 16 * (dc + 1))
                        for its in range(4):
                            vector.wait_ge(s_o, j + 1)
                            vector.scalar_tensor_tensor(
                                attn_sb[dc][:, 512 * its:512 * (its + 1)],
                                ps_o[j % 2][:], bo_sb[:, dc:dc + 1],
                                x_stage[0][:, 512 * its:512 * (its + 1)],
                                ALU.add, ALU.add).then_inc(s_y, 1)
                            j += 1
                    vector.wait_ge(s_y, 4 * NCH)
                    nv = 0
                    for its in range(4):
                        vector.wait_ge(s_s, its + 1)
                        vector.wait_ge(s_s2, NCH * (its + 1))
                        vector.tensor_scalar(mu_f[:], ps_s1[:], 1.0 / DM, None,
                                             ALU.mult)
                        vector.tensor_scalar(m2_f[:], ps_s2[:], 1.0 / DM, None,
                                             ALU.mult).then_inc(s_v, 1)
                        nv += 1
                        vector.wait_ge(s_v, nv)
                        vector.tensor_tensor(t_f[:], mu_f[:], mu_f[:],
                                             ALU.mult).then_inc(s_v, 1)
                        nv += 1
                        vector.wait_ge(s_v, nv)
                        vector.tensor_tensor(var_f[:], m2_f[:], t_f[:],
                                             ALU.subtract).then_inc(s_v, 1)
                        nv += 1
                        vector.tensor_copy(mu_bf[:], mu_f[:]).then_inc(s_stat, 1)
                        vector.wait_ge(s_stat, 3 * its + 2)
                        vector.tensor_copy(rstd_bf[:], rstd_f[:]).then_inc(s_stat, 1)
                        for c in range(NCH):
                            vector.wait_ge(s_bcln, 2 * (its + 1))
                            vector.tensor_tensor(
                                tnorm[:], attn_sb[c][:, 512 * its:512 * (its + 1)],
                                ps_bm[:], ALU.subtract).then_inc(s_v, 1)
                            nv += 1
                            vector.wait_ge(s_v, nv)
                            vector.tensor_tensor(
                                q_sb[c][:, 512 * its:512 * (its + 1)],
                                tnorm[:], ps_brd[:], ALU.mult).then_inc(s_x1, 1)

                @block2.scalar
                def _(scalar):
                    for its in range(4):
                        scalar.wait_ge(s_stat, 3 * its + 1)
                        scalar.activation(lnv_f[:], var_f[:], AF.Ln,
                                          bias=eps_sb[0:1, 0:1]).then_inc(s_scs, 1)
                        scalar.wait_ge(s_scs, its + 1)
                        scalar.activation(rstd_f[:], lnv_f[:], AF.Exp,
                                          scale=-0.5).then_inc(s_stat, 1)

                @block2.gpsimd
                def _(gpsimd):
                    gq = 0
                    for its in range(4):
                        for c in range(NCH):
                            gpsimd.wait_ge(s_y, 4 * c + its + 1)
                            if gq >= 2:
                                gpsimd.wait_ge(s_s2, gq - 1)
                            gpsimd.tensor_tensor(
                                ysq_sb[gq % 2][:],
                                attn_sb[c][:, 512 * its:512 * (its + 1)],
                                attn_sb[c][:, 512 * its:512 * (its + 1)],
                                ALU.mult).then_inc(s_sq, 1)
                            gq += 1

                @block2.tensor
                def _(tensor):
                    tensor.wait_ge(s_r, 1)
                    k = 0
                    for hp in range(NCH):
                        for its in range(4):
                            if k >= 2:
                                tensor.wait_ge(s_n, k - 1)
                            tensor.matmul(ps_br[k % 2][:],
                                          selhp_sb[0:16, 128 * hp:128 * (hp + 1)],
                                          r_bf[0:16, 512 * its:512 * (its + 1)],
                                          start=True, stop=True).then_inc(s_bc, 1)
                            k += 1
                    tensor.wait_ge(s_n, 4 * NCH)
                    j = 0
                    for dc in range(NCH):
                        tensor.wait_ge(s_wo, dc + 1)
                        for its in range(4):
                            if j >= 2:
                                tensor.wait_ge(s_y, j - 1)
                            for hp in range(NCH):
                                ins = tensor.matmul(
                                    ps_o[j % 2][:],
                                    wo_bf[dc % 2][:, 128 * hp:128 * (hp + 1)],
                                    q_sb[hp][:, 512 * its:512 * (its + 1)],
                                    start=(hp == 0), stop=(hp == NCH - 1))
                            ins.then_inc(s_o, 1)
                            j += 1
                    gq = 0
                    for its in range(4):
                        if its >= 1:
                            tensor.wait_ge(s_stat, 3 * its)
                        for c in range(NCH):
                            ins = tensor.matmul(ps_s1[:], ones_col[:],
                                                attn_sb[c][:, 512 * its:512 * (its + 1)],
                                                start=(c == 0), stop=(c == NCH - 1))
                        ins.then_inc(s_s, 1)
                        for c in range(NCH):
                            tensor.wait_ge(s_sq, gq + 1)
                            ins = tensor.matmul(ps_s2[:], ones_col[:],
                                                ysq_sb[gq % 2][:],
                                                start=(c == 0), stop=(c == NCH - 1))
                            ins.then_inc(s_s2, 1)
                            gq += 1
                        tensor.wait_ge(s_stat, 3 * its + 1)
                        if its >= 1:
                            tensor.wait_ge(s_x1, 8 * its)
                        tensor.matmul(ps_bm[:], ones_row[:], mu_bf[:],
                                      start=True, stop=True)
                        tensor.wait_ge(s_stat, 3 * its + 3)
                        tensor.matmul(ps_brd[:], ones_row[:], rstd_bf[:],
                                      start=True, stop=True).then_inc(s_bcln, 2)
    return nc





class LNEmit:
    def __init__(self, nc, ctx, ctxP, name):
        sb = lambda n, shape, dt=F32: ctx.enter_context(
            nc.sbuf_tensor(f"{name}_{n}", shape, dt))
        self.ps_s1 = ctxP.enter_context(nc.psum_tensor(f"{name}_ps_s1", [1, 512], F32))
        self.ps_s2 = ctxP.enter_context(nc.psum_tensor(f"{name}_ps_s2", [1, 512], F32))
        self.ps_bm = ctxP.enter_context(nc.psum_tensor(f"{name}_ps_bm", [128, 512], F32))
        self.ps_brd = ctxP.enter_context(nc.psum_tensor(f"{name}_ps_brd", [128, 512], F32))
        self.mu_f = sb("mu_f", [1, 512]); self.m2_f = sb("m2_f", [1, 512])
        self.t_f = sb("t_f", [1, 512]); self.var_f = sb("var_f", [1, 512])
        self.lnv_f = sb("lnv_f", [1, 512]); self.rstd_f = sb("rstd_f", [1, 512])
        self.mu_bf = sb("mu_bf", [1, 512], BF16)
        self.rstd_bf = sb("rstd_bf", [1, 512], BF16)
        self.tnorm = sb("tnorm", [128, 512], BF16)
        self.ysq = [sb(f"ysq{i}", [128, 512], BF16) for i in range(2)]
        self.ones_col = sb("ones_col", [128, 1], BF16)
        self.ones_row = sb("ones_row", [1, 128], BF16)
        self.eps = sb("eps", [1, 1], F32)
        sem = lambda n: ctx.enter_context(nc.semaphore(f"{name}_{n}"))
        self.s_sq = sem("sq"); self.s_s = sem("s"); self.s_s2 = sem("s2")
        self.s_stat = sem("stat"); self.s_bcln = sem("bcln")
        self.s_x1 = sem("x1"); self.s_v = sem("v"); self.s_scs = sem("scs")
        self.s_ini = sem("ini")
        self.nv = 0

    def vector_init(self, vector):
        vector.memset(self.ones_col[:], 1.0)
        vector.memset(self.ones_row[:], 1.0)
        vector.memset(self.eps[:], 1e-5).then_inc(self.s_ini, 1)

    def pool_ops(self, gpsimd, y_ap, wait_y):
        gq = 0
        for its in range(4):
            for c in range(NCH):
                wait_y(gpsimd, its, c)
                if gq >= 2:
                    gpsimd.wait_ge(self.s_s2, gq - 1)
                gpsimd.tensor_tensor(self.ysq[gq % 2][:], y_ap(c, its),
                                     y_ap(c, its), ALU.mult).then_inc(self.s_sq, 1)
                gq += 1

    def tensor_ops(self, tensor, y_ap, wait_y):
        gq = 0
        for its in range(4):
            if its >= 1:
                tensor.wait_ge(self.s_stat, 3 * its)
            for c in range(NCH):
                wait_y(tensor, its, c)
                ins = tensor.matmul(self.ps_s1[:], self.ones_col[:],
                                    y_ap(c, its),
                                    start=(c == 0), stop=(c == NCH - 1))
            ins.then_inc(self.s_s, 1)
            for c in range(NCH):
                tensor.wait_ge(self.s_sq, gq + 1)
                tensor.matmul(self.ps_s2[:], self.ones_col[:],
                              self.ysq[gq % 2][:],
                              start=(c == 0),
                              stop=(c == NCH - 1)).then_inc(self.s_s2, 1)
                gq += 1
            tensor.wait_ge(self.s_stat, 3 * its + 1)
            if its >= 1:
                tensor.wait_ge(self.s_x1, 8 * its)
            tensor.matmul(self.ps_bm[:], self.ones_row[:], self.mu_bf[:],
                          start=True, stop=True)
            tensor.wait_ge(self.s_stat, 3 * its + 3)
            tensor.matmul(self.ps_brd[:], self.ones_row[:], self.rstd_bf[:],
                          start=True, stop=True).then_inc(self.s_bcln, 2)

    def scalar_ops(self, scalar):
        scalar.wait_ge(self.s_ini, 1)
        for its in range(4):
            scalar.wait_ge(self.s_stat, 3 * its + 1)
            scalar.activation(self.lnv_f[:], self.var_f[:], AF.Ln,
                              bias=self.eps[0:1, 0:1]).then_inc(self.s_scs, 1)
            scalar.wait_ge(self.s_scs, its + 1)
            scalar.activation(self.rstd_f[:], self.lnv_f[:], AF.Exp,
                              scale=-0.5).then_inc(self.s_stat, 1)

    def vector_ops(self, vector, y_ap, out_ap):
        for its in range(4):
            vector.wait_ge(self.s_s, its + 1)
            vector.wait_ge(self.s_s2, NCH * (its + 1))
            vector.tensor_scalar(self.mu_f[:], self.ps_s1[:], 1.0 / DM, None,
                                 ALU.mult)
            vector.tensor_scalar(self.m2_f[:], self.ps_s2[:], 1.0 / DM, None,
                                 ALU.mult).then_inc(self.s_v, 1)
            self.nv += 1
            vector.wait_ge(self.s_v, self.nv)
            vector.tensor_tensor(self.t_f[:], self.mu_f[:], self.mu_f[:],
                                 ALU.mult).then_inc(self.s_v, 1)
            self.nv += 1
            vector.wait_ge(self.s_v, self.nv)
            vector.tensor_tensor(self.var_f[:], self.m2_f[:], self.t_f[:],
                                 ALU.subtract).then_inc(self.s_v, 1)
            self.nv += 1
            vector.tensor_copy(self.mu_bf[:], self.mu_f[:]).then_inc(self.s_stat, 1)
            vector.wait_ge(self.s_stat, 3 * its + 2)
            vector.tensor_copy(self.rstd_bf[:],
                               self.rstd_f[:]).then_inc(self.s_stat, 1)
            for c in range(NCH):
                vector.wait_ge(self.s_bcln, 2 * (its + 1))
                vector.tensor_tensor(self.tnorm[:], y_ap(c, its),
                                     self.ps_bm[:], ALU.subtract).then_inc(self.s_v, 1)
                self.nv += 1
                vector.wait_ge(self.s_v, self.nv)
                vector.tensor_tensor(out_ap(c, its), self.tnorm[:],
                                     self.ps_brd[:], ALU.mult).then_inc(self.s_x1, 1)



def build_l0ffn():
    nc = bass.Bass("TRN2", target_bir_lowering=False, debug=False)
    x1_in = nc.dram_tensor("x1_fm", [DM, TPC], BF16, kind="ExternalInput").ap()
    halo_in = nc.dram_tensor("halo_pc", [128, 8], F32, kind="ExternalInput").ap()
    mur_in = nc.dram_tensor("mur_pc", [128, 8], F32, kind="ExternalInput").ap()
    muk_in = nc.dram_tensor("muk_pc", [128, 8], F32, kind="ExternalInput").ap()
    w_in = {n: nc.dram_tensor(n, [DM, DM], F32, kind="ExternalInput").ap()
            for n in ("WrT", "WkT", "WvT")}
    x2_out = nc.dram_tensor("x2_fm", [DM, TPC], BF16, kind="ExternalOutput").ap()

    ctx = ExitStack()
    sb, ps, sem = _mk(nc, ctx)
    with ctx:
        x1_sb = [sb(f"x1_{c}", [128, TPC], BF16) for c in range(NCH)]
        xr_sb = [sb(f"xr_{c}", [128, TPC], BF16) for c in range(NCH)]
        xk_sb = [sb(f"xk_{c}", [128, TPC], BF16) for c in range(NCH)]
        ka2_sb = [sb(f"ka2_{c}", [128, TPC], BF16) for c in range(NCH)]
        wA = [sb(f"wA{c}", [128, DM], BF16) for c in range(NCH)]
        wB = [sb(f"wB{c}", [128, DM], BF16) for c in range(NCH)]
        wstage = sb("wstage", [128, DM], F32)
        halo_sb = sb("halo_s", [128, 8], F32)
        mur_sb = sb("mur_s", [128, 8], F32)
        muk_sb = sb("muk_s", [128, 8], F32)
        xprev = sb("xprev", [128, TPC], BF16)
        dtmp = sb("dtmp", [128, TPC], BF16)
        ka_t = [sb(f"ka_t{i}", [128, 512], BF16) for i in range(2)]
        r_t = [sb(f"r_t{i}", [128, 512], BF16) for i in range(2)]
        t_t = [sb(f"t_t{i}", [128, 512], BF16) for i in range(2)]

        s_ld = sem("s_ld"); s_cast = sem("s_cast"); s_x = sem("s_x")
        s_ldw3 = sem("s_ldw3")
        s_mmk = sem("s_mmk"); s_rl = sem("s_rl"); s_ka2 = sem("s_ka2")
        s_mm2 = sem("s_mm2"); s_sg = sem("s_sg"); s_ff = sem("s_ff")
        s_y = sem("s_y"); s_st = sem("s_st"); s_v2 = sem("s_v2")

        ctxA = ExitStack()
        ps_k = [ctxA.enter_context(nc.psum_tensor(f"ps_k{i}", [128, 512], F32))
                for i in range(2)]
        ps_kv = [ctxA.enter_context(nc.psum_tensor(f"ps_kv{i}", [128, 512], F32))
                 for i in range(2)]
        ps_r = [ctxA.enter_context(nc.psum_tensor(f"ps_r{i}", [128, 512], F32))
                for i in range(2)]
        with ctxA:
            with nc.Block() as block:
                @block.sync
                def _(sync):
                    for c in range(NCH):
                        sync.dma_start(out=x1_sb[c][:],
                                       in_=x1_in[128 * c:128 * (c + 1), :]).then_inc(s_ld, 16)
                    for t, src_ in ((halo_sb, halo_in), (mur_sb, mur_in),
                                    (muk_sb, muk_in)):
                        sync.dma_start(out=t[:], in_=src_).then_inc(s_ld, 16)
                    i = 0
                    for wn in ("WkT", "WrT", "WvT"):
                        for c in range(NCH):
                            if i > 0:
                                sync.wait_ge(s_cast, i)
                            if wn == "WrT" and c == 0:
                                sync.wait_ge(s_mmk, 32)
                            sync.dma_start(out=wstage[:],
                                           in_=w_in[wn][128 * c:128 * (c + 1),
                                                        :]).then_inc(s_ldw3, 16)
                            i += 1

                @block.vector
                def _(vector):
                    i = 0
                    for c in range(NCH):
                        vector.wait_ge(s_ldw3, 16 * (i + 1))
                        vector.tensor_copy(wA[c][:],
                                           wstage[:]).then_inc(s_cast, 1)
                        i += 1
                    nv = 0
                    vector.wait_ge(s_ld, 16 * 11)
                    for c in range(NCH):
                        vector.tensor_copy(xprev[:, 1:TPC],
                                           x1_sb[c][:, 0:TPC - 1])
                        vector.tensor_copy(xprev[:, 0:1],
                                           halo_sb[:, c:c + 1]).then_inc(s_v2, 1)
                        nv += 1
                        vector.wait_ge(s_v2, nv)
                        vector.tensor_tensor(dtmp[:], x1_sb[c][:], xprev[:],
                                             ALU.subtract).then_inc(s_v2, 1)
                        nv += 1
                        vector.wait_ge(s_v2, nv)
                        vector.scalar_tensor_tensor(
                            xr_sb[c][:], dtmp[:], mur_sb[:, c:c + 1], xprev[:],
                            ALU.mult, ALU.add)
                        vector.scalar_tensor_tensor(
                            xk_sb[c][:], dtmp[:], muk_sb[:, c:c + 1], xprev[:],
                            ALU.mult, ALU.add).then_inc(s_x, 1)
                    for wn, dst in (("WrT", wA), ("WvT", wB)):
                        for c in range(NCH):
                            vector.wait_ge(s_ldw3, 16 * (i + 1))
                            vector.tensor_copy(dst[c][:],
                                               wstage[:]).then_inc(s_cast, 1)
                            i += 1
                    j = 0
                    for hp in range(NCH):
                        for its in range(4):
                            vector.wait_ge(s_sg, j + 1)
                            vector.scalar_tensor_tensor(
                                t_t[j % 2][:], ps_kv[j % 2][:], 1.0,
                                r_t[j % 2][:], ALU.mult, ALU.mult).then_inc(s_ff, 1)
                            vector.wait_ge(s_ff, j + 1)
                            vector.tensor_tensor(
                                x1_sb[hp][:, 512 * its:512 * (its + 1)],
                                x1_sb[hp][:, 512 * its:512 * (its + 1)],
                                t_t[j % 2][:], ALU.add).then_inc(s_y, 1)
                            j += 1

                @block.tensor
                def _(tensor):
                    j = 0
                    for hp in range(NCH):
                        for its in range(4):
                            tensor.wait_ge(s_cast, 8)
                            tensor.wait_ge(s_x, NCH)
                            if j >= 2:
                                tensor.wait_ge(s_rl, j - 1)
                            for c in range(NCH):
                                ins = tensor.matmul(
                                    ps_k[j % 2][:],
                                    wA[c][:, 128 * hp:128 * (hp + 1)],
                                    xk_sb[c][:, 512 * its:512 * (its + 1)],
                                    start=(c == 0), stop=(c == NCH - 1))
                            ins.then_inc(s_mmk, 1)
                            j += 1
                    tensor.wait_ge(s_cast, 24)
                    tensor.wait_ge(s_ka2, 32)
                    j = 0
                    for hp in range(NCH):
                        for its in range(4):
                            if j >= 2:
                                tensor.wait_ge(s_ff, j - 1)
                            for c in range(NCH):
                                ins = tensor.matmul(
                                    ps_kv[j % 2][:],
                                    wB[c][:, 128 * hp:128 * (hp + 1)],
                                    ka2_sb[c][:, 512 * its:512 * (its + 1)],
                                    start=(c == 0), stop=(c == NCH - 1))
                            ins.then_inc(s_mm2, 1)
                            if j >= 2:
                                tensor.wait_ge(s_sg, j - 1)
                            for c in range(NCH):
                                ins = tensor.matmul(
                                    ps_r[j % 2][:],
                                    wA[c][:, 128 * hp:128 * (hp + 1)],
                                    xr_sb[c][:, 512 * its:512 * (its + 1)],
                                    start=(c == 0), stop=(c == NCH - 1))
                            ins.then_inc(s_mm2, 1)
                            j += 1

                @block.scalar
                def _(scalar):
                    for j in range(32):
                        scalar.wait_ge(s_mmk, j + 1)
                        if j >= 2:
                            scalar.wait_ge(s_ka2, j - 1)
                        scalar.activation(ka_t[j % 2][:], ps_k[j % 2][:],
                                          AF.Relu).then_inc(s_rl, 1)
                    for j in range(32):
                        scalar.wait_ge(s_mm2, 2 * j + 2)
                        if j >= 2:
                            scalar.wait_ge(s_ff, j - 1)
                        scalar.activation(r_t[j % 2][:], ps_r[j % 2][:],
                                          AF.Sigmoid).then_inc(s_sg, 1)

                @block.gpsimd
                def _(gpsimd):
                    for j in range(32):
                        hp, its = divmod(j, 4)
                        gpsimd.wait_ge(s_rl, j + 1)
                        gpsimd.tensor_tensor(
                            ka2_sb[hp][:, 512 * its:512 * (its + 1)],
                            ka_t[j % 2][:], ka_t[j % 2][:],
                            ALU.mult).then_inc(s_ka2, 1)

        ctxB = ExitStack()
        ln = LNEmit(nc, ctx, ctxB, "ln2")
        with ctxB:
            y_ap = lambda c, its: x1_sb[c][:, 512 * its:512 * (its + 1)]
            out_ap = lambda c, its: xr_sb[c][:, 512 * its:512 * (its + 1)]

            def wait_y(eng, its, c):
                eng.wait_ge(s_y, 4 * NCH)

            with nc.Block() as block2:
                @block2.sync
                def _(sync):
                    for c in range(NCH):
                        sync.wait_ge(ln.s_x1, 3 * NCH + c + 1)
                        sync.dma_start(out=x2_out[128 * c:128 * (c + 1), :],
                                       in_=xr_sb[c][:]).then_inc(s_st, 16)
                    sync.wait_ge(s_st, 16 * NCH)

                @block2.vector
                def _(vector):
                    ln.vector_init(vector)
                    ln.vector_ops(vector, y_ap, out_ap)

                @block2.scalar
                def _(scalar):
                    ln.scalar_ops(scalar)

                @block2.gpsimd
                def _(gpsimd):
                    ln.pool_ops(gpsimd, y_ap, wait_y)

                @block2.tensor
                def _(tensor):
                    tensor.wait_ge(ln.s_ini, 1)
                    ln.tensor_ops(tensor, y_ap, wait_y)
    return nc



def build_l1qkv():
    nc = bass.Bass("TRN2", target_bir_lowering=False, debug=False)
    x2_in = nc.dram_tensor("x2_fm", [DM, TPC], BF16, kind="ExternalInput").ap()
    w_in = {n: nc.dram_tensor(n, [DM, DM], F32, kind="ExternalInput").ap()
            for n in ("WqT", "WkT", "WvT")}
    b_in = {n: nc.dram_tensor(n, [128, 8], F32, kind="ExternalInput").ap()
            for n in ("bq_pc", "bk_pc", "bv_pc")}
    q_out = nc.dram_tensor("q_fm", [DM, TPC], BF16, kind="ExternalOutput").ap()
    k_out = nc.dram_tensor("k_fm", [DM, TPC], BF16, kind="ExternalOutput").ap()
    v_out = nc.dram_tensor("v_fm", [DM, TPC], BF16, kind="ExternalOutput").ap()
    sv_out = nc.dram_tensor("sv_p", [128, 8], F32, kind="ExternalOutput").ap()

    ctx = ExitStack()
    sb, ps, sem = _mk(nc, ctx)
    with ctx:
        x_sb = [sb(f"x_{c}", [128, TPC], BF16) for c in range(NCH)]
        wstage = sb("wstage", [128, DM], F32)
        w_bf = [sb(f"w_bf{c}", [128, DM], BF16) for c in range(NCH)]
        bias_sb = {n: sb(n + "_s", [128, 8], F32) for n in ("bq", "bk", "bv")}
        evac = [sb(f"evac{i}", [128, TPC], BF16) for i in range(2)]
        sv_sb = sb("sv_s", [128, 8], F32)
        pst = [ps(f"pst{i}", [128, 512]) for i in range(4)]

        s_ld = sem("s_ld"); s_cast = sem("s_cast"); s_mm = sem("s_mm")
        s_ev = sem("s_ev"); s_st = sem("s_st"); s_sv = sem("s_sv")
        s_ldw4 = sem("s_ldw4")

        wnames = ("WqT", "WkT", "WvT")
        bnames = ("bq", "bk", "bv")
        outs = (q_out, k_out, v_out)

        with nc.Block() as block:
            @block.sync
            def _(sync):
                for c in range(NCH):
                    sync.dma_start(out=x_sb[c][:],
                                   in_=x2_in[128 * c:128 * (c + 1), :]).then_inc(s_ld, 16)
                for n in bnames:
                    sync.dma_start(out=bias_sb[n][:],
                                   in_=b_in[n + "_pc"]).then_inc(s_ld, 16)
                i = 0
                for wi, wn in enumerate(wnames):
                    for c in range(NCH):
                        if i > 0:
                            sync.wait_ge(s_cast, i)
                        if c == 0 and wi > 0:
                            sync.wait_ge(s_mm, 8 * wi)
                        sync.dma_start(out=wstage[:],
                                       in_=w_in[wn][128 * c:128 * (c + 1),
                                                    :]).then_inc(s_ldw4, 16)
                        i += 1

            @block.gpsimd
            def _(gpsimd):
                g = 0
                for wi in range(3):
                    for hp in range(NCH):
                        g += 1
                        gpsimd.wait_ge(s_ev, g)
                        gpsimd.dma_start(out=outs[wi][128 * hp:128 * (hp + 1), :],
                                         in_=evac[(g - 1) % 2][:]).then_inc(s_st, 16)
                        gpsimd.wait_ge(s_st, 16 * g)
                gpsimd.wait_ge(s_sv, NCH)
                gpsimd.dma_start(out=sv_out, in_=sv_sb[:]).then_inc(s_st, 16)
                gpsimd.wait_ge(s_st, 16 * (g + 1))

            @block.vector
            def _(vector):
                i = 0
                for wi in range(3):
                    for c in range(NCH):
                        vector.wait_ge(s_ldw4, 16 * (i + 1))
                        vector.tensor_copy(w_bf[c][:],
                                           wstage[:]).then_inc(s_cast, 1)
                        i += 1
                for hp in range(NCH):
                    vector.wait_ge(s_ev, 16 + hp + 1)
                    vector.reduce_sum(sv_sb[:, hp:hp + 1],
                                      evac[(16 + hp) % 2][:],
                                      axis=AX.X).then_inc(s_sv, 1)

            @block.scalar
            def _(scalar):
                scalar.wait_ge(s_ld, 16 * 11)
                g = 0
                for wi in range(3):
                    for hp in range(NCH):
                        g += 1
                        scalar.wait_ge(s_mm, g)
                        if g > 2:
                            scalar.wait_ge(s_st, 16 * (g - 2))
                        for its in range(4):
                            ins = scalar.activation(
                                evac[(g - 1) % 2][:, 512 * its:512 * (its + 1)],
                                pst[its][:], AF.Identity,
                                bias=bias_sb[bnames[wi]][:, hp:hp + 1])
                        ins.then_inc(s_ev, 1)

            @block.tensor
            def _(tensor):
                tensor.wait_ge(s_ld, 16 * 11)
                g = 0
                for wi in range(3):
                    for hp in range(NCH):
                        tensor.wait_ge(s_cast, 8 * wi + 8)
                        g += 1
                        if g > 1:
                            tensor.wait_ge(s_ev, g - 1)
                        for its in range(4):
                            for c in range(NCH):
                                ins = tensor.matmul(
                                    pst[its][:],
                                    w_bf[c][:, 128 * hp:128 * (hp + 1)],
                                    x_sb[c][:, 512 * its:512 * (its + 1)],
                                    start=(c == 0), stop=(c == NCH - 1))
                        ins.then_inc(s_mm, 1)
    return nc



DELTAS = [0, 1, 2, 4, 8, 16, 32, 64, 128, 256, 512, 1024, 2048, 4096]
BIGSLOTS = [(d, i) for d in (1024, 2048, 4096) for i in range(4)
            if 512 + 512 * i - d < 0]
BIGIDX = {s: i for i, s in enumerate(BIGSLOTS)}
NBIG = len(BIGSLOTS)


def build_l1attn():
    nc = bass.Bass("TRN2", target_bir_lowering=False, debug=False)
    x2_in = nc.dram_tensor("x2_fm", [DM, TPC], BF16, kind="ExternalInput").ap()
    q_in = nc.dram_tensor("q_fm", [DM, TPC], BF16, kind="ExternalInput").ap()
    kwin_in = nc.dram_tensor("kwin", [DM, 2560], BF16, kind="ExternalInput").ap()
    vwin_in = nc.dram_tensor("vwin", [DM, 2560], BF16, kind="ExternalInput").ap()
    kbig_in = nc.dram_tensor("kbig", [DM, 512 * NBIG], BF16, kind="ExternalInput").ap()
    vbig_in = nc.dram_tensor("vbig", [DM, 512 * NBIG], BF16, kind="ExternalInput").ap()
    sv4_in = nc.dram_tensor("sv4", [128, 32], F32, kind="ExternalInput").ap()
    selp_in = nc.dram_tensor("selp", [128, 522], BF16, kind="ExternalInput").ap()
    e2t_in = nc.dram_tensor("e2t", [2, 128], BF16, kind="ExternalInput").ap()
    wo_in = nc.dram_tensor("WoT", [DM, DM], F32, kind="ExternalInput").ap()
    bo_in = nc.dram_tensor("bo_pc", [128, 8], F32, kind="ExternalInput").ap()
    x3_out = nc.dram_tensor("x3_fm", [DM, TPC], BF16, kind="ExternalOutput").ap()

    ctx = ExitStack()
    sb, ps, sem = _mk(nc, ctx)
    with ctx:
        x2_sb = [sb(f"x2_{c}", [128, TPC], BF16) for c in range(NCH)]
        q_sb = [sb(f"q_{c}", [128, TPC], BF16) for c in range(NCH)]
        kwin_sb = [sb(f"kw_{c}", [128, 2560], BF16) for c in range(NCH)]
        vwin_sb = [sb(f"vw_{c}", [128, 2560], BF16) for c in range(NCH)]
        kbig_t = [sb(f"kb_{i}", [128, 512], BF16) for i in range(3)]
        vbig_t = [sb(f"vb_{i}", [128, 512], BF16) for i in range(3)]
        sv4_sb = sb("sv4_s", [128, 32], F32)
        sv_sb = sb("sv_s", [128, 8], F32)
        selp_sb = sb("selp_s", [128, 522], BF16)
        e2t_sb = sb("e2t_s", [2, 128], BF16)
        wstage = sb("wstage", [128, DM], F32)
        w_bf = [sb(f"w_bf{c}", [128, DM], BF16) for c in range(NCH)]
        bo_sb = sb("bo_s", [128, 8], F32)
        p_t = [sb(f"p_t{i}", [128, 512], BF16) for i in range(2)]
        pexp_sb = [sb(f"pexp{i}", [128, 512], BF16) for i in range(4)]
        u_sb = [sb(f"u{i}", [128, 512], BF16) for i in range(4)]
        tz_f = sb("tz_f", [2, 512], F32)
        rz_f = sb("rz_f", [2, 512], F32)
        rz_bf = sb("rz_bf", [2, 512], BF16)
        t_av = [sb(f"t_av{i}", [128, 512], BF16) for i in range(2)]
        acc = [sb(f"acc{i}", [128, 512], BF16) for i in range(2)]

        s_ld = sem("s_ld"); s_cast = sem("s_cast"); s_pp = sem("s_pp")
        s_scm = sem("s_scm"); s_ex = sem("s_ex"); s_u = sem("s_u")
        s_z = sem("s_z"); s_bc = sem("s_bc"); s_avv = sem("s_avv")
        s_rz = sem("s_rz"); s_bcz = sem("s_bcz"); s_ep = sem("s_ep")
        s_kb = sem("s_kb"); s_vb = sem("s_vb"); s_v5 = sem("s_v5")
        s_o = sem("s_o"); s_y = sem("s_y"); s_st = sem("s_st")

        ITERS = [(c, i) for c in range(NCH) for i in range(4)]
        big_uses = []
        for n, (c8, its) in enumerate(ITERS):
            for di, d in enumerate(DELTAS):
                if (d, its) in BIGIDX:
                    big_uses.append((n, di))
        buse_ord = {u: i for i, u in enumerate(big_uses)}
        NLD0 = 8 + 8 + 8 + 8 + 5

        e2_ap = lambda: selp_sb[:, 0:2]
        zsel_ap = lambda b: selp_sb[:, 2 + 2 * b:2 + 2 * b + 2]
        selu_ap = lambda k: selp_sb[:, 10 + 128 * k:10 + 128 * (k + 1)]

        def ksrc(c8, its, di):
            d = DELTAS[di]
            col = 512 + 512 * its - d
            if col >= 0:
                return kwin_sb[c8][:, col:col + 512]
            return kbig_t[buse_ord[(ITERS.index((c8, its)), di)] % 3][:]

        def vsrc(c8, its, di):
            d = DELTAS[di]
            col = 512 + 512 * its - d
            if col >= 0:
                return vwin_sb[c8][:, col:col + 512]
            return vbig_t[buse_ord[(ITERS.index((c8, its)), di)] % 3][:]

        ctxA = ExitStack()
        ps_sc = [ctxA.enter_context(nc.psum_tensor(f"ps_sc{i}", [128, 512], F32))
                 for i in range(4)]
        ps_z = ctxA.enter_context(nc.psum_tensor("ps_z", [2, 512], F32))
        ps_u = [ctxA.enter_context(nc.psum_tensor(f"ps_u{i}", [128, 512], F32))
                for i in range(2)]
        ps_rz = ctxA.enter_context(nc.psum_tensor("ps_rz", [128, 512], F32))
        with ctxA:
            with nc.Block() as block:
                @block.sync
                def _(sync):
                    nld = 0
                    for arr, src_ in ((x2_sb, x2_in), (q_sb, q_in),
                                      (kwin_sb, kwin_in), (vwin_sb, vwin_in)):
                        for c in range(NCH):
                            sync.dma_start(out=arr[c][:],
                                           in_=src_[128 * c:128 * (c + 1), :]).then_inc(s_ld, 16)
                            nld += 1
                    for t, src_ in ((sv4_sb, sv4_in), (selp_sb, selp_in),
                                    (e2t_sb, e2t_in), (bo_sb, bo_in)):
                        sync.dma_start(out=t[:], in_=src_).then_inc(s_ld, 16)
                        nld += 1
                    for bu_i, (n, di) in enumerate(big_uses):
                        c8, its = ITERS[n]
                        d = DELTAS[di]
                        slot = BIGIDX[(d, its)]
                        if bu_i >= 3:
                            pn, pdi = big_uses[bu_i - 3]
                            sync.wait_ge(s_pp, 14 * pn + pdi + 1)
                            sync.wait_ge(s_avv, 14 * pn + pdi + 1)
                        sync.dma_start(
                            out=kbig_t[bu_i % 3][:],
                            in_=kbig_in[128 * c8:128 * (c8 + 1),
                                        512 * slot:512 * (slot + 1)]).then_inc(s_kb, 16)
                        sync.dma_start(
                            out=vbig_t[bu_i % 3][:],
                            in_=vbig_in[128 * c8:128 * (c8 + 1),
                                        512 * slot:512 * (slot + 1)]).then_inc(s_vb, 16)
                        sync.wait_ge(s_kb, 16 * (bu_i + 1))
                        sync.wait_ge(s_vb, 16 * (bu_i + 1))

                @block.gpsimd
                def _(gpsimd):
                    pass

                @block.tensor
                def _(tensor):
                    tensor.wait_ge(s_ld, 16 * 36)
                    tensor.wait_ge(s_v5, 4)
                    for n, (c8, its) in enumerate(ITERS):
                        if n >= 1:
                            tensor.wait_ge(s_ep, n)
                        for di, d in enumerate(DELTAS):
                            t = 14 * n + di
                            b, k = di // 4, di % 4
                            tensor.wait_ge(s_pp, t + 1)
                            tensor.matmul(ps_sc[b][32 * k:32 * k + 2, :],
                                          e2_ap(), p_t[t % 2][:],
                                          start=True, stop=True,
                                          tile_position=(0, 32 * k)).then_inc(s_scm, 1)
                        for b in range(4):
                            tensor.wait_ge(s_u, 4 * n + b + 1)
                            ins = tensor.matmul(ps_z[0:2, :], zsel_ap(b),
                                                u_sb[b][:], start=(b == 0),
                                                stop=(b == 3))
                        ins.then_inc(s_z, 1)
                        for di, d in enumerate(DELTAS):
                            t = 14 * n + di
                            b, k = di // 4, di % 4
                            if t >= 2:
                                tensor.wait_ge(s_avv, t - 1)
                            tensor.matmul(ps_u[t % 2][:], selu_ap(k),
                                          u_sb[b][:], start=True,
                                          stop=True).then_inc(s_bc, 1)
                        tensor.wait_ge(s_rz, n + 1)
                        if n >= 1:
                            tensor.wait_ge(s_ep, n)
                        tensor.matmul(ps_rz[:], e2t_sb[0:2, :],
                                      rz_bf[0:2, :], start=True,
                                      stop=True).then_inc(s_bcz, 1)

                @block.scalar
                def _(scalar):
                    for n in range(len(ITERS)):
                        if n >= 1:
                            scalar.wait_ge(s_ep, n)
                        for b in range(4):
                            scalar.wait_ge(s_scm, 14 * n + min(4 * b + 4, 14))
                            if n >= 1:
                                scalar.wait_ge(s_u, 4 * n + b - 3)
                            scalar.activation(pexp_sb[b][:], ps_sc[b][:],
                                              AF.Exp,
                                              scale=0.125).then_inc(s_ex, 1)

                @block.vector
                def _(vector):
                    vector.wait_ge(s_ld, 16 * 36)
                    sv4v = sv4_sb[:].rearrange("p (c j) -> p c j", j=4)
                    vector.tensor_tensor(sv_sb[:], sv4v[:, :, 0],
                                         sv4v[:, :, 1], ALU.add).then_inc(s_v5, 1)
                    vector.wait_ge(s_v5, 1)
                    vector.tensor_tensor(sv_sb[:], sv_sb[:],
                                         sv4v[:, :, 2], ALU.add).then_inc(s_v5, 1)
                    vector.wait_ge(s_v5, 2)
                    vector.tensor_tensor(sv_sb[:], sv_sb[:],
                                         sv4v[:, :, 3], ALU.add).then_inc(s_v5, 1)
                    nv = 3
                    for b in range(4):
                        vector.memset(ps_sc[b][:], 0.0)
                    vector.memset(ps_z[:], 0.0)
                    vector.memset(ps_u[0][:], 0.0)
                    vector.memset(ps_u[1][:], 0.0)
                    vector.memset(ps_rz[:], 0.0).then_inc(s_v5, 1)
                    nv += 1
                    vector.wait_ge(s_ld, 16 * 36)
                    for n, (c8, its) in enumerate(ITERS):
                        for di, d in enumerate(DELTAS):
                            t = 14 * n + di
                            if t >= 2:
                                vector.wait_ge(s_scm, t - 1)
                            if (d, its) in BIGIDX:
                                vector.wait_ge(s_kb, 16 * (buse_ord[(n, di)] + 1))
                            vector.tensor_tensor(
                                p_t[t % 2][:],
                                q_sb[c8][:, 512 * its:512 * (its + 1)],
                                ksrc(c8, its, di), ALU.mult).then_inc(s_pp, 1)
                        for b in range(4):
                            vector.wait_ge(s_ex, 4 * n + b + 1)
                            vector.tensor_scalar(u_sb[b][:], pexp_sb[b][:], 1.0,
                                                 None,
                                                 ALU.subtract).then_inc(s_u, 1)
                        vector.wait_ge(s_z, n + 1)
                        vector.tensor_scalar(tz_f[:], ps_z[0:2, :], 8192.0,
                                             None, ALU.add).then_inc(s_v5, 1)
                        nv += 1
                        vector.wait_ge(s_v5, nv)
                        vector.reciprocal(rz_f[:], tz_f[:]).then_inc(s_v5, 1)
                        nv += 1
                        vector.wait_ge(s_v5, nv)
                        vector.tensor_copy(rz_bf[:], rz_f[:]).then_inc(s_rz, 1)
                        for di, d in enumerate(DELTAS):
                            t = 14 * n + di
                            vector.wait_ge(s_bc, t + 1)
                            if (d, its) in BIGIDX:
                                vector.wait_ge(s_vb, 16 * (buse_ord[(n, di)] + 1))
                            if di == 0:
                                vector.tensor_tensor(acc[0][:],
                                                     vsrc(c8, its, di),
                                                     ps_u[t % 2][:],
                                                     ALU.mult).then_inc(s_avv, 1)
                            else:
                                vector.tensor_tensor(t_av[t % 2][:],
                                                     vsrc(c8, its, di),
                                                     ps_u[t % 2][:],
                                                     ALU.mult).then_inc(s_avv, 1)
                                vector.wait_ge(s_avv, t + 1)
                                vector.tensor_tensor(acc[di % 2][:],
                                                     acc[(di - 1) % 2][:],
                                                     t_av[t % 2][:], ALU.add)
                        vector.wait_ge(s_bcz, n + 1)
                        vector.scalar_tensor_tensor(
                            q_sb[c8][:, 512 * its:512 * (its + 1)],
                            acc[(len(DELTAS) - 1) % 2][:],
                            sv_sb[:, c8:c8 + 1], ps_rz[:],
                            ALU.add, ALU.mult).then_inc(s_ep, 1)

        ctxB = ExitStack()
        ps_o = [ctxB.enter_context(nc.psum_tensor(f"ps_o{i}", [128, 512], F32))
                for i in range(2)]
        ln = LNEmit(nc, ctx, ctxB, "ln3")
        with ctxB:
            y_ap = lambda c, its: x2_sb[c][:, 512 * its:512 * (its + 1)]
            out_ap = lambda c, its: q_sb[c][:, 512 * its:512 * (its + 1)]

            def wait_y(eng, its, c):
                eng.wait_ge(s_y, 4 * NCH)

            with nc.Block() as block2:
                @block2.sync
                def _(sync):
                    for c in range(NCH):
                        if c > 0:
                            sync.wait_ge(s_cast, c)
                        sync.dma_start(out=wstage[:],
                                       in_=wo_in[128 * c:128 * (c + 1), :]).then_inc(s_ld, 16)
                    for c in range(NCH):
                        sync.wait_ge(ln.s_x1, 3 * NCH + c + 1)
                        sync.dma_start(out=x3_out[128 * c:128 * (c + 1), :],
                                       in_=q_sb[c][:]).then_inc(s_st, 16)
                    sync.wait_ge(s_st, 16 * NCH)

                @block2.vector
                def _(vector):
                    ln.vector_init(vector)
                    for c in range(NCH):
                        vector.wait_ge(s_ld, 16 * (36 + c + 1))
                        vector.tensor_copy(w_bf[c][:],
                                           wstage[:]).then_inc(s_cast, 1)
                    j = 0
                    for dc in range(NCH):
                        for its in range(4):
                            vector.wait_ge(s_o, j + 1)
                            vector.scalar_tensor_tensor(
                                x2_sb[dc][:, 512 * its:512 * (its + 1)],
                                ps_o[j % 2][:], bo_sb[:, dc:dc + 1],
                                x2_sb[dc][:, 512 * its:512 * (its + 1)],
                                ALU.add, ALU.add).then_inc(s_y, 1)
                            j += 1
                    ln.vector_ops(vector, y_ap, out_ap)

                @block2.scalar
                def _(scalar):
                    ln.scalar_ops(scalar)

                @block2.gpsimd
                def _(gpsimd):
                    ln.pool_ops(gpsimd, y_ap, wait_y)

                @block2.tensor
                def _(tensor):
                    tensor.wait_ge(s_ep, len(ITERS))
                    tensor.wait_ge(s_cast, NCH)
                    j = 0
                    for dc in range(NCH):
                        for its in range(4):
                            if j >= 2:
                                tensor.wait_ge(s_y, j - 1)
                            for hp in range(NCH):
                                ins = tensor.matmul(
                                    ps_o[j % 2][:],
                                    w_bf[hp][:, 128 * dc:128 * (dc + 1)],
                                    q_sb[hp][:, 512 * its:512 * (its + 1)],
                                    start=(hp == 0), stop=(hp == NCH - 1))
                            ins.then_inc(s_o, 1)
                            j += 1
                    tensor.wait_ge(ln.s_ini, 1)
                    ln.tensor_ops(tensor, y_ap, wait_y)
    return nc



def build_moe_gate():
    nc = bass.Bass("TRN2", target_bir_lowering=False, debug=False)
    x3_in = nc.dram_tensor("x3_fm", [DM, TPC], BF16, kind="ExternalInput").ap()
    wred_in = nc.dram_tensor("w_red", [DM, 16], F32, kind="ExternalInput").ap()
    wg_in = nc.dram_tensor("wg", [8, 16], F32, kind="ExternalInput").ap()
    iota8_in = nc.dram_tensor("iota8", [128, 8], F32, kind="ExternalInput").ap()
    eye_in = nc.dram_tensor("eye", [128, 128], F32, kind="ExternalInput").ap()
    eidx_out = nc.dram_tensor("eidx", [128, 16], F32, kind="ExternalOutput").ap()
    score_out = nc.dram_tensor("score", [128, 16], F32, kind="ExternalOutput").ap()
    x3tm_out = nc.dram_tensor("x3tm", [TPC, DM], BF16, kind="ExternalOutput").ap()

    ctx = ExitStack()
    sb, ps, sem = _mk(nc, ctx)
    with ctx:
        x3_sb = [sb(f"x3_{c}", [128, TPC], BF16) for c in range(NCH)]
        wred_sb = sb("wred_s", [128, 16 * NCH], F32)
        wred_bf = sb("wred_bf", [128, 16 * NCH], BF16)
        wg_sb = sb("wg_s", [8, 16], F32)
        iota8_sb = sb("iota8_s", [128, 8], F32)
        eye_sb = sb("eye_s", [128, 128], F32)
        eye_bf = sb("eye_bf", [128, 128], BF16)
        red_bf = sb("red_bf", [16, TPC], BF16)
        wg2 = sb("wg2", [8, 16], F32)
        n2 = sb("n2", [8, 1], F32)
        lnn = sb("lnn", [8, 1], F32)
        nrm = sb("nrm", [8, 1], F32)
        rn = sb("rn", [8, 1], F32)
        wgn_bf = sb("wgn_bf", [8, 16], BF16)
        wgnT = sb("wgnT", [16, 8], BF16)
        mx = sb("mx", [128, 1], F32)
        nmx = sb("nmx", [128, 1], F32)
        e_t = sb("e_t", [128, 8], BF16)
        ssum = sb("ssum", [128, 1], F32)
        onehot = sb("onehot", [128, 8], BF16)
        idxm = sb("idxm", [128, 8], F32)
        eidx_sb = sb("eidx_s", [128, 16], F32)
        score_sb = sb("score_s", [128, 16], F32)
        xtm_st = [sb(f"xtm{i}", [128, DM], BF16) for i in range(2)]

        s_ld = sem("s_ld"); s_v = sem("s_v"); s_mm = sem("s_mm")
        s_g = sem("s_g"); s_t = sem("s_t"); s_ts = sem("s_ts")
        s_st = sem("s_st"); s_sc = sem("s_sc"); s_rd = sem("s_rd")

        ctxA = ExitStack()
        ps_red = [ctxA.enter_context(nc.psum_tensor(f"ps_red{i}", [16, 512], F32))
                  for i in range(2)]
        ps_l = [ctxA.enter_context(nc.psum_tensor(f"ps_l{i}", [128, 8], F32))
                for i in range(2)]
        ps_wt = ctxA.enter_context(nc.psum_tensor("ps_wt", [16, 8], BF16))
        ps_tr = [ctxA.enter_context(nc.psum_tensor(f"ps_tr{i}", [128, 128], BF16))
                 for i in range(2)]
        with ctxA:
            with nc.Block() as block:
                @block.sync
                def _(sync):
                    for c in range(NCH):
                        sync.dma_start(out=x3_sb[c][:],
                                       in_=x3_in[128 * c:128 * (c + 1), :]).then_inc(s_ld, 16)
                    sync.dma_start(
                        out=wred_sb[:].rearrange("p (c g) -> p c g", g=16),
                        in_=wred_in.rearrange("(c p) g -> p c g", p=128)).then_inc(s_ld, 16)
                    sync.dma_start(out=wg_sb[:], in_=wg_in).then_inc(s_ld, 16)
                    sync.dma_start(out=iota8_sb[:], in_=iota8_in).then_inc(s_ld, 16)
                    sync.dma_start(out=eye_sb[:], in_=eye_in).then_inc(s_ld, 16)
                    sync.wait_ge(s_sc, 16)
                    sync.dma_start(out=eidx_out, in_=eidx_sb[:]).then_inc(s_st, 16)
                    sync.dma_start(out=score_out, in_=score_sb[:]).then_inc(s_st, 16)
                    sync.wait_ge(s_st, 32)
                    for jt in range(16):
                        sync.wait_ge(s_ts, 8 * (jt + 1))
                        sync.dma_start(out=x3tm_out[128 * jt:128 * (jt + 1), :],
                                       in_=xtm_st[jt % 2][:]).then_inc(s_st, 16)
                        sync.wait_ge(s_st, 16 * (jt + 3))

                @block.vector
                def _(vector):
                    nv = 0
                    vector.wait_ge(s_ld, 16 * 12)
                    vector.tensor_copy(wred_bf[:], wred_sb[:])
                    vector.tensor_copy(eye_bf[:], eye_sb[:]).then_inc(s_v, 1)
                    nv += 1
                    vector.wait_ge(s_g, 1)
                    vector.reduce_sum(n2[:], wg2[:], axis=AX.X).then_inc(s_v, 1)
                    nv += 1
                    vector.wait_ge(s_g, 2)
                    vector.tensor_scalar(rn[:], nrm[:], 1e-4, None,
                                         ALU.max).then_inc(s_v, 1)
                    nv += 1
                    vector.wait_ge(s_v, nv)
                    vector.reciprocal(rn[:], rn[:]).then_inc(s_v, 1)
                    nv += 1
                    vector.wait_ge(s_v, nv)
                    vector.tensor_scalar(wgn_bf[:], wg_sb[:], rn[:, 0:1], None,
                                         ALU.mult).then_inc(s_v, 1)
                    nv += 1
                    vector.wait_ge(s_mm, 1)
                    vector.tensor_copy(wgnT[:], ps_wt[:]).then_inc(s_v, 1)
                    nv += 1
                    for jt in range(16):
                        vector.wait_ge(s_mm, 1 + 4 + jt + 1)
                        vector.reduce_max(mx[:], ps_l[jt % 2][:], axis=AX.X)
                        vector.reduce_max(nmx[:], ps_l[jt % 2][:], axis=AX.X,
                                          negate=True).then_inc(s_v, 1)
                        nv += 1
                        vector.wait_ge(s_g, 2 + jt + 1)
                        vector.reciprocal(score_sb[:, jt:jt + 1],
                                          ssum[:]).then_inc(s_v, 1)
                        nv += 1
                        vector.wait_ge(s_v, nv)
                        vector.tensor_scalar(onehot[:], ps_l[jt % 2][:],
                                             mx[:, 0:1], None,
                                             ALU.is_ge).then_inc(s_v, 1)
                        nv += 1
                        vector.wait_ge(s_v, nv)
                        vector.tensor_tensor(idxm[:], onehot[:], iota8_sb[:],
                                             ALU.mult).then_inc(s_v, 1)
                        nv += 1
                        vector.wait_ge(s_v, nv)
                        vector.reduce_sum(eidx_sb[:, jt:jt + 1], idxm[:],
                                          axis=AX.X).then_inc(s_sc, 1)
                    for jt in range(16):
                        if jt >= 2:
                            vector.wait_ge(s_st, 16 * (2 + jt - 1))
                        for c in range(NCH):
                            vector.wait_ge(s_t, 8 * jt + c + 1)
                            vector.tensor_copy(
                                xtm_st[jt % 2][:, 128 * c:128 * (c + 1)],
                                ps_tr[(8 * jt + c) % 2][:]).then_inc(s_ts, 1)

                @block.scalar
                def _(scalar):
                    scalar.wait_ge(s_ld, 16 * 12)
                    scalar.activation(wg2[:], wg_sb[:], AF.Square).then_inc(s_g, 1)
                    scalar.wait_ge(s_v, 2)
                    scalar.activation(lnn[:], n2[:], AF.Ln)
                    scalar.activation(nrm[:], lnn[:], AF.Exp,
                                      scale=0.5).then_inc(s_g, 1)
                    for its in range(4):
                        scalar.wait_ge(s_mm, 1 + its + 1)
                        scalar.copy(red_bf[0:16, 512 * its:512 * (its + 1)],
                                    ps_red[its % 2][0:16, :]).then_inc(s_rd, 1)
                    for jt in range(16):
                        scalar.wait_ge(s_v, 6 + 4 * jt + 1)
                        scalar.activation(e_t[:], ps_l[jt % 2][:], AF.Exp,
                                          bias=nmx[:, 0:1],
                                          accum_out=ssum[:, 0:1]).then_inc(s_g, 1)

                @block.tensor
                def _(tensor):
                    tensor.wait_ge(s_v, 5)
                    tensor.transpose(ps_wt[:], wgn_bf[:],
                                     eye_bf[0:8, 0:8]).then_inc(s_mm, 1)
                    tensor.wait_ge(s_ld, 16 * 12)
                    tensor.wait_ge(s_v, 1)
                    for its in range(4):
                        if its >= 2:
                            tensor.wait_ge(s_rd, its - 1)
                        for c in range(NCH):
                            ins = tensor.matmul(
                                ps_red[its % 2][:],
                                wred_bf[:, 16 * c:16 * (c + 1)],
                                x3_sb[c][:, 512 * its:512 * (its + 1)],
                                start=(c == 0), stop=(c == NCH - 1))
                        ins.then_inc(s_mm, 1)
                    for jt in range(16):
                        its = jt // 4
                        tensor.wait_ge(s_rd, its + 1)
                        if jt >= 2:
                            tensor.wait_ge(s_sc, jt - 1)
                        tensor.matmul(ps_l[jt % 2][:],
                                      red_bf[0:16, 128 * jt:128 * (jt + 1)],
                                      wgnT[0:16, :], start=True,
                                      stop=True).then_inc(s_mm, 1)
                    for jt in range(16):
                        for c in range(NCH):
                            t = 8 * jt + c
                            if t >= 2:
                                tensor.wait_ge(s_ts, t - 1)
                            tensor.transpose(ps_tr[t % 2][:],
                                             x3_sb[c][:, 128 * jt:128 * (jt + 1)],
                                             eye_bf[:]).then_inc(s_t, 1)

                @block.gpsimd
                def _(gpsimd):
                    pass
    return nc



def build_moe_expert():
    nc = bass.Bass("TRN2", target_bir_lowering=False, debug=False)
    eidx_in = nc.dram_tensor("eidx_all", [128, 128], F32, kind="ExternalInput").ap()
    xtm_in = nc.dram_tensor("x3tm_all", [16384, DM], BF16, kind="ExternalInput").ap()
    w1_in = nc.dram_tensor("w1f", [8192, 4096], F32, kind="ExternalInput").ap()
    w2_in = nc.dram_tensor("w2f", [32768, 1024], F32, kind="ExternalInput").ap()
    b1_in = nc.dram_tensor("b1f", [8, 4096], F32, kind="ExternalInput").ap()
    b2_in = nc.dram_tensor("b2f", [8, 1024], F32, kind="ExternalInput").ap()
    rank_in = nc.dram_tensor("rankf", [1, 1], F32, kind="ExternalInput").ap()
    tri_in = nc.dram_tensor("tri128", [128, 128], F32, kind="ExternalInput").ap()
    thr_in = nc.dram_tensor("thr", [8, 32], F32, kind="ExternalInput").ap()
    eye_in = nc.dram_tensor("eye", [128, 128], F32, kind="ExternalInput").ap()
    slotb_in = nc.dram_tensor("slotbase", [1, 20], F32, kind="ExternalInput").ap()
    c8b_in = nc.dram_tensor("c8base128", [1, 8], F32, kind="ExternalInput").ap()
    j32_in = nc.dram_tensor("j32base128", [1, 32], F32, kind="ExternalInput").ap()
    iseg_in = nc.dram_tensor("iotaseg", [128, 3], F32, kind="ExternalInput").ap()
    z_out = nc.dram_tensor("z_slice", [2560, DM], BF16, kind="ExternalOutput").ap()
    post_out = nc.dram_tensor("posT", [128, 128], F32, kind="ExternalOutput").ap()

    d_sorted = nc.dram_tensor("d_sorted", [20480, DM], BF16, kind="Internal").ap()
    d_w2 = nc.dram_tensor("d_w2", [4096, 1024], F32, kind="Internal").ap()

    b1v = b1_in.rearrange("e (c p) -> (e c) p", p=128)
    b2v = b2_in.rearrange("e (c p) -> (e c) p", p=128)

    NIDX = 230
    ctx = ExitStack()
    sb, ps, sem = _mk(nc, ctx)
    with ctx:
        em_sb = sb("em_s", [128, 128], F32)
        tri_sb = sb("tri_s", [128, 128], F32)
        tri_bf = sb("tri_bf", [128, 128], BF16)
        thr_sb = sb("thr_s", [8, 32], F32)
        eye_sb = sb("eye_s", [128, 128], F32)
        eye_bf = sb("eye_bf", [128, 128], BF16)
        rank_sb = sb("rank_s", [1, 1], F32)
        slotb_sb = sb("slotb_s", [1, 20], F32)
        c8b_sb = sb("c8b_s", [1, 8], F32)
        j32_sb = sb("j32_s", [1, 32], F32)
        iseg_sb = sb("iseg_s", [128, 3], F32)
        ones128 = sb("ones128", [128, 128], BF16)
        ones_col = sb("onesc", [128, 1], BF16)
        onescf = sb("onescf", [1, 128], F32)
        ones_row = sb("onesr", [1, 128], BF16)
        onesr8 = sb("onesr8", [1, 8], BF16)
        onesr20 = sb("onesr20", [1, 20], BF16)
        onesr32 = sb("onesr32", [1, 32], BF16)
        m_sb = [sb(f"m_{e}", [128, 128], BF16) for e in range(8)]
        scin = sb("scin", [128, 128], F32)
        wex = [sb(f"wex_{e}", [128, 128], F32) for e in range(8)]
        totb = sb("totb", [128, 8], F32)
        totb_bf = sb("totb_bf", [128, 8], BF16)
        tot_sb = sb("tot_s", [8, 1], F32)
        geb = sb("geb", [8, 32], BF16)
        blk8 = sb("blk8", [8, 1], F32)
        pad_f = sb("pad_f", [8, 1], F32)
        pad_bf = sb("pad_bf", [8, 1], BF16)
        go_f = sb("go_f", [8, 1], F32)
        go_bf = sb("go_bf", [8, 1], BF16)
        gorow_bf = sb("gorow_bf", [1, 8], BF16)
        gorow_f = sb("gorow_f", [1, 8], F32)
        bo_sb = sb("bo_s", [128, 8], F32)
        off_sb = sb("off_s", [128, 8], F32)
        ptmp = [sb(f"ptmp{i}", [128, 128], F32) for i in range(2)]
        pacc = [sb(f"pacc{i}", [128, 128], F32) for i in range(2)]
        posT_f = sb("posT_f", [128, 128], F32)
        posT_i = sb("posT_i", [128, 128], I32)
        zsrc = sb("zsrc", [128, DM], BF16)
        v1 = sb("v1", [1, 1], F32)
        flg = sb("flg", [1, 8], BF16)
        e_f = sb("e_f", [1, 5], F32)
        et_f = sb("et_f", [1, 5], F32)
        allidx = sb("allidx", [1, NIDX], F32)
        idx_f = sb("idx_f", [128, NIDX], F32)
        idx_i = sb("idx_i", [128, NIDX], I32)
        xsc = [sb(f"xsc{i}", [128, DM], BF16) for i in range(2)]
        xtm_blk = [sb(f"xtb{i}", [128, DM], BF16) for i in range(2)]
        xfm_blk = [sb(f"xfm{c}", [128, 512], BF16) for c in range(NCH)]
        w1g = [sb(f"w1g{i}", [128, 4096], F32) for i in range(2)]
        w1b8 = [sb(f"w1b8_{c}", [128, 4096], BF16) for c in range(NCH)]
        wtmp2b = [sb(f"wtmp0b{i}", [128, 1024], F32) for i in range(2)]
        w2stage = sb("w2s", [128, 4096], F32)
        w2_bf = sb("w2b", [128, 4096], BF16)
        b1g = sb("b1g", [128, 128], F32)
        b2g = sb("b2g", [128, 128], F32)
        b1_pc = sb("b1pc", [128, 32], F32)
        b2_pc = sb("b2pc", [128, 8], F32)
        h_sb = sb("h_s", [128, 32 * 512], BF16)
        z_bf = [sb(f"zb{d}", [128, 512], BF16) for d in range(NCH)]
        ztm_st = [sb(f"ztm{i}", [128, DM], BF16) for i in range(2)]

        s_ld = sem("s_ld"); s_va = sem("s_va"); s_ta = sem("s_ta")
        s_ze = sem("s_ze"); s_sca = sem("s_sca"); s_xl2 = sem("s_xl2")
        s_tok = sem("s_tok"); s_tr = sem("s_tr"); s_xe = sem("s_xe")
        s_wf = sem("s_wf"); s_wf2 = sem("s_wf2"); s_ws = sem("s_ws")
        s_wc1 = sem("s_wc1"); s_ws2 = sem("s_ws2")
        s_wc = sem("s_wc"); s_hm = sem("s_hm"); s_ge = sem("s_ge")
        s_z2 = sem("s_z2"); s_ev = sem("s_ev"); s_ztr = sem("s_ztr")
        s_zev = sem("s_zev"); s_st = sem("s_st"); s_bt = sem("s_bt")
        s_bte = sem("s_bte"); s_stp = sem("s_stp")

        marks = {}

        ctxA = ExitStack()
        ps_bo = ctxA.enter_context(nc.psum_tensor("ps_bo", [128, 8], F32))
        ps_te = ctxA.enter_context(nc.psum_tensor("ps_te", [8, 1], F32))
        ps_go = ctxA.enter_context(nc.psum_tensor("ps_go", [8, 1], F32))
        ps_got = ctxA.enter_context(nc.psum_tensor("ps_got", [1, 8], BF16))
        ps_gof = ctxA.enter_context(nc.psum_tensor("ps_gof", [1, 8], F32))
        ps_gob = ctxA.enter_context(nc.psum_tensor("ps_gob", [128, 8], F32))
        ps_pt = ctxA.enter_context(nc.psum_tensor("ps_pt", [128, 128], F32))
        ps_ib = ctxA.enter_context(nc.psum_tensor("ps_ib", [128, NIDX], F32))
        with ctxA:
            with nc.Block() as block:
                @block.sync
                def _(sync):
                    for t, src_ in ((em_sb, eidx_in), (tri_sb, tri_in),
                                    (thr_sb, thr_in), (eye_sb, eye_in),
                                    (rank_sb, rank_in), (slotb_sb, slotb_in),
                                    (c8b_sb, c8b_in), (j32_sb, j32_in),
                                    (iseg_sb, iseg_in)):
                        sync.dma_start(out=t[:], in_=src_).then_inc(s_ld, 16)
                    sync.wait_ge(s_va, 1)
                    for i in range(20480 // 128):
                        sync.dma_start(out=d_sorted[128 * i:128 * (i + 1), :],
                                       in_=zsrc[:]).then_inc(s_ze, 16)
                    sync.wait_ge(s_va, 240)
                    sync.dma_start(out=post_out, in_=posT_f[:]).then_inc(s_stp, 16)
                    sync.wait_ge(s_stp, 16)
                    for j in range(128):
                        if j >= 2:
                            sync.wait_ge(s_sca, 16 * (j - 1))
                        sync.dma_start(out=xsc[j % 2][:],
                                       in_=xtm_in[128 * j:128 * (j + 1),
                                                  :]).then_inc(s_xl2, 16)

                @block.vector
                def _(vector):
                    nv = 0
                    def step(ins, name=None):
                        nonlocal nv
                        ins.then_inc(s_va, 1)
                        nv += 1
                        if name:
                            marks[name] = nv
                        vector.wait_ge(s_va, nv)
                    vector.memset(zsrc[:], 0.0).then_inc(s_va, 1)
                    nv += 1
                    vector.memset(ones128[:], 1.0)
                    vector.memset(ones_col[:], 1.0)
                    vector.memset(onescf[:], 1.0)
                    vector.memset(ones_row[:], 1.0)
                    vector.memset(onesr8[:], 1.0)
                    vector.memset(onesr20[:], 1.0)
                    vector.memset(onesr32[:], 1.0)
                    vector.memset(b1g[:], 0.0)
                    vector.memset(b2g[:], 0.0)
                    vector.wait_ge(s_ld, 16 * 9)
                    vector.tensor_copy(tri_bf[:], tri_sb[:])
                    step(vector.tensor_copy(eye_bf[:], eye_sb[:]), "ini")
                    for e in range(8):
                        step(vector.tensor_scalar(m_sb[e][:], em_sb[:],
                                                  float(e), None, ALU.is_equal))
                        step(vector.tensor_tensor_scan(scin[:], ones128[:],
                                                       m_sb[e][:], 0.0,
                                                       ALU.mult, ALU.add))
                        step(vector.tensor_tensor(wex[e][:], scin[:], m_sb[e][:],
                                                  ALU.subtract))
                        step(vector.reduce_sum(totb[:, e:e + 1], m_sb[e][:],
                                               axis=AX.X))
                    step(vector.tensor_copy(totb_bf[:], totb[:]), "totb")
                    vector.wait_ge(s_ta, 2)
                    step(vector.tensor_copy(tot_sb[:], ps_te[:]))
                    step(vector.tensor_scalar(geb[:], thr_sb[:], tot_sb[:, 0:1],
                                              None, ALU.is_lt))
                    step(vector.reduce_sum(blk8[:], geb[:], axis=AX.X))
                    step(vector.tensor_scalar(pad_f[:], blk8[:], 512.0, None,
                                              ALU.mult))
                    step(vector.tensor_copy(pad_bf[:], pad_f[:]), "pad")
                    vector.wait_ge(s_ta, 3)
                    step(vector.tensor_copy(go_f[:], ps_go[:]))
                    step(vector.tensor_copy(go_bf[:], go_f[:]), "gobf")
                    vector.wait_ge(s_ta, 5)
                    step(vector.tensor_copy(gorow_bf[:], ps_got[:]))
                    step(vector.tensor_copy(gorow_f[:], ps_gof[:]), "gorow")
                    vector.wait_ge(s_ta, 6)
                    step(vector.tensor_copy(bo_sb[:], ps_bo[:]))
                    step(vector.tensor_tensor(off_sb[:], bo_sb[:], ps_gob[:],
                                              ALU.add))
                    for e in range(8):
                        step(vector.scalar_tensor_tensor(
                            ptmp[e % 2][:], wex[e][:], off_sb[:, e:e + 1],
                            m_sb[e][:], ALU.add, ALU.mult))
                        if e == 0:
                            step(vector.tensor_copy(pacc[0][:], ptmp[0][:]))
                        else:
                            step(vector.tensor_tensor(pacc[e % 2][:],
                                                      pacc[(e - 1) % 2][:],
                                                      ptmp[e % 2][:], ALU.add))
                    marks["pacc"] = nv
                    step(vector.tensor_scalar(v1[:], rank_sb[:], 2560.0, None,
                                              ALU.mult))
                    step(vector.scalar_tensor_tensor(
                        allidx[0:1, 200:220], onesr20[:], v1[0:1, 0:1],
                        slotb_sb[:], ALU.mult, ALU.add))
                    for k in range(5):
                        step(vector.tensor_scalar(flg[:], gorow_f[:],
                                                  allidx[0:1, 200 + 4 * k:
                                                         200 + 4 * k + 1],
                                                  None, ALU.is_le))
                        step(vector.reduce_sum(e_f[:, k:k + 1], flg[:],
                                               axis=AX.X))
                    step(vector.tensor_scalar(e_f[:], e_f[:], 1.0, None,
                                              ALU.subtract))
                    step(vector.tensor_scalar(et_f[:], e_f[:], 1024.0, None,
                                              ALU.mult))
                    for k in range(5):
                        step(vector.scalar_tensor_tensor(
                            allidx[0:1, 8 * k:8 * (k + 1)], onesr8[:],
                            et_f[0:1, k:k + 1], c8b_sb[:], ALU.mult, ALU.add))
                    step(vector.tensor_scalar(et_f[:], e_f[:], 4096.0, None,
                                              ALU.mult))
                    for k in range(5):
                        step(vector.scalar_tensor_tensor(
                            allidx[0:1, 40 + 32 * k:40 + 32 * (k + 1)],
                            onesr32[:], et_f[0:1, k:k + 1], j32_sb[:],
                            ALU.mult, ALU.add))
                    step(vector.tensor_scalar(allidx[0:1, 220:225], e_f[:],
                                              32.0, None, ALU.mult))
                    step(vector.tensor_scalar(allidx[0:1, 225:230], e_f[:],
                                              8.0, None, ALU.mult), "allidx")
                    vector.wait_ge(s_ta, 8)
                    step(vector.tensor_scalar(idx_f[:, 0:220], ps_ib[:, 0:220],
                                              iseg_sb[:, 0:1], None, ALU.add))
                    step(vector.tensor_scalar(idx_f[:, 220:225],
                                              ps_ib[:, 220:225],
                                              iseg_sb[:, 1:2], None, ALU.add))
                    step(vector.tensor_scalar(idx_f[:, 225:230],
                                              ps_ib[:, 225:230],
                                              iseg_sb[:, 2:3], None, ALU.add))
                    step(vector.tensor_copy(idx_i[:], idx_f[:]), "idx")
                    vector.wait_ge(s_ta, 7)
                    step(vector.tensor_copy(posT_f[:], ps_pt[:]))
                    step(vector.tensor_copy(posT_i[:], posT_f[:]), "post")
                    while nv < 240:
                        nv += 1
                        vector.sem_inc(s_va, 1)

                @block.tensor
                def _(tensor):
                    tensor.wait_ge(s_va, marks["totb"])
                    tensor.matmul(ps_bo[:], tri_bf[:], totb_bf[:],
                                  start=True, stop=True).then_inc(s_ta, 1)
                    tensor.matmul(ps_te[:], totb_bf[:], ones_col[:, 0:1],
                                  start=True, stop=True).then_inc(s_ta, 1)
                    tensor.wait_ge(s_va, marks["pad"])
                    tensor.matmul(ps_go[:], tri_bf[0:8, 0:8], pad_bf[:, 0:1],
                                  start=True, stop=True).then_inc(s_ta, 1)
                    tensor.wait_ge(s_va, marks["gobf"])
                    tensor.transpose(ps_got[:], go_bf[:, 0:1],
                                     eye_bf[0:8, 0:8]).then_inc(s_ta, 1)
                    tensor.transpose(ps_gof[:], go_f[:, 0:1],
                                     eye_sb[0:8, 0:8]).then_inc(s_ta, 1)
                    tensor.wait_ge(s_va, marks["gorow"])
                    tensor.matmul(ps_gob[:], ones_row[:], gorow_bf[:],
                                  start=True, stop=True).then_inc(s_ta, 1)
                    tensor.wait_ge(s_va, marks["pacc"])
                    tensor.transpose(ps_pt[:], pacc[7 % 2][:],
                                     eye_sb[:]).then_inc(s_ta, 1)
                    tensor.wait_ge(s_va, marks["allidx"])
                    tensor.matmul(ps_ib[:], onescf[:], allidx[:],
                                  start=True, stop=True).then_inc(s_ta, 1)

                @block.gpsimd
                def _(gpsimd):
                    gpsimd.wait_ge(s_ze, 16 * 160)
                    gpsimd.wait_ge(s_va, 240)
                    for j in range(128):
                        gpsimd.wait_ge(s_xl2, 16 * (j + 1))
                        gpsimd.indirect_dma_start(
                            out=d_sorted[:, :],
                            out_offset=bass.IndirectOffsetOnAxis(
                                ap=posT_i[:, j:j + 1], axis=0),
                            in_=xsc[j % 2][:],
                            in_offset=None).then_inc(s_sca, 16)
                    gpsimd.wait_ge(s_sca, 16 * 128)

        ctxB = ExitStack()
        ps_tr = [ctxB.enter_context(nc.psum_tensor(f"ps_tr{i}", [128, 128], BF16))
                 for i in range(2)]
        ps_h = [ctxB.enter_context(nc.psum_tensor(f"ps_h{i}", [128, 512], F32))
                for i in range(2)]
        ps_zz = [ctxB.enter_context(nc.psum_tensor(f"ps_zz{i}", [128, 512], F32))
                 for i in range(2)]
        ps_bt = ctxB.enter_context(nc.psum_tensor("ps_bt", [128, 128], F32))
        with ctxB:
            with nc.Block() as block2:
                @block2.gpsimd
                def _(gpsimd):
                    nf = 0
                    nf2 = 0
                    for k in range(5):
                        for m in range(4):
                            g = 4 * k + m
                            if g >= 2:
                                gpsimd.wait_ge(s_tr, 8 * (g - 1))
                            gpsimd.indirect_dma_start(
                                out=xtm_blk[m % 2][:], out_offset=None,
                                in_=d_sorted[:, :],
                                in_offset=bass.IndirectOffsetOnAxis(
                                    ap=idx_i[:, 200 + g:200 + g + 1],
                                    axis=0)).then_inc(s_tok, 16)
                            gpsimd.wait_ge(s_tok, 16 * (6 * k + m + 1))
                        if k >= 1:
                            gpsimd.wait_ge(s_bt, 2 * k)
                        gpsimd.indirect_dma_start(
                            out=b1g[:], out_offset=None, in_=b1v[:, :],
                            in_offset=bass.IndirectOffsetOnAxis(
                                ap=idx_i[:, 220 + k:220 + k + 1], axis=0),
                            bounds_check=256, oob_is_err=False).then_inc(s_tok, 16)
                        gpsimd.wait_ge(s_tok, 16 * (6 * k + 5))
                        gpsimd.indirect_dma_start(
                            out=b2g[:], out_offset=None, in_=b2v[:, :],
                            in_offset=bass.IndirectOffsetOnAxis(
                                ap=idx_i[:, 225 + k:225 + k + 1], axis=0),
                            bounds_check=64, oob_is_err=False).then_inc(s_tok, 16)
                        gpsimd.wait_ge(s_tok, 16 * (6 * k + 6))
                        for c8 in range(NCH):
                            i1 = 8 * k + c8
                            if i1 >= 2:
                                gpsimd.wait_ge(s_wc1, i1 - 1)
                            gpsimd.indirect_dma_start(
                                out=w1g[i1 % 2][:, :], out_offset=None,
                                in_=w1_in[:, :],
                                in_offset=bass.IndirectOffsetOnAxis(
                                    ap=idx_i[:, 8 * k + c8:8 * k + c8 + 1],
                                    axis=0)).then_inc(s_wf, 16)
                        for j in range(32):
                            i2 = 32 * k + j
                            if i2 >= 2:
                                gpsimd.wait_ge(s_wf2, 16 * (i2 - 1))
                            gpsimd.indirect_dma_start(
                                out=wtmp2b[i2 % 2][:, :], out_offset=None,
                                in_=w2_in[:, :],
                                in_offset=bass.IndirectOffsetOnAxis(
                                    ap=idx_i[:, 40 + 32 * k + j:
                                             40 + 32 * k + j + 1],
                                    axis=0)).then_inc(s_wf, 16)

                @block2.sync
                def _(sync):
                    for k in range(5):
                        for dc in range(NCH):
                            if k > 0 or dc > 0:
                                sync.wait_ge(s_wc, 8 * k + dc)
                            sync.wait_ge(s_wf2, 16 * 32 * (k + 1))
                            sync.dma_start(
                                out=w2stage[:].rearrange("p (f w) -> p f w",
                                                         w=128),
                                in_=d_w2.rearrange("(f p) w -> p f w", p=128)
                                [:, :, 128 * dc:128 * (dc + 1)]).then_inc(s_ws2, 16)
                        for m in range(4):
                            sync.wait_ge(s_zev, 8 * (4 * k + m + 1))
                            sync.dma_start(
                                out=z_out[512 * k + 128 * m:
                                          512 * k + 128 * (m + 1), :],
                                in_=ztm_st[m % 2][:]).then_inc(s_st, 16)
                            sync.wait_ge(s_st, 16 * (4 * k + m + 1))
                    sync.wait_ge(s_st, 16 * 20)

                @block2.vector
                def _(vector):
                    for k in range(5):
                        for m in range(4):
                            for c8 in range(NCH):
                                t = 32 * k + 8 * m + c8
                                vector.wait_ge(s_tr, t + 1)
                                if k >= 1:
                                    vector.wait_ge(s_hm, 32 * k)
                                vector.tensor_copy(
                                    xfm_blk[c8][:, 128 * m:128 * (m + 1)],
                                    ps_tr[t % 2][:]).then_inc(s_xe, 1)
                        vector.wait_ge(s_bt, 2 * k + 1)
                        if k >= 1:
                            vector.wait_ge(s_ge, 32 * k)
                        vector.tensor_copy(b1_pc[:],
                                           ps_bt[:, 0:32]).then_inc(s_bte, 1)
                        vector.wait_ge(s_bt, 2 * k + 2)
                        if k >= 1:
                            vector.wait_ge(s_ev, 8 * k)
                        vector.tensor_copy(b2_pc[:],
                                           ps_bt[:, 0:8]).then_inc(s_bte, 1)
                        for c8 in range(NCH):
                            i1 = 8 * k + c8
                            vector.wait_ge(s_wf, 16 * (40 * k + c8 + 1))
                            if k >= 1:
                                vector.wait_ge(s_hm, 32 * k)
                            vector.tensor_copy(w1b8[c8][:],
                                               w1g[i1 % 2][:]).then_inc(s_wc1, 1)
                        for dc in range(NCH):
                            vector.wait_ge(s_ws2, 16 * (8 * k + dc + 1))
                            if 8 * k + dc >= 1:
                                vector.wait_ge(s_z2, 8 * k + dc)
                            vector.tensor_copy(w2_bf[:],
                                               w2stage[:]).then_inc(s_wc, 1)
                        for m in range(4):
                            if 4 * k + m >= 2:
                                vector.wait_ge(s_st, 16 * (4 * k + m - 1))
                            for dc in range(NCH):
                                t = 32 * k + 8 * m + dc
                                vector.wait_ge(s_ztr, t + 1)
                                vector.tensor_copy(
                                    ztm_st[m % 2][:, 128 * dc:128 * (dc + 1)],
                                    ps_tr[t % 2][:]).then_inc(s_zev, 1)

                @block2.scalar
                def _(scalar):
                    for k in range(5):
                        if k >= 1:
                            scalar.wait_ge(s_ws2, 16 * 8 * k)
                        for j in range(32):
                            i2 = 32 * k + j
                            scalar.wait_ge(s_wf, 16 * (40 * k + 8 + j + 1))
                            scalar.dma_start(
                                out=d_w2[128 * j:128 * (j + 1), :],
                                in_=wtmp2b[i2 % 2][:]).then_inc(s_wf2, 16)
                        for fc in range(32):
                            u = 32 * k + fc
                            scalar.wait_ge(s_hm, u + 1)
                            scalar.wait_ge(s_bte, 2 * k + 1)
                            if k >= 1:
                                scalar.wait_ge(s_z2, 8 * k)
                            scalar.activation(
                                h_sb[:, 512 * fc:512 * (fc + 1)],
                                ps_h[fc % 2][:], AF.Gelu,
                                bias=b1_pc[:, fc:fc + 1]).then_inc(s_ge, 1)
                        for dc in range(NCH):
                            scalar.wait_ge(s_z2, 8 * k + dc + 1)
                            scalar.wait_ge(s_bte, 2 * k + 2)
                            if k >= 1:
                                scalar.wait_ge(s_ztr, 32 * k)
                            scalar.activation(
                                z_bf[dc][:], ps_zz[dc % 2][:], AF.Identity,
                                bias=b2_pc[:, dc:dc + 1]).then_inc(s_ev, 1)

                @block2.tensor
                def _(tensor):
                    for k in range(5):
                        for m in range(4):
                            tensor.wait_ge(s_tok, 16 * (6 * k + m + 1))
                            if k >= 1:
                                tensor.wait_ge(s_zev, 32 * k)
                            for c8 in range(NCH):
                                t = 32 * k + 8 * m + c8
                                if t >= 2:
                                    tensor.wait_ge(s_xe, t - 1)
                                tensor.transpose(
                                    ps_tr[t % 2][:],
                                    xtm_blk[m % 2][:, 128 * c8:128 * (c8 + 1)],
                                    eye_bf[:]).then_inc(s_tr, 1)
                        tensor.wait_ge(s_tok, 16 * (6 * k + 6))
                        if k >= 1:
                            tensor.wait_ge(s_bte, 2 * k)
                        tensor.transpose(ps_bt[:], b1g[:],
                                         eye_sb[:]).then_inc(s_bt, 1)
                        tensor.wait_ge(s_bte, 2 * k + 1)
                        tensor.transpose(ps_bt[:], b2g[:],
                                         eye_sb[:]).then_inc(s_bt, 1)
                        for fc in range(32):
                            u = 32 * k + fc
                            tensor.wait_ge(s_wc1, 8 * (k + 1))
                            tensor.wait_ge(s_xe, 32 * (k + 1))
                            if u >= 2:
                                tensor.wait_ge(s_ge, u - 1)
                            if k >= 1:
                                tensor.wait_ge(s_z2, 8 * k)
                            for c8 in range(NCH):
                                ins = tensor.matmul(
                                    ps_h[fc % 2][:],
                                    w1b8[c8][:, 128 * fc:128 * (fc + 1)],
                                    xfm_blk[c8][:], start=(c8 == 0),
                                    stop=(c8 == NCH - 1))
                            ins.then_inc(s_hm, 1)
                        for dc in range(NCH):
                            zu = 8 * k + dc
                            tensor.wait_ge(s_wc, 8 * k + dc + 1)
                            tensor.wait_ge(s_ge, 32 * (k + 1))
                            if zu >= 2:
                                tensor.wait_ge(s_ev, zu - 1)
                            for fc in range(32):
                                ins = tensor.matmul(
                                    ps_zz[dc % 2][:],
                                    w2_bf[:, 128 * fc:128 * (fc + 1)],
                                    h_sb[:, 512 * fc:512 * (fc + 1)],
                                    start=(fc == 0), stop=(fc == 31))
                            ins.then_inc(s_z2, 1)
                        for m in range(4):
                            for dc in range(NCH):
                                t = 32 * k + 8 * m + dc
                                tensor.wait_ge(s_ev, 8 * k + dc + 1)
                                if t >= 2:
                                    tensor.wait_ge(s_zev, t - 1)
                                tensor.transpose(
                                    ps_tr[t % 2][:],
                                    z_bf[dc][:, 128 * m:128 * (m + 1)],
                                    eye_bf[:]).then_inc(s_ztr, 1)
    return nc



def build_moe_final():
    nc = bass.Bass("TRN2", target_bir_lowering=False, debug=False)
    z_in = nc.dram_tensor("z_all", [20480, DM], BF16, kind="ExternalInput").ap()
    pos_in = nc.dram_tensor("pos_own", [128, 16], F32, kind="ExternalInput").ap()
    sc_in = nc.dram_tensor("score", [128, 16], F32, kind="ExternalInput").ap()
    xtm_in = nc.dram_tensor("x3tm", [TPC, DM], BF16, kind="ExternalInput").ap()
    out_tm = nc.dram_tensor("out_tm", [TPC, DM], F32, kind="ExternalOutput").ap()

    ctx = ExitStack()
    sb, ps, sem = _mk(nc, ctx)
    with ctx:
        pos_sb = sb("pos_s", [128, 16], F32)
        pos_i = sb("pos_i", [128, 16], I32)
        sc_sb = sb("sc_s", [128, 16], F32)
        x3t = [sb(f"x3t{i}", [128, DM], BF16) for i in range(2)]
        zg = [sb(f"zg{i}", [128, DM], BF16) for i in range(2)]
        y_f = [sb(f"y_f{i}", [128, DM], F32) for i in range(2)]
        ysq = sb("ysq", [128, DM], F32)
        s1 = sb("s1", [128, 1], F32)
        s2 = sb("s2", [128, 1], F32)
        mu = sb("mu", [128, 1], F32)
        m2 = sb("m2", [128, 1], F32)
        mu2 = sb("mu2", [128, 1], F32)
        var = sb("var", [128, 1], F32)
        lnv = sb("lnv", [128, 1], F32)
        rstd = sb("rstd", [128, 1], F32)
        o_f = [sb(f"o_f{i}", [128, DM], F32) for i in range(2)]
        eps4 = sb("eps4", [128, 1], F32)

        s_ld = sem("s_ld"); s_gi = sem("s_gi"); s_zg = sem("s_zg")
        s_v = sem("s_v"); s_sc = sem("s_sc"); s_o = sem("s_o")
        s_st = sem("s_st"); s_xl = sem("s_xl")

        with nc.Block() as block:
            @block.sync
            def _(sync):
                for t, src_ in ((pos_sb, pos_in), (sc_sb, sc_in)):
                    sync.dma_start(out=t[:], in_=src_).then_inc(s_ld, 16)
                for j in range(18):
                    if j < 16:
                        if j >= 2:
                            sync.wait_ge(s_v, 8 * (j - 1))
                        sync.dma_start(out=x3t[j % 2][:],
                                       in_=xtm_in[128 * j:128 * (j + 1),
                                                  :]).then_inc(s_xl, 16)
                        sync.wait_ge(s_xl, 16 * (j + 1))
                    if j >= 2:
                        sync.wait_ge(s_v, 8 * (j - 2) + 7)
                        sync.dma_start(out=out_tm[128 * (j - 2):128 * (j - 1), :],
                                       in_=o_f[(j - 2) % 2][:]).then_inc(s_st, 16)
                        sync.wait_ge(s_st, 16 * (j - 1))
                sync.wait_ge(s_st, 16 * 16)

            @block.gpsimd
            def _(gpsimd):
                gpsimd.wait_ge(s_gi, 1)
                for j in range(16):
                    if j >= 2:
                        gpsimd.wait_ge(s_v, 8 * (j - 1) + 1)
                    gpsimd.indirect_dma_start(
                        out=zg[j % 2][:], out_offset=None,
                        in_=z_in[:, :],
                        in_offset=bass.IndirectOffsetOnAxis(
                            ap=pos_i[:, j:j + 1], axis=0)).then_inc(s_zg, 16)
                    gpsimd.wait_ge(s_zg, 16 * (j + 1))

            @block.vector
            def _(vector):
                vector.memset(eps4[:], 1e-5)
                vector.wait_ge(s_ld, 32)
                vector.tensor_copy(pos_i[:], pos_sb[:]).then_inc(s_gi, 1)
                nv = 0
                for j in range(16):
                    vector.wait_ge(s_zg, 16 * (j + 1))
                    vector.wait_ge(s_xl, 16 * (j + 1))
                    vector.scalar_tensor_tensor(
                        y_f[j % 2][:], zg[j % 2][:], sc_sb[:, j:j + 1],
                        x3t[j % 2][:], ALU.mult, ALU.add).then_inc(s_v, 1)
                    nv += 1
                    vector.wait_ge(s_v, nv)
                    vector.tensor_tensor(ysq[:], y_f[j % 2][:], y_f[j % 2][:],
                                         ALU.mult)
                    vector.reduce_sum(s1[:], y_f[j % 2][:],
                                      axis=AX.X).then_inc(s_v, 1)
                    nv += 1
                    vector.wait_ge(s_v, nv)
                    vector.reduce_sum(s2[:], ysq[:], axis=AX.X).then_inc(s_v, 1)
                    nv += 1
                    vector.wait_ge(s_v, nv)
                    vector.tensor_scalar(mu[:], s1[:], 1.0 / DM, None, ALU.mult)
                    vector.tensor_scalar(m2[:], s2[:], 1.0 / DM, None,
                                         ALU.mult).then_inc(s_v, 1)
                    nv += 1
                    vector.wait_ge(s_v, nv)
                    vector.tensor_tensor(mu2[:], mu[:], mu[:], ALU.mult).then_inc(s_v, 1)
                    nv += 1
                    vector.wait_ge(s_v, nv)
                    vector.tensor_tensor(var[:], m2[:], mu2[:],
                                         ALU.subtract).then_inc(s_v, 1)
                    nv += 1
                    if j >= 2:
                        vector.wait_ge(s_st, 16 * (j - 1))
                    vector.wait_ge(s_sc, j + 1)
                    vector.tensor_scalar(o_f[j % 2][:], y_f[j % 2][:],
                                         mu[:, 0:1], rstd[:, 0:1],
                                         ALU.subtract, ALU.mult).then_inc(s_v, 1)
                    nv += 1
                    vector.sem_inc(s_v, 1)
                    nv += 1

            @block.scalar
            def _(scalar):
                for j in range(16):
                    scalar.wait_ge(s_v, 8 * j + 6)
                    scalar.activation(lnv[:], var[:], AF.Ln,
                                      bias=eps4[:, 0:1])
                    scalar.activation(rstd[:], lnv[:], AF.Exp,
                                      scale=-0.5).then_inc(s_sc, 1)
    return nc



def _core_view(full):
    out = []
    for c in range(NCORE):
        b, cq = divmod(c, 4)
        out.append(np.ascontiguousarray(full[b, 2048 * cq:2048 * (cq + 1)]))
    return out


def _pc(v):
    return np.ascontiguousarray(np.asarray(v).reshape(8, 128).T)


def _asT(a):
    return np.ascontiguousarray(np.asarray(a).T)


def _l0attn_prep(inz):
    jj = np.arange(128)
    ii = np.arange(ITW)
    biasv = np.zeros((4, 128, NL0T), np.float32)
    maskt = np.zeros((4, 128, NL0M * ITW), BF)
    for t, (hp, h2, h, it, s) in enumerate(L0TILES):
        jpos = 128 * s + jj
        for cq in range(4):
            i0g = 2048 * cq + ITW * it
            if t in L0MASK:
                biasv[cq, :, t] = -SLOPE[h] * jpos
                mi = _L0MASK_ORD[t]
                m = (jpos[:, None] - i0g <= ii[None, :])
                maskt[cq, :, ITW * mi:ITW * (mi + 1)] = m.astype(BF)
            else:
                if 128 * s > i0g + ITW - 1:
                    biasv[cq, :, t] = -1e30
                else:
                    biasv[cq, :, t] = -SLOPE[h] * jpos
    sel = np.zeros((16, 8 * 128), np.float32)
    for hp in range(8):
        for m in range(128):
            sel[2 * hp + m // 64, 128 * hp + m] = 1.0
    return biasv, maskt, sel.astype(BF)


def _l0_windows(kfm_g, vtm_g):
    kw = np.zeros((128, KW_TOT), BF)
    for hp in range(8):
        kw[:, int(KOFF[hp]):int(KOFF[hp]) + JPAIR[hp]] = \
            kfm_g[128 * hp:128 * (hp + 1), 0:JPAIR[hp]]
    vw = np.zeros((128, VW_TOT), BF)
    for h in range(16):
        for s in range(SH[h]):
            slot = int(VOFF[h]) + s
            vw[:, 65 * slot:65 * slot + 64] = \
                vtm_g[128 * s:128 * (s + 1), 64 * h:64 * (h + 1)]
            vw[:, 65 * slot + 64] = 1.0
    return kw, vw


def _l1_windows(kf, vf, cq):
    base = 2048 * cq
    kwin = np.zeros((1024, 2560), BF)
    vwin = np.zeros((1024, 2560), BF)
    lo = base - 512
    s = max(0, lo)
    kwin[:, s - lo:] = kf[:, s:base + 2048]
    vwin[:, s - lo:] = vf[:, s:base + 2048]
    kbig = np.zeros((1024, 512 * NBIG), BF)
    vbig = np.zeros((1024, 512 * NBIG), BF)
    for slot, (d, i) in enumerate(BIGSLOTS):
        g0 = base + 512 * i - d
        gs = max(0, g0)
        if g0 + 512 > 0:
            kbig[:, 512 * slot + (gs - g0):512 * (slot + 1)] = kf[:, gs:g0 + 512]
            vbig[:, 512 * slot + (gs - g0):512 * (slot + 1)] = vf[:, gs:g0 + 512]
    return kwin, vwin, kbig, vbig


def _l1_selectors():
    selp = np.zeros((128, 522), np.float32)
    selp[0:64, 0] = 1.0
    selp[64:128, 1] = 1.0
    for bnk in range(4):
        nslots = 4 if bnk < 3 else 2
        for kk in range(nslots):
            selp[32 * kk + 0, 2 + 2 * bnk + 0] = 1.0
            selp[32 * kk + 1, 2 + 2 * bnk + 1] = 1.0
    for kk in range(4):
        for m in range(128):
            selp[32 * kk + (m // 64), 10 + 128 * kk + m] = 1.0
    e2t = np.zeros((2, 128), np.float32)
    for m in range(128):
        e2t[m // 64, m] = 1.0
    return selp.astype(BF), e2t.astype(BF)


def kernel(**inputs):
    inz = {k: np.asarray(v) for k, v in inputs.items()}
    R = list(range(NCORE))
    x = inz["L0_attn_x"]
    xc = _core_view(x)

    nc = build_l0qkv()
    im = [{"x_fm": _asT(xc[c]), "Wq": inz["L0_attn_Wq"],
           "Wk": inz["L0_attn_Wk"], "Wv": inz["L0_attn_Wv"]} for c in R]
    r1 = run_bass_kernel_spmd(nc, im, core_ids=R)

    biasv, maskt, selhp = _l0attn_prep(inz)
    bo_pc = _pc(inz["L0_attn_bo"])
    kv_b = []
    for b in range(2):
        kfm_g = np.concatenate([r1.results[4 * b + q]["k_fm"] for q in range(4)],
                               axis=1)
        vtm_g = np.concatenate([r1.results[4 * b + q]["v_tm"] for q in range(4)],
                               axis=0)
        kv_b.append(_l0_windows(kfm_g, vtm_g))
    nc = build_l0attn()
    im = []
    for c in R:
        b, cq = divmod(c, 4)
        im.append({"q_fm": r1.results[c]["q_fm"], "kwin": kv_b[b][0],
                   "vwin": kv_b[b][1],
                   "biasv": np.ascontiguousarray(biasv[cq]),
                   "maskt": np.ascontiguousarray(maskt[cq]),
                   "selhp": selhp, "Wo": inz["L0_attn_Wo"],
                   "bo_pc": bo_pc, "x_fm": _asT(xc[c])})
    r2 = run_bass_kernel_spmd(nc, im, core_ids=R)

    nc = build_l0ffn()
    im = []
    for c in R:
        b, cq = divmod(c, 4)
        if cq == 0:
            halo = np.zeros((128, 8), np.float32)
        else:
            halo = _pc(r2.results[c - 1]["x1_fm"][:, -1].astype(np.float32))
        im.append({"x1_fm": r2.results[c]["x1_fm"], "halo_pc": halo,
                   "mur_pc": _pc(inz["L0_ffn_mu_r"]),
                   "muk_pc": _pc(inz["L0_ffn_mu_k"]),
                   "WrT": _asT(inz["L0_ffn_W_r"]),
                   "WkT": _asT(inz["L0_ffn_W_k"]),
                   "WvT": _asT(inz["L0_ffn_W_v"])})
    r3 = run_bass_kernel_spmd(nc, im, core_ids=R)

    nc = build_l1qkv()
    im = [{"x2_fm": r3.results[c]["x2_fm"],
           "WqT": _asT(inz["L1_attn_Wq"]), "WkT": _asT(inz["L1_attn_Wk"]),
           "WvT": _asT(inz["L1_attn_Wv"]), "bq_pc": _pc(inz["L1_attn_bq"]),
           "bk_pc": _pc(inz["L1_attn_bk"]), "bv_pc": _pc(inz["L1_attn_bv"])}
          for c in R]
    r4 = run_bass_kernel_spmd(nc, im, core_ids=R)

    selp, e2t = _l1_selectors()
    wins = {}
    sv4_b = {}
    for b in range(2):
        kf = np.concatenate([r4.results[4 * b + q]["k_fm"] for q in range(4)],
                            axis=1)
        vf = np.concatenate([r4.results[4 * b + q]["v_fm"] for q in range(4)],
                            axis=1)
        for cq in range(4):
            wins[(b, cq)] = _l1_windows(kf, vf, cq)
        sv4 = np.zeros((128, 32), np.float32)
        for q in range(4):
            sv4[:, np.arange(8) * 4 + q] = r4.results[4 * b + q]["sv_p"]
        sv4_b[b] = sv4
    nc = build_l1attn()
    im = []
    for c in R:
        b, cq = divmod(c, 4)
        kwin, vwin, kbig, vbig = wins[(b, cq)]
        im.append({"x2_fm": r3.results[c]["x2_fm"],
                   "q_fm": r4.results[c]["q_fm"], "kwin": kwin, "vwin": vwin,
                   "kbig": kbig, "vbig": vbig, "sv4": sv4_b[b], "selp": selp,
                   "e2t": e2t, "WoT": _asT(inz["L1_attn_Wo"]),
                   "bo_pc": _pc(inz["L1_attn_bo"])})
    r5 = run_bass_kernel_spmd(nc, im, core_ids=R)

    nc = build_moe_gate()
    iota8 = np.broadcast_to(np.arange(8, dtype=np.float32)[None, :],
                            (128, 8)).copy()
    eye = np.eye(128, dtype=np.float32)
    im = [{"x3_fm": r5.results[c]["x3_fm"], "w_red": inz["L1_ffn_w_red"],
           "wg": inz["L1_ffn_wg"], "iota8": iota8, "eye": eye} for c in R]
    r6a = run_bass_kernel_spmd(nc, im, core_ids=R)

    eidx_all = np.concatenate(
        [np.ascontiguousarray(r6a.results[c]["eidx"].T) for c in R],
        axis=0)
    x3tm_all = np.concatenate([r6a.results[c]["x3tm"] for c in R], axis=0)
    tri = np.triu(np.ones((128, 128), np.float32), 1)
    thr = np.broadcast_to((512.0 * np.arange(32, dtype=np.float32))[None, :],
                          (8, 32)).copy()
    slotbase = (512 * (np.arange(20) // 4) +
                128 * (np.arange(20) % 4)).astype(np.float32)[None, :]
    c8b = (128.0 * np.arange(8, dtype=np.float32))[None, :]
    j32b = (128.0 * np.arange(32, dtype=np.float32))[None, :]
    iseg = np.zeros((128, 3), np.float32)
    iseg[:, 0] = np.arange(128)
    iseg[:, 1] = np.where(np.arange(128) < 32, np.arange(128), 1e6)
    iseg[:, 2] = np.where(np.arange(128) < 8, np.arange(128), 1e6)
    w1f = np.ascontiguousarray(inz["L1_ffn_w1"].reshape(8192, 4096))
    w2f = np.ascontiguousarray(inz["L1_ffn_w2"].reshape(32768, 1024))
    nc = build_moe_expert()
    im = []
    for c in R:
        im.append({"eidx_all": eidx_all, "x3tm_all": x3tm_all, "w1f": w1f,
                   "w2f": w2f, "b1f": inz["L1_ffn_b1"], "b2f": inz["L1_ffn_b2"],
                   "rankf": np.array([[float(c)]], np.float32), "tri128": tri,
                   "thr": thr, "eye": eye, "slotbase": slotbase,
                   "c8base128": c8b, "j32base128": j32b, "iotaseg": iseg})
    r6b = run_bass_kernel_spmd(nc, im, core_ids=R)

    z_all = np.concatenate([r6b.results[c]["z_slice"] for c in R], axis=0)
    nc = build_moe_final()
    im = []
    for c in R:
        posT = r6b.results[c]["posT"]
        im.append({"z_all": z_all,
                   "pos_own": np.ascontiguousarray(posT[:, 16 * c:16 * (c + 1)]),
                   "score": r6a.results[c]["score"],
                   "x3tm": r6a.results[c]["x3tm"]})
    r6c = run_bass_kernel_spmd(nc, im, core_ids=R)

    out = np.empty((2, NSEQ, DM), np.float32)
    for c in R:
        b, cq = divmod(c, 4)
        out[b, 2048 * cq:2048 * (cq + 1)] = r6c.results[c]["out_tm"]
    return out
